# version 14
# baseline (speedup 1.0000x reference)
import numpy as np
import ml_dtypes
from contextlib import ExitStack

import concourse.bass as bass
import concourse.mybir as mybir
from concourse.bass_utils import run_bass_kernel_spmd

F32 = mybir.dt.float32
BF16 = mybir.dt.bfloat16
AF = mybir.ActivationFunctionType
ALU = mybir.AluOpType
AX = mybir.AxisListType

B, S, H = 2, 8192, 2048
NCORES = 8
TOK = 2048
NH = 16
HD = 128
NHC = H // 128
NP = S // TOK
NQT = TOK // 512
NKC = S // 128
NE, I2, I, R = 8, 2048, 1024, 16
NIC = I // 128
EPS = 1e-5
ISQ = float(1.0 / np.sqrt(128.0))
BIG = 1.0e9

ENGINES = ("tensor", "vector", "scalar", "gpsimd", "sync")


class Sem:
    def __init__(self, handle, name):
        self.h = handle
        self.name = name
        self.n = 0

    def inc(self, k):
        self.n += k
        return self.n


class Buf:
    __slots__ = ("ready", "free", "name")

    def __init__(self, name=""):
        self.ready = {}
        self.free = {}
        self.name = name


def _merge(d, sem, val):
    if d.get(sem, -1) < val:
        d[sem] = val


class Plan:
    def __init__(self, nc, sems):
        self.nc = nc
        self.prog = {e: [] for e in ENGINES}
        self.sems = sems
        self.hwm = {e: {} for e in ENGINES}

    def step(self, eng, build, reads=(), writes=(), inc=True, extra_waits=()):
        waits = {}
        for b in reads:
            for s, v in b.ready.items():
                _merge(waits, s, v)
        for b in writes:
            for s, v in b.free.items():
                _merge(waits, s, v)
        for s, v in extra_waits:
            _merge(waits, s, v)
        sem = self.sems[eng]
        hw = self.hwm[eng]
        wl = []
        for s, v in waits.items():
            assert v <= s.n, f"wait on future {s.name}:{v}>{s.n} (eng {eng})"
            if hw.get(s, -1) < v:
                wl.append((s, v))
                hw[s] = v
        amt = 16 if eng in ("gpsimd", "sync") else 1
        tgt = sem.inc(amt) if inc else None

        def fn(Eng, build=build, wl=tuple(wl), sem=sem, doinc=inc, amt=amt):
            for s, v in wl:
                Eng.wait_ge(s.h, v)
            ins = build(Eng)
            if doinc:
                ins.then_inc(sem.h, amt)
        self.prog[eng].append(fn)
        if inc:
            for b in reads:
                _merge(b.free, sem, tgt)
            for b in writes:
                b.ready = {sem: tgt}
        return tgt

    def raw_wait(self, eng, sem, val):
        if val <= 0:
            return
        hw = self.hwm[eng]
        assert val <= sem.n
        if hw.get(sem, -1) >= val:
            return
        hw[sem] = val
        self.prog[eng].append(lambda Eng, s=sem, v=val: Eng.wait_ge(s.h, v))

    def phase_barrier(self, all_sems):
        snap = [(s, s.n) for s in all_sems]
        for e in ENGINES:
            for s, v in snap:
                self.raw_wait(e, s, v)


def build_program(dbg=(), phases=6):
    nc = bass.Bass("TRN2", target_bir_lowering=False, debug=False)

    def dram_in(name, shape, dtype=F32):
        return nc.dram_tensor(name, shape, dtype, kind="ExternalInput").ap()

    def scratch(name, shape, dtype):
        kind = "ExternalOutput" if name in dbg else "Internal"
        return nc.dram_tensor(name, shape, dtype, kind=kind).ap()

    xT = dram_in("xT", [H, S])
    cosD = dram_in("cosD", [128, S])
    sinD = dram_in("sinD", [128, S])
    swapm = dram_in("swapm", [128, 128])
    wq = dram_in("wq", [NHC, 128, NHC, 128])
    wk = dram_in("wk", [NHC, 128, NHC, 128])
    wv = dram_in("wv", [NHC, 128, NHC, 128])
    wo = dram_in("wo", [NHC, 128, NHC, 128])
    bias3 = dram_in("bias3", [128, 3, 16])
    gwT = dram_in("gwT", [128, NHC, NE])
    glA = dram_in("glA", [R, H])
    glBT = dram_in("glBT", [R, NE])
    gupw = dram_in("gupw", [NE, NHC, 128, NHC, 128])
    downw = dram_in("downw", [NE, NHC, 128, NIC, 128])
    gupA = dram_in("gupA", [NE, 128, NHC, R])
    gupBT = dram_in("gupBT", [NE, R, I2])
    downA = dram_in("downA", [NE, 128, NIC, R])
    downBT = dram_in("downBT", [NE, R, H])
    ident = dram_in("ident", [128, 128])

    OUT = nc.dram_tensor("out", [H, TOK], F32, kind="ExternalOutput").ap()

    Ks = scratch("Ks", [NH, 128, S], BF16)
    Vs = scratch("Vs", [NH, 128, S], BF16)
    As = scratch("As", [NH, 128, TOK], BF16)
    X1 = scratch("X1", [NHC, 128, TOK], F32)
    X1n = scratch("X1n", [NHC, 128, TOK], F32)
    ACTs = scratch("ACTs", [NE, NIC, 128, TOK], BF16)
    RDBG = (nc.dram_tensor("RDBG", [128, 512], F32, kind="ExternalOutput").ap()
            if "RDBG" in dbg else None)

    es = ExitStack()
    sem_handles = {}
    for sname in ("st", "sv", "sa", "sg", "ss"):
        sem_handles[sname] = Sem(es.enter_context(nc.semaphore(sname)), sname)
    st, sv, sa, sg, ss = (sem_handles[k] for k in ("st", "sv", "sa", "sg", "ss"))
    ALLSEMS = [st, sv, sa, sg, ss]
    sems = {"tensor": st, "vector": sv, "scalar": sa, "gpsimd": sg, "sync": ss}
    P = Plan(nc, sems)

    def sb(stack, name, shape, dtype, side="left"):
        return stack.enter_context(nc.sbuf_tensor(name, shape, dtype, side=side))

    ones_c_f = sb(es, "ones_c_f", [128, 1], F32)
    ones_c_b = sb(es, "ones_c_b", [128, 1], BF16)
    ones_r_f = sb(es, "ones_r_f", [1, 128], F32)
    id_f = sb(es, "id_f", [128, 128], F32)
    id_b = sb(es, "id_b", [128, 128], BF16)
    swp_f = sb(es, "swp_f", [128, 128], F32)
    swp_b = sb(es, "swp_b", [128, 128], BF16)
    biases = sb(es, "biases", [128, 48], F32)
    negC = sb(es, "negC", [128, NH], F32)
    nslots = sb(es, "nslots", [1, 320], F32)
    crow = sb(es, "crow", [1, NH], F32)
    crow2 = sb(es, "crow2", [1, NH], F32)
    negrow = sb(es, "negrow", [1, NH], F32)
    epsrow = sb(es, "epsrow", [1, 1], F32)

    b_ones = Buf("ones")
    b_id = Buf("id")
    b_biases = Buf("biases")
    b_negC = Buf("negC")
    b_nslots = Buf("nslots")
    b_Qres = Buf("Qres")
    b_Ks = Buf("Ks")
    b_Vs = Buf("Vs")
    b_As = Buf("As")
    b_X1 = Buf("X1")
    b_X1n = Buf("X1n")
    b_x1n_sb = Buf("x1n_sb")
    b_ACTs = Buf("ACTs")
    b_yacc = Buf("yacc")

    right_q = ExitStack()
    Qres = sb(right_q, "Qres", [128, NH * TOK], BF16, side="right")

    P.step("vector", lambda E: E.memset(ones_c_f[:], 1.0), writes=[b_ones])
    P.step("vector", lambda E: E.memset(ones_c_b[:], 1.0), writes=[b_ones])
    P.step("vector", lambda E: E.memset(ones_r_f[:], 1.0), writes=[b_ones])
    P.step("vector", lambda E: E.memset(epsrow[:], EPS), writes=[b_ones])
    P.step("sync", lambda E: E.dma_start(out=id_f[:], in_=ident), writes=[b_id])
    P.step("vector", lambda E: E.tensor_copy(id_b[:], id_f[:]), reads=[b_id], writes=[b_id])
    P.step("sync", lambda E: E.dma_start(out=swp_f[:], in_=swapm), writes=[b_id])
    P.step("vector", lambda E: E.tensor_copy(swp_b[:], swp_f[:]), reads=[b_id], writes=[b_id])
    P.step("sync", lambda E: E.dma_start(
        out=biases[:], in_=bias3.rearrange("p a b -> p (a b)")), writes=[b_biases])

    p1 = ExitStack()
    Xp = sb(p1, "Xp", [128, NHC * TOK], BF16)
    wcol = [sb(p1, f"wcol{i}", [128, NHC * 128], BF16) for i in range(2)]
    tmpb = [sb(p1, f"tmp{i}", [128, 512], F32) for i in range(3)]
    sqb = [sb(p1, f"sq{i}", [128, 512], BF16) for i in range(3)]
    stage = [sb(p1, f"stage{i}", [128, TOK], BF16) for i in range(2)]
    vtmp = [sb(p1, f"vtmp{i}", [128, 512], BF16) for i in range(3)]
    vstage = [sb(p1, f"vstage{i}", [128, TOK], BF16) for i in range(2)]
    cosb = sb(p1, "cosb", [128, TOK], F32)
    sinb = sb(p1, "sinb", [128, TOK], F32)
    rp1 = [sb(p1, f"rp1_{i}", [128, 512], F32) for i in range(3)]
    rp2 = [sb(p1, f"rp2_{i}", [128, 512], BF16) for i in range(3)]
    psP = [p1.enter_context(nc.psum_tensor(f"psP{i}", [128, 512], F32)) for i in range(3)]
    psN = p1.enter_context(nc.psum_tensor("psN", [1, 512], F32))
    psSw = [p1.enter_context(nc.psum_tensor(f"psSw{i}", [128, 512], F32)) for i in range(3)]
    psVt1 = p1.enter_context(nc.psum_tensor("psVt1", [128, 512], BF16))

    b_Xp = Buf("Xp")
    b_wcol = [Buf(f"wcol{i}") for i in range(2)]
    b_tmp = [Buf(f"tmp{i}") for i in range(3)]
    b_sq = [Buf(f"sq{i}") for i in range(3)]
    b_stage = [Buf(f"stage{i}") for i in range(2)]
    b_vtmp = [Buf(f"vtmp{i}") for i in range(3)]
    b_vstage = [Buf(f"vstage{i}") for i in range(2)]
    b_psVt1 = Buf("psVt1")
    b_cos = Buf("cos")
    b_rp1 = [Buf(f"rp1_{i}") for i in range(3)]
    b_rp2 = [Buf(f"rp2_{i}") for i in range(3)]
    b_psSw = [Buf(f"psSw{i}") for i in range(3)]
    b_psP = [Buf(f"psP{i}") for i in range(3)]
    b_psN = Buf("psN")

    cnt1 = [0]
    pending1 = []

    def p1_panel(p):
        for f in pending1:
            f()
        pending1.clear()
        for hc in range(NHC):
            P.step("gpsimd",
                   lambda E, hc=hc, p=p: E.dma_start(
                       out=Xp[:, hc * TOK:(hc + 1) * TOK],
                       in_=xT[hc * 128:(hc + 1) * 128, p * TOK:(p + 1) * TOK]),
                   writes=[b_Xp] if hc == NHC - 1 else [],
                   extra_waits=tuple(b_Xp.free.items()) if hc == 0 else ())
        P.step("sync", lambda E, p=p: E.dma_start(
            out=cosb[:], in_=cosD[:, p * TOK:(p + 1) * TOK]),
            writes=[b_cos], extra_waits=tuple(b_cos.free.items()))
        P.step("sync", lambda E, p=p: E.dma_start(
            out=sinb[:], in_=sinD[:, p * TOK:(p + 1) * TOK]),
            writes=[b_cos])

        tensors = [("k", wk, 1), ("v", wv, 2)] + ([("q", wq, 0)] if p == 0 else [])

        def flush():
            for f in pending1:
                f()
            pending1.clear()

        for tname, wsrc, bidx in tensors:
            for dc in range(NHC):
                wb = dc % 2
                sbf = stage[dc % 2]
                b_sbf = b_stage[dc % 2]
                P.step("gpsimd",
                       lambda E, wsrc=wsrc, dc=dc, wb=wb: E.dma_start(
                           out=wcol[wb][:],
                           in_=wsrc[dc].rearrange("p h c -> p (h c)")),
                       writes=[b_wcol[wb]])
                for qt in range(NQT):
                    pp = cnt1[0] % 3
                    tb = cnt1[0] % 3
                    cnt1[0] += 1

                    def mm_group(E, dc=dc, qt=qt, wb=wb, pp=pp):
                        last = None
                        for hc in range(NHC):
                            last = E.matmul(
                                psP[pp][:],
                                wcol[wb][:, hc * 128:(hc + 1) * 128],
                                Xp[:, hc * TOK + qt * 512: hc * TOK + qt * 512 + 512],
                                start=(hc == 0), stop=(hc == NHC - 1))
                        return last
                    P.step("tensor", mm_group,
                           reads=[b_wcol[wb], b_Xp], writes=[b_psP[pp]])
                    flush()

                    def tail(tname=tname, dc=dc, qt=qt, pp=pp, tb=tb, p=p,
                             bidx=bidx, sbf=sbf, b_sbf=b_sbf):
                        if tname == "v":
                            vb = tb
                            vs = stage[dc % 2]
                            P.step("scalar",
                                   lambda E: E.add(
                                       vtmp[vb][:], psP[pp][:],
                                       biases[:, bidx * 16 + dc: bidx * 16 + dc + 1]),
                                   reads=[b_psP[pp], b_biases], writes=[b_vtmp[vb]])

                            def vtr(E, vb=vb):
                                last = None
                                for s4 in range(4):
                                    last = E.matmul(
                                        psVt1[:, s4 * 128:(s4 + 1) * 128],
                                        vtmp[vb][:, s4 * 128:(s4 + 1) * 128],
                                        id_b[:], start=(s4 == 0), stop=(s4 == 3),
                                        is_transpose=True)
                                return last
                            P.step("tensor", vtr, reads=[b_vtmp[vb], b_id],
                                   writes=[b_psVt1])
                            P.step("vector", lambda E, qt=qt, dc=dc: E.tensor_copy(
                                vstage[dc % 2][:, qt * 512:qt * 512 + 512], psVt1[:]),
                                reads=[b_psVt1],
                                writes=[b_vstage[dc % 2]] if qt == NQT - 1 else [],
                                extra_waits=tuple(b_vstage[dc % 2].free.items()) if qt == 0 else ())
                            return
                        P.step("scalar",
                               lambda E: E.add(
                                   tmpb[tb][:], psP[pp][:],
                                   biases[:, bidx * 16 + dc: bidx * 16 + dc + 1]),
                               reads=[b_psP[pp], b_biases], writes=[b_tmp[tb]])
                        P.step("scalar",
                               lambda E: E.activation(
                                   sqb[tb][:], tmpb[tb][:], AF.Square),
                               reads=[b_tmp[tb]], writes=[b_sq[tb]])
                        if tname == "q":
                            dst, dof, dstbuf = Qres, dc * TOK + qt * 512, b_Qres
                        else:
                            dst, dof, dstbuf = sbf, qt * 512, b_sbf
                        cs = qt * 512
                        rb = tb
                        P.step("vector", lambda E: E.tensor_mul(
                            rp1[rb][:], tmpb[tb][:], cosb[:, cs:cs + 512]),
                            reads=[b_tmp[tb], b_cos], writes=[b_rp1[rb]])
                        P.step("vector", lambda E: E.tensor_mul(
                            rp2[rb][:], tmpb[tb][:], sinb[:, cs:cs + 512]),
                            reads=[b_tmp[tb], b_cos], writes=[b_rp2[rb]])
                        P.step("tensor",
                               lambda E: E.matmul(
                                   psN[:], ones_c_b[:], sqb[tb][:], start=True, stop=True),
                               reads=[b_sq[tb], b_ones], writes=[b_psN])
                        P.step("tensor", lambda E: E.matmul(
                            psSw[rb][:], swp_b[:], rp2[rb][:], start=True, stop=True),
                            reads=[b_rp2[rb], b_id], writes=[b_psSw[rb]])
                        P.step("vector", lambda E, dst=dst, dof=dof: E.tensor_add(
                            dst[:, dof:dof + 512], rp1[rb][:], psSw[rb][:]),
                            reads=[b_rp1[rb], b_psSw[rb]],
                            writes=[dstbuf] if qt == NQT - 1 else [],
                            extra_waits=tuple(dstbuf.free.items()) if qt == 0 else ())
                        slot = (dc * 16 + p * 4 + qt) if tname == "k" else (256 + dc * 4 + qt)
                        P.step("vector",
                               lambda E, slot=slot: E.tensor_reduce(
                                   nslots[0:1, slot:slot + 1], psN[:], AX.X, ALU.max),
                               reads=[b_psN], writes=[b_nslots])
                    pending1.append(tail)
                if tname == "k":
                    def kdma(dc=dc, p=p, sbf=sbf, b_sbf=b_sbf):
                        P.step("sync",
                               lambda E: E.dma_start(
                                   out=Ks[dc][:, p * TOK:(p + 1) * TOK], in_=sbf[:]),
                               reads=[b_sbf], writes=[b_Ks])
                    pending1.append(kdma)
                elif tname == "v":
                    def vdma(dc=dc, p=p):
                        P.step("sync",
                               lambda E: E.dma_start(
                                   out=Vs[dc][:, p * TOK:(p + 1) * TOK],
                                   in_=vstage[dc % 2][:]),
                               reads=[b_vstage[dc % 2]], writes=[b_Vs])
                    pending1.append(vdma)
        flush()

    for p in range(NP):
        p1_panel(p)
    p1.close()
    P.phase_barrier(ALLSEMS)

    if phases >= 2:
        pnc = ExitStack()
        psNC = pnc.enter_context(nc.psum_tensor("psNC", [128, NH], F32))
        b_psNC = Buf("psNC")
        P.step("vector", lambda E: E.tensor_reduce(
            crow[:], nslots[0:1, 0:256].rearrange("a (d s) -> a d s", s=16),
            AX.X, ALU.max), reads=[b_nslots], writes=[b_negC])
        P.step("vector", lambda E: E.tensor_reduce(
            crow2[:], nslots[0:1, 256:320].rearrange("a (d s) -> a d s", s=4),
            AX.X, ALU.max), reads=[b_nslots], writes=[b_negC])
        P.step("vector", lambda E: E.tensor_mul(crow[:], crow[:], crow2[:]),
               reads=[b_negC], writes=[b_negC])
        P.step("scalar", lambda E: E.activation(crow2[:], crow[:], AF.Sqrt, scale=1.0 / 128.0),
               reads=[b_negC], writes=[b_negC])
        P.step("vector", lambda E: E.tensor_scalar_mul(negrow[:], crow2[:], -1.0),
               reads=[b_negC], writes=[b_negC])
        P.step("tensor", lambda E: E.matmul(psNC[:], ones_r_f[:], negrow[:],
                                            start=True, stop=True),
               reads=[b_negC, b_ones], writes=[b_psNC])
        P.step("vector", lambda E: E.tensor_copy(negC[:], psNC[:]),
               reads=[b_psNC], writes=[b_negC])
        P.phase_barrier(ALLSEMS)
        pnc.close()

        p2 = ExitStack()
        Kh = [sb(p2, f"Kh{i}", [128, S], BF16) for i in range(2)]
        Vtb = [sb(p2, f"Vtb{i}", [128, S], BF16) for i in range(2)]
        Pt = [sb(p2, f"Pt{i}", [128, 1024], BF16) for i in range(6)]
        linv = sb(p2, "linv", [1, 512], F32)
        lbc = sb(p2, "lbc", [128, 512], F32)
        ls2 = [sb(p2, f"ls2_{i}", [128, 512], BF16) for i in range(2)]
        ls4 = sb(p2, "ls4", [128, 512], BF16)
        laccf = [sb(p2, f"laccf{i}", [128, 512], F32) for i in range(2)]
        ast = [sb(p2, f"ast{i}", [128, TOK], BF16) for i in range(2)]
        psS = [p2.enter_context(nc.psum_tensor(f"psS{i}", [128, 1024], F32)) for i in range(2)]
        psO = [p2.enter_context(nc.psum_tensor(f"psO{i}", [128, 512], F32)) for i in range(2)]
        psLB = p2.enter_context(nc.psum_tensor("psLB", [128, 512], F32))

        b_Kh = [Buf(f"Kh{i}") for i in range(2)]
        b_Vt2 = [Buf(f"Vtb{i}") for i in range(2)]
        b_Pt = [Buf(f"Pt{i}") for i in range(6)]
        b_linv = Buf("linv")
        b_lbc = Buf("lbc")
        b_ls2 = [Buf(f"ls2_{i}") for i in range(2)]
        b_ls4 = Buf("ls4")
        b_laccf = [Buf(f"laccf{i}") for i in range(2)]
        b_ast = [Buf(f"ast{i}") for i in range(2)]
        b_psS = [Buf(f"psS{i}") for i in range(2)]
        b_psO = [Buf(f"psO{i}") for i in range(2)]
        b_psLB = Buf("psLB")

        pend2 = []
        for h in range(NH):
            hb = h % 2
            P.step("sync", lambda E, h=h, hb=hb: E.dma_start(out=Kh[hb][:], in_=Ks[h]),
                   reads=[b_Ks], writes=[b_Kh[hb]])
            P.step("sync", lambda E, h=h, hb=hb: E.dma_start(out=Vtb[hb][:], in_=Vs[h]),
                   reads=[b_Vs], writes=[b_Vt2[hb]])
            for qt in range(NQT):
                qoff = h * TOK + qt * 512
                npair = NKC // 2

                def plan_smm(ip, hb=hb, qoff=qoff):
                    pp = ip % 2
                    pt = ip % 6
                    kc0, kc1 = 2 * ip, 2 * ip + 1

                    def smm(E, kc0=kc0, kc1=kc1, pp=pp, hb=hb, qoff=qoff):
                        E.matmul(psS[pp][:, 0:512],
                                 Kh[hb][:, kc0 * 128:(kc0 + 1) * 128],
                                 Qres[:, qoff:qoff + 512], start=True, stop=True)
                        return E.matmul(psS[pp][:, 512:1024],
                                        Kh[hb][:, kc1 * 128:(kc1 + 1) * 128],
                                        Qres[:, qoff:qoff + 512], start=True, stop=True)
                    P.step("tensor", smm, reads=[b_Kh[hb], b_Qres], writes=[b_psS[pp]])
                    P.step("scalar", lambda E, pp=pp, pt=pt, h=h: E.activation(
                        Pt[pt][:], psS[pp][:], AF.Exp,
                        bias=negC[:, h:h + 1], scale=ISQ),
                        reads=[b_psS[pp], b_negC], writes=[b_Pt[pt]])

                qo = qt % 2

                def plan_pvl(ip, npair=npair, qo=qo):
                    pt = ip % 6
                    kc0, kc1 = 2 * ip, 2 * ip + 1

                    def pvl(E, kc0=kc0, kc1=kc1, pt=pt, ip=ip, npair=npair, hb=hb, qo=qo):
                        E.matmul(psO[qo][:], Vtb[hb][:, kc0 * 128:(kc0 + 1) * 128],
                                 Pt[pt][:, 0:512], start=(ip == 0), stop=False)
                        return E.matmul(psO[qo][:], Vtb[hb][:, kc1 * 128:(kc1 + 1) * 128],
                                        Pt[pt][:, 512:1024], start=False,
                                        stop=(ip == npair - 1))
                    P.step("tensor", pvl, reads=[b_Pt[pt], b_Vt2[hb]],
                           writes=[b_psO[qo]])
                    lsp = ip % 2
                    P.step("vector" if ip % 2 == 0 else "gpsimd",
                           lambda E, pt=pt, lsp=lsp: E.tensor_add(
                               ls2[lsp][:], Pt[pt][:, 0:512], Pt[pt][:, 512:1024]),
                           reads=[b_Pt[pt]], writes=[b_ls2[lsp]])
                    if ip % 2 == 1:
                        P.step("vector", lambda E: E.tensor_add(
                            ls4[:], ls2[0][:], ls2[1][:]),
                            reads=[b_ls2[0], b_ls2[1]], writes=[b_ls4])
                        if ip == 1:
                            P.step("vector", lambda E, qo=qo: E.tensor_copy(
                                laccf[qo][:], ls4[:]),
                                reads=[b_ls4], writes=[b_laccf[qo]])
                        else:
                            P.step("vector", lambda E, qo=qo: E.tensor_add(
                                laccf[qo][:], ls4[:], laccf[qo][:]),
                                reads=[b_ls4, b_laccf[qo]], writes=[b_laccf[qo]])

                for ip in range(npair):
                    plan_smm(ip)
                    if ip == 16:
                        for f in pend2:
                            f()
                        pend2.clear()
                    if ip >= 2:
                        plan_pvl(ip - 2)
                plan_pvl(npair - 2)
                plan_pvl(npair - 1)

                def qt_end(hb=hb, qt=qt, qo=qo):
                    P.step("tensor", lambda E, qo=qo: E.matmul(
                        psLB[0:1, :], ones_c_f[:], laccf[qo][:], start=True, stop=True),
                        reads=[b_laccf[qo], b_ones], writes=[b_psLB])
                    P.step("vector", lambda E: E.reciprocal(linv[:], psLB[0:1, :]),
                           reads=[b_psLB], writes=[b_linv])
                    P.step("tensor", lambda E: E.matmul(
                        psLB[:], ones_r_f[:], linv[:], start=True, stop=True),
                        reads=[b_linv, b_ones], writes=[b_psLB])
                    P.step("vector", lambda E: E.tensor_copy(lbc[:], psLB[:]),
                           reads=[b_psLB], writes=[b_lbc])
                    P.step("vector", lambda E, hb=hb, qt=qt, qo=qo: E.tensor_mul(
                        ast[hb][:, qt * 512:qt * 512 + 512], psO[qo][:], lbc[:]),
                        reads=[b_psO[qo], b_lbc],
                        writes=[b_ast[hb]] if qt == NQT - 1 else [],
                        extra_waits=tuple(b_ast[hb].free.items()) if qt == 0 else ())
                pend2.append(qt_end)
            for f in pend2:
                f()
            pend2.clear()
            P.step("sync", lambda E, h=h, hb=hb: E.dma_start(out=As[h], in_=ast[hb][:]),
                   reads=[b_ast[hb]], writes=[b_As])

        P.phase_barrier(ALLSEMS)
        p2.close()
        right_q.close()

    if phases >= 3:
        right_x = ExitStack()
        x1n_sb = sb(right_x, "x1n_sb", [128, NHC * TOK], BF16, side="right")

        p3 = ExitStack()
        ah = sb(p3, "ah", [128, NHC * 1024], BF16)
        wob = [sb(p3, f"wob{i}", [128, NHC * 128], BF16) for i in range(2)]
        rtile = [sb(p3, f"rtile{i}", [128, 512], F32) for i in range(2)]
        x1t = [sb(p3, f"x1t{i}", [128, 512], F32) for i in range(2)]
        x1b = [sb(p3, f"x1b{i}", [128, 512], BF16) for i in range(2)]
        sq3 = [sb(p3, f"sq3{i}", [128, 512], BF16) for i in range(2)]
        rows_mu = sb(p3, "rows_mu3", [1, TOK], F32)
        rows_e2 = sb(p3, "rows_e23", [1, TOK], F32)
        rows_m2 = sb(p3, "rows_m23", [1, TOK], F32)
        rows_rs = sb(p3, "rows_rs3", [1, TOK], F32)
        rows_s = sb(p3, "rows_s3", [1, TOK], F32)
        psOut = [p3.enter_context(nc.psum_tensor(f"psOut{i}", [128, 512], F32)) for i in range(2)]
        psSum = p3.enter_context(nc.psum_tensor("psSum3", [1, 1024], F32))
        psSq = p3.enter_context(nc.psum_tensor("psSq3", [1, 1024], F32))

        b_ah = Buf("ah")
        b_wob = [Buf(f"wob{i}") for i in range(2)]
        b_rtile = [Buf(f"rt{i}") for i in range(2)]
        b_x1t = [Buf(f"x1t{i}") for i in range(2)]
        b_x1b = [Buf(f"x1b{i}") for i in range(2)]
        b_sq3 = [Buf(f"sq3{i}") for i in range(2)]
        b_rows = Buf("rows3")
        b_psOut = [Buf(f"psOut{i}") for i in range(2)]
        b_psSum = Buf("psSum3")
        b_psSq = Buf("psSq3")

        cnt3 = [0]
        pend3 = []
        for half in range(2):
            for dc in range(NH):
                P.step("sync", lambda E, dc=dc, half=half: E.dma_start(
                    out=ah[:, dc * 1024:(dc + 1) * 1024],
                    in_=As[dc][:, half * 1024:(half + 1) * 1024]),
                    reads=[b_As], writes=[b_ah] if dc == NH - 1 else [],
                    extra_waits=tuple(b_ah.free.items()) if dc == 0 else ())
            for hoc in range(NHC):
                wb = hoc % 2
                P.step("gpsimd", lambda E, hoc=hoc, wb=wb: E.dma_start(
                    out=wob[wb][:], in_=wo[hoc].rearrange("p h c -> p (h c)")),
                    writes=[b_wob[wb]])
                for sqt in range(2):
                    qt = half * 2 + sqt
                    pp = cnt3[0] % 2
                    cnt3[0] += 1

                    def omm(E, wb=wb, sqt=sqt, pp=pp):
                        last = None
                        for dc in range(NHC):
                            last = E.matmul(
                                psOut[pp][:], wob[wb][:, dc * 128:(dc + 1) * 128],
                                ah[:, dc * 1024 + sqt * 512: dc * 1024 + sqt * 512 + 512],
                                start=(dc == 0), stop=(dc == NHC - 1))
                        return last
                    P.step("tensor", omm, reads=[b_wob[wb], b_ah], writes=[b_psOut[pp]])
                    for f in pend3:
                        f()
                    pend3.clear()

                    def tail3(hoc=hoc, qt=qt, pp=pp, sqt=sqt):
                        P.step("sync", lambda E: E.dma_start(
                            out=rtile[pp][:],
                            in_=xT[hoc * 128:(hoc + 1) * 128, qt * 512:qt * 512 + 512]),
                            writes=[b_rtile[pp]])
                        P.step("vector", lambda E: E.tensor_add(
                            x1t[pp][:], psOut[pp][:], rtile[pp][:]),
                            reads=[b_psOut[pp], b_rtile[pp]], writes=[b_x1t[pp]])
                        P.step("sync", lambda E: E.dma_start(
                            out=X1[hoc][:, qt * 512:qt * 512 + 512], in_=x1t[pp][:]),
                            reads=[b_x1t[pp]], writes=[b_X1])
                        P.step("vector", lambda E: E.tensor_copy(
                            x1b[pp][:], x1t[pp][:]),
                            reads=[b_x1t[pp]], writes=[b_x1b[pp]])
                        P.step("scalar", lambda E: E.activation(
                            sq3[pp][:], x1t[pp][:], AF.Square),
                            reads=[b_x1t[pp]], writes=[b_sq3[pp]])
                        P.step("tensor", lambda E: E.matmul(
                            psSum[0:1, sqt * 512:sqt * 512 + 512], ones_c_b[:], x1b[pp][:],
                            start=(hoc == 0), stop=(hoc == NHC - 1)),
                            reads=[b_x1b[pp], b_ones], writes=[b_psSum])
                        P.step("tensor", lambda E: E.matmul(
                            psSq[0:1, sqt * 512:sqt * 512 + 512], ones_c_b[:], sq3[pp][:],
                            start=(hoc == 0), stop=(hoc == NHC - 1)),
                            reads=[b_sq3[pp], b_ones], writes=[b_psSq])
                    pend3.append(tail3)
            for f in pend3:
                f()
            pend3.clear()
            P.step("vector", lambda E, half=half: E.tensor_scalar_mul(
                rows_mu[0:1, half * 1024:(half + 1) * 1024], psSum[:], 1.0 / H),
                reads=[b_psSum], writes=[b_rows])
            P.step("vector", lambda E, half=half: E.tensor_scalar_mul(
                rows_e2[0:1, half * 1024:(half + 1) * 1024], psSq[:], 1.0 / H),
                reads=[b_psSq], writes=[b_rows])
        P.step("scalar", lambda E: E.activation(rows_m2[:], rows_mu[:], AF.Square),
               reads=[b_rows], writes=[b_rows])
        P.step("vector", lambda E: E.tensor_sub(rows_rs[:], rows_e2[:], rows_m2[:]),
               reads=[b_rows], writes=[b_rows])
        P.step("scalar", lambda E: E.activation(rows_m2[:], rows_rs[:], AF.Ln, bias=epsrow[:]),
               reads=[b_rows], writes=[b_rows])
        P.step("scalar", lambda E: E.activation(rows_rs[:], rows_m2[:], AF.Exp, scale=-0.5),
               reads=[b_rows], writes=[b_rows])
        P.step("vector", lambda E: E.tensor_mul(rows_s[:], rows_mu[:], rows_rs[:]),
               reads=[b_rows], writes=[b_rows])
        bcR = sb(p3, "bcR3", [128, TOK], F32)
        bcS = sb(p3, "bcS3", [128, TOK], F32)
        b_bc = Buf("bc3")
        psBc = [p3.enter_context(nc.psum_tensor(f"psBc3{i}", [128, 512], F32)) for i in range(2)]
        b_psBc = [Buf(f"psBc3{i}") for i in range(2)]
        for qt in range(NQT):
            P.step("tensor", lambda E, qt=qt: E.matmul(
                psBc[0][:], ones_r_f[:], rows_rs[0:1, qt * 512:qt * 512 + 512],
                start=True, stop=True),
                reads=[b_rows, b_ones], writes=[b_psBc[0]])
            P.step("vector", lambda E, qt=qt: E.tensor_copy(
                bcR[:, qt * 512:qt * 512 + 512], psBc[0][:]),
                reads=[b_psBc[0]], writes=[b_bc] if qt == NQT - 1 else [])
            P.step("tensor", lambda E, qt=qt: E.matmul(
                psBc[1][:], ones_r_f[:], rows_s[0:1, qt * 512:qt * 512 + 512],
                start=True, stop=True),
                reads=[b_rows, b_ones], writes=[b_psBc[1]])
            P.step("vector", lambda E, qt=qt: E.tensor_copy(
                bcS[:, qt * 512:qt * 512 + 512], psBc[1][:]),
                reads=[b_psBc[1]], writes=[b_bc] if qt == NQT - 1 else [])
        x1L = [sb(p3, f"x1L{i}", [128, TOK], F32) for i in range(2)]
        x1M = [sb(p3, f"x1M{i}", [128, TOK], F32) for i in range(2)]
        b_x1L = [Buf(f"x1L{i}") for i in range(2)]
        b_x1M = [Buf(f"x1M{i}") for i in range(2)]
        for hoc in range(NHC):
            pp = hoc % 2
            P.step("sync", lambda E, hoc=hoc, pp=pp: E.dma_start(
                out=x1L[pp][:], in_=X1[hoc]),
                reads=[b_X1], writes=[b_x1L[pp]])
            P.step("vector", lambda E, pp=pp: E.tensor_mul(
                x1M[pp][:], x1L[pp][:], bcR[:]),
                reads=[b_x1L[pp], b_bc], writes=[b_x1M[pp]])
            P.step("vector", lambda E, pp=pp: E.tensor_sub(
                x1L[pp][:], x1M[pp][:], bcS[:]),
                reads=[b_x1M[pp]], writes=[b_x1L[pp]])
            P.step("vector", lambda E, pp=pp, hoc=hoc: E.tensor_copy(
                x1n_sb[:, hoc * TOK:(hoc + 1) * TOK], x1L[pp][:]),
                reads=[b_x1L[pp]],
                writes=[b_x1n_sb] if hoc == NHC - 1 else [],
                extra_waits=tuple(b_x1n_sb.free.items()) if hoc == 0 else ())
            P.step("sync", lambda E, pp=pp, hoc=hoc: E.dma_start(
                out=X1n[hoc], in_=x1L[pp][:]),
                reads=[b_x1L[pp]], writes=[b_X1n])
        P.phase_barrier(ALLSEMS)
        p3.close()

    if phases >= 4:
        p4 = ExitStack()
        glAb = sb(p4, "glAb", [R, H], BF16)
        glBTb = sb(p4, "glBTb", [R, NE], BF16)
        gwTb = sb(p4, "gwTb", [128, NHC * NE], F32)
        geff = sb(p4, "geff", [128, NHC * NE], BF16)
        LT = sb(p4, "LT", [128, NE], F32)
        nm1 = sb(p4, "nm1", [128, 1], F32)
        m1p = sb(p4, "m1p", [128, 1], F32)
        Ee = sb(p4, "Ee", [128, NE], F32)
        msk1 = sb(p4, "msk1", [128, NE], F32)
        msk2 = sb(p4, "msk2", [128, NE], F32)
        Lm = sb(p4, "Lm", [128, NE], F32)
        nm2 = sb(p4, "nm2", [128, 1], F32)
        m2p = sb(p4, "m2p", [128, 1], F32)
        msks = sb(p4, "msks", [128, NE], F32)
        wun = sb(p4, "wun", [128, NE], F32)
        den = sb(p4, "den", [128, 1], F32)
        dinv = sb(p4, "dinv", [128, 1], F32)
        w_all = sb(p4, "w_all", [128, 16 * NE], F32)
        wtmp = sb(p4, "wtmp", [1, TOK], F32)
        wbc = sb(p4, "wbc", [128, TOK], F32)
        gAb = sb(p4, "gAb", [128, NHC * R], BF16)
        gBTb = sb(p4, "gBTb", [R, I2], BF16)
        t2sb = sb(p4, "t2sb", [R, TOK], BF16)
        wg = [sb(p4, f"wg{i}", [128, NHC * 128], BF16) for i in range(2)]
        wu = [sb(p4, f"wu{i}", [128, NHC * 128], BF16) for i in range(2)]
        sg4 = [sb(p4, f"sg4{i}", [128, 512], BF16) for i in range(2)]
        ub4 = [sb(p4, f"ub4{i}", [128, 512], BF16) for i in range(2)]
        act4 = [sb(p4, f"act4{i}", [128, 512], BF16) for i in range(2)]

        b_glA = Buf("glA")
        b_glBT = Buf("glBT")
        b_gwT = Buf("gwT")
        b_geff = Buf("geff")
        b_r = Buf("router")
        b_w_all = Buf("w_all")
        b_wtmp = Buf("wtmp")
        b_wbc = Buf("wbc")
        b_gA = Buf("gA")
        b_gBT = Buf("gBT")
        b_t2 = Buf("t2")
        b_wg = [Buf(f"wg{i}") for i in range(2)]
        b_wu = [Buf(f"wu{i}") for i in range(2)]
        b_sg4 = [Buf(f"sg4{i}") for i in range(2)]
        b_ub4 = [Buf(f"ub4{i}") for i in range(2)]
        b_act4 = [Buf(f"act4{i}") for i in range(2)]

        P.step("gpsimd", lambda E: E.dma_start(out=glAb[:], in_=glA), writes=[b_glA])
        P.step("gpsimd", lambda E: E.dma_start(out=glBTb[:], in_=glBT), writes=[b_glBT])
        P.step("sync", lambda E: E.dma_start(
            out=gwTb[:], in_=gwT.rearrange("p h e -> p (h e)")), writes=[b_gwT])

        pr = ExitStack()
        psGE = pr.enter_context(nc.psum_tensor("psGE", [128, NE], F32))
        psLt = [pr.enter_context(nc.psum_tensor(f"psLt{i}", [128, NE], F32)) for i in range(2)]
        b_psGE = Buf("psGE")
        b_psLt = [Buf(f"psLt{i}") for i in range(2)]
        for hc in range(NHC):
            P.step("tensor", lambda E, hc=hc: E.matmul(
                psGE[:], glAb[:, hc * 128:(hc + 1) * 128], glBTb[:],
                start=True, stop=True),
                reads=[b_glA, b_glBT], writes=[b_psGE])
            P.step("vector", lambda E, hc=hc: E.scalar_tensor_tensor(
                geff[:, hc * NE:(hc + 1) * NE], psGE[:], 2.0,
                gwTb[:, hc * NE:(hc + 1) * NE], ALU.mult, ALU.add),
                reads=[b_psGE, b_gwT], writes=[b_geff] if hc == NHC - 1 else [])

        for tc in range(16):
            lp = tc % 2

            def rmm(E, tc=tc, lp=lp):
                last = None
                for hc in range(NHC):
                    last = E.matmul(
                        psLt[lp][:],
                        x1n_sb[:, hc * TOK + tc * 128: hc * TOK + tc * 128 + 128],
                        geff[:, hc * NE:(hc + 1) * NE],
                        start=(hc == 0), stop=(hc == NHC - 1))
                return last
            P.step("tensor", rmm, reads=[b_x1n_sb, b_geff], writes=[b_psLt[lp]])
            P.step("vector", lambda E, lp=lp: E.tensor_copy(LT[:], psLt[lp][:]),
                   reads=[b_psLt[lp]], writes=[b_r])
            P.step("vector", lambda E: E.tensor_reduce(nm1[:], LT[:], AX.X, ALU.max,
                                                       negate=True),
                   reads=[b_r], writes=[b_r])
            P.step("scalar", lambda E: E.activation(Ee[:], LT[:], AF.Exp, bias=nm1[:]),
                   reads=[b_r], writes=[b_r])
            P.step("vector", lambda E: E.tensor_scalar_mul(m1p[:], nm1[:], -1.0),
                   reads=[b_r], writes=[b_r])
            P.step("vector", lambda E: E.tensor_single_scalar(msk1[:], LT[:], m1p[:],
                                                              ALU.is_ge),
                   reads=[b_r], writes=[b_r])
            P.step("vector", lambda E: E.tensor_scalar_mul(msk2[:], msk1[:], -BIG),
                   reads=[b_r], writes=[b_r])
            P.step("vector", lambda E: E.tensor_add(Lm[:], LT[:], msk2[:]),
                   reads=[b_r], writes=[b_r])
            P.step("vector", lambda E: E.tensor_reduce(nm2[:], Lm[:], AX.X, ALU.max,
                                                       negate=True),
                   reads=[b_r], writes=[b_r])
            P.step("vector", lambda E: E.tensor_scalar_mul(m2p[:], nm2[:], -1.0),
                   reads=[b_r], writes=[b_r])
            P.step("vector", lambda E: E.tensor_single_scalar(msk2[:], Lm[:], m2p[:],
                                                              ALU.is_ge),
                   reads=[b_r], writes=[b_r])
            P.step("vector", lambda E: E.tensor_add(msks[:], msk1[:], msk2[:]),
                   reads=[b_r], writes=[b_r])
            P.step("vector", lambda E: E.tensor_mul(wun[:], Ee[:], msks[:]),
                   reads=[b_r], writes=[b_r])
            P.step("vector", lambda E: E.tensor_reduce(den[:], wun[:], AX.X, ALU.add),
                   reads=[b_r], writes=[b_r])
            P.step("vector", lambda E: E.reciprocal(dinv[:], den[:]),
                   reads=[b_r], writes=[b_r])
            P.step("vector", lambda E, tc=tc: E.tensor_single_scalar(
                w_all[:, tc * NE:(tc + 1) * NE], wun[:], dinv[:], ALU.mult),
                reads=[b_r], writes=[b_w_all] if tc == 15 else [])
        if RDBG is not None:
            P.step("sync", lambda E: E.dma_start(out=RDBG[:, 0:128], in_=w_all[:]),
                   reads=[b_w_all])
            P.step("sync", lambda E: E.dma_start(out=RDBG[:, 128:256], in_=gwTb[:]),
                   reads=[b_gwT])
            P.step("vector", lambda E: E.tensor_copy(wbc[:, 0:128], geff[:]),
                   reads=[b_geff], writes=[b_wbc])
            P.step("sync", lambda E: E.dma_start(out=RDBG[:, 256:384], in_=wbc[:, 0:128]),
                   reads=[b_wbc])
            P.step("sync", lambda E: E.dma_start(out=RDBG[:, 384:392], in_=LT[:]),
                   reads=[b_r])
            P.step("sync", lambda E: E.dma_start(out=RDBG[:, 392:400], in_=Ee[:]),
                   reads=[b_r])
            P.step("sync", lambda E: E.dma_start(out=RDBG[:, 400:408], in_=msks[:]),
                   reads=[b_r])
            P.step("sync", lambda E: E.dma_start(out=RDBG[:, 408:409], in_=den[:], allow_slow_non_contiguous=True),
                   reads=[b_r])
        P.phase_barrier(ALLSEMS)
        pr.close()

        pg = ExitStack()
        psTr = pg.enter_context(nc.psum_tensor("psTr", [1, 512], F32))
        psW = pg.enter_context(nc.psum_tensor("psW", [128, 512], F32))
        psT2 = pg.enter_context(nc.psum_tensor("psT2", [R, 512], F32))
        psG = [pg.enter_context(nc.psum_tensor(f"psG{i}", [128, 512], F32)) for i in range(2)]
        psU = [pg.enter_context(nc.psum_tensor(f"psU{i}", [128, 512], F32)) for i in range(2)]
        b_psTr = Buf("psTr")
        b_psW = Buf("psW")
        b_psT2 = Buf("psT2")
        b_psG = [Buf(f"psG{i}") for i in range(2)]
        b_psU = [Buf(f"psU{i}") for i in range(2)]

        for e in range(NE):
            for tg in range(4):
                def trw(E, tg=tg, e=e):
                    last = None
                    for j4 in range(4):
                        tc = tg * 4 + j4
                        last = E.matmul(
                            psTr[0:1, j4 * 128:(j4 + 1) * 128],
                            w_all[:, tc * NE + e: tc * NE + e + 1], id_f[:],
                            start=(j4 == 0), stop=(j4 == 3), is_transpose=True)
                    return last
                P.step("tensor", trw, reads=[b_w_all, b_id], writes=[b_psTr])
                P.step("vector", lambda E, tg=tg: E.tensor_copy(
                    wtmp[0:1, tg * 512:(tg + 1) * 512], psTr[0:1, :]),
                    reads=[b_psTr], writes=[b_wtmp] if tg == 3 else [],
                    extra_waits=tuple(b_wtmp.free.items()) if tg == 0 else ())
            for qt in range(NQT):
                P.step("tensor", lambda E, qt=qt: E.matmul(
                    psW[:], ones_r_f[:], wtmp[0:1, qt * 512:qt * 512 + 512],
                    start=True, stop=True),
                    reads=[b_wtmp, b_ones], writes=[b_psW])
                P.step("vector", lambda E, qt=qt: E.tensor_copy(
                    wbc[:, qt * 512:qt * 512 + 512], psW[:]),
                    reads=[b_psW], writes=[b_wbc] if qt == NQT - 1 else [],
                    extra_waits=tuple(b_wbc.free.items()) if qt == 0 else ())
            P.step("gpsimd", lambda E, e=e: E.dma_start(
                out=gAb[:], in_=gupA[e].rearrange("p h r -> p (h r)")),
                writes=[b_gA], extra_waits=tuple(b_gA.free.items()))
            P.step("gpsimd", lambda E, e=e: E.dma_start(out=gBTb[:], in_=gupBT[e]),
                   writes=[b_gBT], extra_waits=tuple(b_gBT.free.items()))
            for qt in range(NQT):
                def t2mm(E, qt=qt):
                    last = None
                    for hc in range(NHC):
                        last = E.matmul(
                            psT2[:], gAb[:, hc * R:(hc + 1) * R],
                            x1n_sb[:, hc * TOK + qt * 512: hc * TOK + qt * 512 + 512],
                            start=(hc == 0), stop=(hc == NHC - 1))
                    return last
                P.step("tensor", t2mm, reads=[b_gA, b_x1n_sb], writes=[b_psT2])
                P.step("vector", lambda E, qt=qt: E.tensor_scalar_mul(
                    t2sb[:, qt * 512:qt * 512 + 512], psT2[:], 2.0),
                    reads=[b_psT2], writes=[b_t2] if qt == NQT - 1 else [],
                    extra_waits=tuple(b_t2.free.items()) if qt == 0 else ())
            for j in range(NIC):
                jb = j % 2
                P.step("gpsimd", lambda E, e=e, j=j, jb=jb: E.dma_start(
                    out=wg[jb][:], in_=gupw[e, j].rearrange("p h c -> p (h c)")),
                    writes=[b_wg[jb]])
                P.step("gpsimd", lambda E, e=e, j=j, jb=jb: E.dma_start(
                    out=wu[jb][:], in_=gupw[e, j + 8].rearrange("p h c -> p (h c)")),
                    writes=[b_wu[jb]])
                for qt in range(NQT):
                    gp = qt % 2

                    def gmm(E, jb=jb, qt=qt, gp=gp, j=j):
                        for hc in range(NHC):
                            E.matmul(psG[gp][:], wg[jb][:, hc * 128:(hc + 1) * 128],
                                     x1n_sb[:, hc * TOK + qt * 512: hc * TOK + qt * 512 + 512],
                                     start=(hc == 0), stop=False)
                        return E.matmul(psG[gp][:], gBTb[:, j * 128:(j + 1) * 128],
                                        t2sb[:, qt * 512:qt * 512 + 512],
                                        start=False, stop=True)
                    P.step("tensor", gmm, reads=[b_wg[jb], b_x1n_sb, b_gBT, b_t2],
                           writes=[b_psG[gp]])

                    def umm(E, jb=jb, qt=qt, gp=gp, j=j):
                        for hc in range(NHC):
                            E.matmul(psU[gp][:], wu[jb][:, hc * 128:(hc + 1) * 128],
                                     x1n_sb[:, hc * TOK + qt * 512: hc * TOK + qt * 512 + 512],
                                     start=(hc == 0), stop=False)
                        return E.matmul(psU[gp][:], gBTb[:, (j + 8) * 128:(j + 9) * 128],
                                        t2sb[:, qt * 512:qt * 512 + 512],
                                        start=False, stop=True)
                    P.step("tensor", umm, reads=[b_wu[jb], b_x1n_sb, b_gBT, b_t2],
                           writes=[b_psU[gp]])
                    P.step("scalar", lambda E, gp=gp: E.activation(
                        sg4[gp][:], psG[gp][:], AF.Silu),
                        reads=[b_psG[gp]], writes=[b_sg4[gp]])
                    P.step("vector", lambda E, gp=gp, qt=qt: E.tensor_mul(
                        ub4[gp][:], psU[gp][:], wbc[:, qt * 512:qt * 512 + 512]),
                        reads=[b_psU[gp], b_wbc], writes=[b_ub4[gp]])
                    P.step("vector", lambda E, gp=gp: E.tensor_mul(
                        act4[gp][:], sg4[gp][:], ub4[gp][:]),
                        reads=[b_sg4[gp], b_ub4[gp]], writes=[b_act4[gp]])
                    P.step("sync", lambda E, e=e, j=j, qt=qt, gp=gp: E.dma_start(
                        out=ACTs[e, j][:, qt * 512:qt * 512 + 512], in_=act4[gp][:]),
                        reads=[b_act4[gp]], writes=[b_ACTs])
        P.phase_barrier(ALLSEMS)
        pg.close()
        p4.close()
        right_x.close()

    if phases >= 5:
        yst = ExitStack()
        yacc = sb(yst, "yacc", [128, NHC * TOK], F32)

        p5 = ExitStack()
        actb = sb(p5, "actb", [128, NIC * TOK], BF16)
        wd = [sb(p5, f"wd{i}", [128, NIC * 128], BF16) for i in range(2)]
        dAb = sb(p5, "dAb", [128, NIC * R], BF16)
        dBTb = sb(p5, "dBTb", [R, H], BF16)
        t3sb = sb(p5, "t3sb", [R, TOK], BF16)
        psT3 = p5.enter_context(nc.psum_tensor("psT3", [R, 512], F32))
        psY = [p5.enter_context(nc.psum_tensor(f"psY{i}", [128, 512], F32)) for i in range(2)]

        b_actb = Buf("actb")
        b_wd = [Buf(f"wd{i}") for i in range(2)]
        b_dA = Buf("dA")
        b_dBT = Buf("dBT")
        b_t3 = Buf("t3")
        b_psT3 = Buf("psT3")
        b_psY = [Buf(f"psY{i}") for i in range(2)]

        for hoc in range(NHC):
            P.step("sync", lambda E, hoc=hoc: E.dma_start(
                out=yacc[:, hoc * TOK:(hoc + 1) * TOK], in_=X1n[hoc]),
                reads=[b_X1n], writes=[b_yacc] if hoc == NHC - 1 else [])

        cnt5 = [0]
        for e in range(NE):
            P.step("gpsimd", lambda E, e=e: E.dma_start(
                out=dAb[:], in_=downA[e].rearrange("p i r -> p (i r)")),
                writes=[b_dA], extra_waits=tuple(b_dA.free.items()))
            P.step("gpsimd", lambda E, e=e: E.dma_start(out=dBTb[:], in_=downBT[e]),
                   writes=[b_dBT], extra_waits=tuple(b_dBT.free.items()))
            for ic in range(NIC):
                P.step("sync", lambda E, e=e, ic=ic: E.dma_start(
                    out=actb[:, ic * TOK:(ic + 1) * TOK], in_=ACTs[e, ic]),
                    reads=[b_ACTs], writes=[b_actb] if ic == NIC - 1 else [],
                    extra_waits=tuple(b_actb.free.items()) if ic == 0 else ())
            for qt in range(NQT):
                def t3mm(E, qt=qt):
                    last = None
                    for ic in range(NIC):
                        last = E.matmul(
                            psT3[:], dAb[:, ic * R:(ic + 1) * R],
                            actb[:, ic * TOK + qt * 512: ic * TOK + qt * 512 + 512],
                            start=(ic == 0), stop=(ic == NIC - 1))
                    return last
                P.step("tensor", t3mm, reads=[b_dA, b_actb], writes=[b_psT3])
                P.step("vector", lambda E, qt=qt: E.tensor_scalar_mul(
                    t3sb[:, qt * 512:qt * 512 + 512], psT3[:], 2.0),
                    reads=[b_psT3], writes=[b_t3] if qt == NQT - 1 else [],
                    extra_waits=tuple(b_t3.free.items()) if qt == 0 else ())
            for hoc in range(NHC):
                wb = hoc % 2
                P.step("gpsimd", lambda E, e=e, hoc=hoc, wb=wb: E.dma_start(
                    out=wd[wb][:], in_=downw[e, hoc].rearrange("p i c -> p (i c)")),
                    writes=[b_wd[wb]])
                for qt in range(NQT):
                    yp = cnt5[0] % 2
                    cnt5[0] += 1

                    def ymm(E, wb=wb, qt=qt, yp=yp, hoc=hoc):
                        for ic in range(NIC):
                            E.matmul(psY[yp][:], wd[wb][:, ic * 128:(ic + 1) * 128],
                                     actb[:, ic * TOK + qt * 512: ic * TOK + qt * 512 + 512],
                                     start=(ic == 0), stop=False)
                        return E.matmul(psY[yp][:], dBTb[:, hoc * 128:(hoc + 1) * 128],
                                        t3sb[:, qt * 512:qt * 512 + 512],
                                        start=False, stop=True)
                    P.step("tensor", ymm, reads=[b_wd[wb], b_actb, b_dBT, b_t3],
                           writes=[b_psY[yp]])
                    P.step("vector", lambda E, yp=yp, hoc=hoc, qt=qt: E.tensor_add(
                        yacc[:, hoc * TOK + qt * 512: hoc * TOK + qt * 512 + 512],
                        psY[yp][:],
                        yacc[:, hoc * TOK + qt * 512: hoc * TOK + qt * 512 + 512]),
                        reads=[b_psY[yp], b_yacc], writes=[b_yacc])
        P.phase_barrier(ALLSEMS)
        p5.close()

        p6 = ExitStack()
        x6 = [sb(p6, f"x6{i}", [128, 1024], F32) for i in range(2)]
        y6b = [sb(p6, f"y6b{i}", [128, 1024], BF16) for i in range(2)]
        sq6b = [sb(p6, f"sq6b{i}", [128, 1024], BF16) for i in range(2)]
        sq6 = [sb(p6, f"sq6{i}", [128, 1024], F32) for i in range(2)]
        rows_mu6 = sb(p6, "rows_mu6", [1, TOK], F32)
        rows_e26 = sb(p6, "rows_e26", [1, TOK], F32)
        rows_m26 = sb(p6, "rows_m26", [1, TOK], F32)
        rows_rs6 = sb(p6, "rows_rs6", [1, TOK], F32)
        rows_s6 = rows_e26
        bcR6 = sb(p6, "bcR6", [128, TOK], F32)
        bcS6 = sb(p6, "bcS6", [128, TOK], F32)
        b_x6 = [Buf(f"x6{i}") for i in range(2)]
        b_y6b = [Buf(f"y6b{i}") for i in range(2)]
        b_sq6b = [Buf(f"sq6b{i}") for i in range(2)]
        b_sq6 = [Buf(f"sq6{i}") for i in range(2)]
        b_rows6 = Buf("rows6")
        b_bc6 = Buf("bc6")

        st6 = ExitStack()
        psSum6 = st6.enter_context(nc.psum_tensor("psSum6", [1, TOK], F32))
        psSq6 = st6.enter_context(nc.psum_tensor("psSq6", [1, TOK], F32))
        b_psSum6 = Buf("psSum6")
        b_psSq6 = Buf("psSq6")
        for ck in range(2 * NHC):
            hoc, hf = ck // 2, ck % 2
            pp = ck % 2
            P.step("vector", lambda E, hoc=hoc, hf=hf, pp=pp: E.tensor_copy(
                y6b[pp][:],
                yacc[:, hoc * TOK + hf * 1024: hoc * TOK + hf * 1024 + 1024]),
                reads=[b_yacc], writes=[b_y6b[pp]])
            P.step("scalar", lambda E, hoc=hoc, hf=hf, pp=pp: E.activation(
                sq6b[pp][:],
                yacc[:, hoc * TOK + hf * 1024: hoc * TOK + hf * 1024 + 1024],
                AF.Square),
                reads=[b_yacc], writes=[b_sq6b[pp]])
            for sq_i in range(2):
                qt = hf * 2 + sq_i
                P.step("tensor", lambda E, pp=pp, hoc=hoc, qt=qt, sq_i=sq_i: E.matmul(
                    psSum6[0:1, qt * 512:qt * 512 + 512], ones_c_b[:],
                    y6b[pp][:, sq_i * 512:sq_i * 512 + 512],
                    start=(hoc == 0), stop=(hoc == NHC - 1)),
                    reads=[b_y6b[pp], b_ones], writes=[b_psSum6])
                P.step("tensor", lambda E, pp=pp, hoc=hoc, qt=qt, sq_i=sq_i: E.matmul(
                    psSq6[0:1, qt * 512:qt * 512 + 512], ones_c_b[:],
                    sq6b[pp][:, sq_i * 512:sq_i * 512 + 512],
                    start=(hoc == 0), stop=(hoc == NHC - 1)),
                    reads=[b_sq6b[pp], b_ones], writes=[b_psSq6])
        P.step("vector", lambda E: E.tensor_scalar_mul(rows_mu6[:], psSum6[:], 1.0 / H),
               reads=[b_psSum6], writes=[b_rows6])
        P.step("vector", lambda E: E.tensor_scalar_mul(rows_e26[:], psSq6[:], 1.0 / H),
               reads=[b_psSq6], writes=[b_rows6])
        P.step("scalar", lambda E: E.activation(rows_m26[:], rows_mu6[:], AF.Square),
               reads=[b_rows6], writes=[b_rows6])
        P.step("vector", lambda E: E.tensor_sub(rows_rs6[:], rows_e26[:], rows_m26[:]),
               reads=[b_rows6], writes=[b_rows6])
        P.step("scalar", lambda E: E.activation(rows_m26[:], rows_rs6[:], AF.Ln, bias=epsrow[:]),
               reads=[b_rows6], writes=[b_rows6])
        P.step("scalar", lambda E: E.activation(rows_rs6[:], rows_m26[:], AF.Exp, scale=-0.5),
               reads=[b_rows6], writes=[b_rows6])
        P.step("vector", lambda E: E.tensor_mul(rows_s6[:], rows_mu6[:], rows_rs6[:]),
               reads=[b_rows6], writes=[b_rows6])
        P.phase_barrier(ALLSEMS)
        st6.close()

        stB6 = ExitStack()
        psBc6 = [stB6.enter_context(nc.psum_tensor(f"psBc6{i}", [128, 512], F32)) for i in range(2)]
        b_psBc6 = [Buf(f"psBc6{i}") for i in range(2)]
        for qt in range(NQT):
            P.step("tensor", lambda E, qt=qt: E.matmul(
                psBc6[0][:], ones_r_f[:], rows_rs6[0:1, qt * 512:qt * 512 + 512],
                start=True, stop=True),
                reads=[b_rows6, b_ones], writes=[b_psBc6[0]])
            P.step("vector", lambda E, qt=qt: E.tensor_copy(
                bcR6[:, qt * 512:qt * 512 + 512], psBc6[0][:]),
                reads=[b_psBc6[0]], writes=[b_bc6] if qt == NQT - 1 else [])
            P.step("tensor", lambda E, qt=qt: E.matmul(
                psBc6[1][:], ones_r_f[:], rows_s6[0:1, qt * 512:qt * 512 + 512],
                start=True, stop=True),
                reads=[b_rows6, b_ones], writes=[b_psBc6[1]])
            P.step("vector", lambda E, qt=qt: E.tensor_copy(
                bcS6[:, qt * 512:qt * 512 + 512], psBc6[1][:]),
                reads=[b_psBc6[1]], writes=[b_bc6] if qt == NQT - 1 else [])
        for ck in range(2 * NHC):
            hoc, hf = ck // 2, ck % 2
            pp = ck % 2
            P.step("vector", lambda E, pp=pp, hoc=hoc, hf=hf: E.tensor_mul(
                sq6[pp][:],
                yacc[:, hoc * TOK + hf * 1024: hoc * TOK + hf * 1024 + 1024],
                bcR6[:, hf * 1024:(hf + 1) * 1024]),
                reads=[b_yacc, b_bc6], writes=[b_sq6[pp]])
            P.step("vector", lambda E, pp=pp, hf=hf: E.tensor_sub(
                x6[pp][:], sq6[pp][:], bcS6[:, hf * 1024:(hf + 1) * 1024]),
                reads=[b_sq6[pp]], writes=[b_x6[pp]])
            P.step("sync", lambda E, pp=pp, hoc=hoc, hf=hf: E.dma_start(
                out=OUT[hoc * 128:(hoc + 1) * 128, hf * 1024:(hf + 1) * 1024],
                in_=x6[pp][:]),
                reads=[b_x6[pp]], writes=[])
        stB6.close()
        p6.close()
        yst.close()

    P.raw_wait("sync", ss, ss.n)
    P.raw_wait("sync", sg, sg.n)
    P.raw_wait("sync", st, st.n)
    P.raw_wait("sync", sv, sv.n)
    P.raw_wait("sync", sa, sa.n)

    nops = {e: len(P.prog[e]) for e in ENGINES}

    with nc.Block() as block:
        @block.tensor
        def _(Eng):
            for fn in P.prog["tensor"]:
                fn(Eng)

        @block.vector
        def _(Eng):
            for fn in P.prog["vector"]:
                fn(Eng)

        @block.scalar
        def _(Eng):
            for fn in P.prog["scalar"]:
                fn(Eng)

        @block.gpsimd
        def _(Eng):
            for fn in P.prog["gpsimd"]:
                fn(Eng)

        @block.sync
        def _(Eng):
            for fn in P.prog["sync"]:
                fn(Eng)

    return nc, nops


def prep_inputs(inp):
    f32 = np.float32
    x = np.asarray(inp["L0_attn_hidden_states"], f32)
    Wq = np.asarray(inp["L0_attn_Wq"], f32)
    Wk = np.asarray(inp["L0_attn_Wk"], f32)
    Wv = np.asarray(inp["L0_attn_Wv"], f32)
    Wo = np.asarray(inp["L0_attn_Wo"], f32)
    bq = np.asarray(inp["L0_attn_bq"], f32)
    bk = np.asarray(inp["L0_attn_bk"], f32)
    bv = np.asarray(inp["L0_attn_bv"], f32)
    gw = np.asarray(inp["L0_ffn_gate_weight"], f32)
    glA_ = np.asarray(inp["L0_ffn_gate_lora_A"], f32)
    glB_ = np.asarray(inp["L0_ffn_gate_lora_B"], f32)
    gup = np.asarray(inp["L0_ffn_gate_up_proj"], f32)
    dwn = np.asarray(inp["L0_ffn_down_proj"], f32)
    gA_ = np.asarray(inp["L0_ffn_gup_lora_A"], f32)
    gB_ = np.asarray(inp["L0_ffn_gup_lora_B"], f32)
    dA_ = np.asarray(inp["L0_ffn_down_lora_A"], f32)
    dB_ = np.asarray(inp["L0_ffn_down_lora_B"], f32)

    def blockT(WT):
        return np.ascontiguousarray(
            WT.reshape(NHC, 128, NHC, 128).transpose(2, 1, 0, 3))

    shared = {
        "wq": blockT(Wq.T), "wk": blockT(Wk.T), "wv": blockT(Wv.T),
        "wo": blockT(Wo.T),
        "bias3": np.ascontiguousarray(
            np.stack([bq, bk, bv]).reshape(3, 16, 128).transpose(2, 0, 1)),
        "gwT": np.ascontiguousarray(gw.reshape(NE, NHC, 128).transpose(2, 1, 0)),
        "glA": np.ascontiguousarray(glA_),
        "glBT": np.ascontiguousarray(glB_.T),
        "gupw": np.ascontiguousarray(
            gup.reshape(NE, NHC, 128, NHC, 128).transpose(0, 3, 2, 1, 4)),
        "downw": np.ascontiguousarray(
            dwn.reshape(NE, NIC, 128, NHC, 128).transpose(0, 3, 2, 1, 4)),
        "gupA": np.ascontiguousarray(
            gA_.reshape(NE, R, NHC, 128).transpose(0, 3, 2, 1)),
        "gupBT": np.ascontiguousarray(gB_.transpose(0, 2, 1)),
        "downA": np.ascontiguousarray(
            dA_.reshape(NE, R, NIC, 128).transpose(0, 3, 2, 1)),
        "downBT": np.ascontiguousarray(dB_.transpose(0, 2, 1)),
        "ident": np.eye(128, dtype=f32),
    }

    inv_freq = 1.0 / (1000000.0 ** (np.arange(0, HD, 2, dtype=np.float64) / HD))
    pos = np.arange(S, dtype=np.float64)
    fr = np.outer(inv_freq, pos)
    cosf = np.cos(fr).astype(f32)
    sinf = np.sin(fr).astype(f32)
    cosd = np.ascontiguousarray(np.vstack([cosf, cosf]))
    sind = np.ascontiguousarray(np.vstack([sinf, -sinf]))
    swap = np.zeros((128, 128), f32)
    swap[0:64, 64:128] = np.eye(64)
    swap[64:128, 0:64] = np.eye(64)
    shared["swapm"] = swap

    percore = []
    for c in range(NCORES):
        b, q = c // 4, c % 4
        percore.append({
            "xT": np.ascontiguousarray(np.roll(x[b].T, -q * TOK, axis=1)),
            "cosD": np.ascontiguousarray(np.roll(cosd, -q * TOK, axis=1)),
            "sinD": np.ascontiguousarray(np.roll(sind, -q * TOK, axis=1)),
        })
    return shared, percore


_CACHED = {}


def kernel(**inputs):
    shared, percore = prep_inputs(inputs)
    if "full" not in _CACHED:
        _CACHED["full"] = build_program()
    nc, nops = _CACHED["full"]
    in_maps = [dict(shared, **m) for m in percore]
    res = run_bass_kernel_spmd(nc, in_maps, core_ids=list(range(NCORES)))
    out = np.empty((B, S, H), np.float32)
    for c in range(NCORES):
        b, q = c // 4, c % 4
        out[b, q * TOK:(q + 1) * TOK, :] = np.asarray(
            res.results[c]["out"], np.float32).T
    return out


# version 20
# speedup vs baseline: 1.0861x; 1.0861x over previous
import numpy as np
import ml_dtypes
from contextlib import ExitStack

import concourse.bass as bass
import concourse.mybir as mybir
from concourse.bass_utils import run_bass_kernel_spmd

F32 = mybir.dt.float32
BF16 = mybir.dt.bfloat16
AF = mybir.ActivationFunctionType
ALU = mybir.AluOpType
AX = mybir.AxisListType

B, S, H = 2, 8192, 2048
NCORES = 8
TOK = 2048
NH = 16
HD = 128
NHC = H // 128
NP = S // TOK
NQT = TOK // 512
NKC = S // 128
NE, I2, I, R = 8, 2048, 1024, 16
NIC = I // 128
EPS = 1e-5
ISQ = float(1.0 / np.sqrt(128.0))
BIG = 1.0e9

ENGINES = ("tensor", "vector", "scalar", "gpsimd", "sync")


class Sem:
    def __init__(self, handle, name):
        self.h = handle
        self.name = name
        self.n = 0

    def inc(self, k):
        self.n += k
        return self.n


class Buf:
    __slots__ = ("ready", "free", "name")

    def __init__(self, name=""):
        self.ready = {}
        self.free = {}
        self.name = name


def _merge(d, sem, val):
    if d.get(sem, -1) < val:
        d[sem] = val


class Plan:
    def __init__(self, nc, sems):
        self.nc = nc
        self.prog = {e: [] for e in ENGINES}
        self.sems = sems
        self.hwm = {e: {} for e in ENGINES}

    def step(self, eng, build, reads=(), writes=(), inc=True, extra_waits=()):
        waits = {}
        for b in reads:
            for s, v in b.ready.items():
                _merge(waits, s, v)
        for b in writes:
            for s, v in b.free.items():
                _merge(waits, s, v)
        for s, v in extra_waits:
            _merge(waits, s, v)
        sem = self.sems[eng]
        hw = self.hwm[eng]
        wl = []
        for s, v in waits.items():
            assert v <= s.n, f"wait on future {s.name}:{v}>{s.n} (eng {eng})"
            if hw.get(s, -1) < v:
                wl.append((s, v))
                hw[s] = v
        amt = 16 if eng in ("gpsimd", "sync") else 1
        tgt = sem.inc(amt) if inc else None

        def fn(Eng, build=build, wl=tuple(wl), sem=sem, doinc=inc, amt=amt):
            for s, v in wl:
                Eng.wait_ge(s.h, v)
            ins = build(Eng)
            if doinc:
                ins.then_inc(sem.h, amt)
        self.prog[eng].append(fn)
        if inc:
            for b in reads:
                _merge(b.free, sem, tgt)
            for b in writes:
                b.ready = {sem: tgt}
        return tgt

    def raw_wait(self, eng, sem, val):
        if val <= 0:
            return
        hw = self.hwm[eng]
        assert val <= sem.n
        if hw.get(sem, -1) >= val:
            return
        hw[sem] = val
        self.prog[eng].append(lambda Eng, s=sem, v=val: Eng.wait_ge(s.h, v))

    def phase_barrier(self, all_sems):
        snap = [(s, s.n) for s in all_sems]
        for e in ENGINES:
            for s, v in snap:
                self.raw_wait(e, s, v)


def build_program(dbg=(), phases=6):
    nc = bass.Bass("TRN2", target_bir_lowering=False, debug=False)

    def dram_in(name, shape, dtype=F32):
        return nc.dram_tensor(name, shape, dtype, kind="ExternalInput").ap()

    def scratch(name, shape, dtype):
        kind = "ExternalOutput" if name in dbg else "Internal"
        return nc.dram_tensor(name, shape, dtype, kind=kind).ap()

    xT = dram_in("xT", [H, S])
    cosD = dram_in("cosD", [128, S])
    sinD = dram_in("sinD", [128, S])
    swapm = dram_in("swapm", [128, 128])
    wq = dram_in("wq", [NHC, 128, NHC, 128])
    wk = dram_in("wk", [NHC, 128, NHC, 128])
    wv = dram_in("wv", [NHC, 128, NHC, 128])
    wo = dram_in("wo", [NHC, 128, NHC, 128])
    bias3 = dram_in("bias3", [128, 3, 16])
    gwT = dram_in("gwT", [128, NHC, NE])
    glA = dram_in("glA", [R, H])
    glBT = dram_in("glBT", [R, NE])
    gupw = dram_in("gupw", [NE, NHC, 128, NHC, 128])
    downw = dram_in("downw", [NE, NHC, 128, NIC, 128])
    gupA = dram_in("gupA", [NE, 128, NHC, R])
    gupBT = dram_in("gupBT", [NE, R, I2])
    downA = dram_in("downA", [NE, 128, NIC, R])
    downBT = dram_in("downBT", [NE, R, H])
    ident = dram_in("ident", [128, 128])

    OUT = nc.dram_tensor("out", [H, TOK], F32, kind="ExternalOutput").ap()

    Ks = scratch("Ks", [NH, 128, S], BF16)
    Vs = scratch("Vs", [NH, 128, S], BF16)
    As = scratch("As", [NH, 128, TOK], BF16)
    X1 = scratch("X1", [NHC, 128, TOK], F32)
    X1n = scratch("X1n", [NHC, 128, TOK], F32)
    ACTs = scratch("ACTs", [NE, NIC, 128, TOK], BF16)
    RDBG = (nc.dram_tensor("RDBG", [128, 512], F32, kind="ExternalOutput").ap()
            if "RDBG" in dbg else None)

    es = ExitStack()
    sem_handles = {}
    for sname in ("st", "sv", "sa", "sg", "ss"):
        sem_handles[sname] = Sem(es.enter_context(nc.semaphore(sname)), sname)
    st, sv, sa, sg, ss = (sem_handles[k] for k in ("st", "sv", "sa", "sg", "ss"))
    ALLSEMS = [st, sv, sa, sg, ss]
    sems = {"tensor": st, "vector": sv, "scalar": sa, "gpsimd": sg, "sync": ss}
    P = Plan(nc, sems)

    def sb(stack, name, shape, dtype, side="left"):
        return stack.enter_context(nc.sbuf_tensor(name, shape, dtype, side=side))

    ones_c_f = sb(es, "ones_c_f", [128, 1], F32)
    ones_c_b = sb(es, "ones_c_b", [128, 1], BF16)
    ones_r_f = sb(es, "ones_r_f", [1, 128], F32)
    id_f = sb(es, "id_f", [128, 128], F32)
    id_b = sb(es, "id_b", [128, 128], BF16)
    swp_f = sb(es, "swp_f", [128, 128], F32)
    swp_b = sb(es, "swp_b", [128, 128], BF16)
    biases = sb(es, "biases", [128, 48], F32)
    negC = sb(es, "negC", [128, NH], F32)
    nslots = sb(es, "nslots", [1, 320], F32)
    crow = sb(es, "crow", [1, NH], F32)
    glAb = sb(es, "glAb", [R, H], BF16)
    glBTb = sb(es, "glBTb", [R, NE], BF16)
    gwTb = sb(es, "gwTb", [128, NHC * NE], F32)
    geff = sb(es, "geff", [128, NHC * NE], BF16)
    crow2 = sb(es, "crow2", [1, NH], F32)
    negrow = sb(es, "negrow", [1, NH], F32)
    epsrow = sb(es, "epsrow", [1, 1], F32)

    b_ones = Buf("ones")
    b_id = Buf("id")
    b_biases = Buf("biases")
    b_negC = Buf("negC")
    b_nslots = Buf("nslots")
    b_Qres = Buf("Qres")
    b_glA = Buf("glA")
    b_glBT = Buf("glBT")
    b_gwT = Buf("gwT")
    b_geff = Buf("geff")
    b_Ks = Buf("Ks")
    b_Vs = Buf("Vs")
    b_As = Buf("As")
    b_X1 = Buf("X1")
    b_X1n = Buf("X1n")
    b_x1n_sb = Buf("x1n_sb")
    b_ACTs = Buf("ACTs")
    b_yacc = Buf("yacc")

    right_q = ExitStack()
    Qres = sb(right_q, "Qres", [128, NH * TOK], BF16, side="right")

    P.step("vector", lambda E: E.memset(ones_c_f[:], 1.0), writes=[b_ones])
    P.step("vector", lambda E: E.memset(ones_c_b[:], 1.0), writes=[b_ones])
    P.step("vector", lambda E: E.memset(ones_r_f[:], 1.0), writes=[b_ones])
    P.step("vector", lambda E: E.memset(epsrow[:], EPS), writes=[b_ones])
    P.step("sync", lambda E: E.dma_start(out=id_f[:], in_=ident), writes=[b_id])
    P.step("vector", lambda E: E.tensor_copy(id_b[:], id_f[:]), reads=[b_id], writes=[b_id])
    P.step("sync", lambda E: E.dma_start(out=swp_f[:], in_=swapm), writes=[b_id])
    P.step("vector", lambda E: E.tensor_copy(swp_b[:], swp_f[:]), reads=[b_id], writes=[b_id])
    P.step("sync", lambda E: E.dma_start(
        out=biases[:], in_=bias3.rearrange("p a b -> p (a b)")), writes=[b_biases])

    p1 = ExitStack()
    Xp = sb(p1, "Xp", [128, NHC * TOK], BF16)
    wcol = [sb(p1, f"wcol{i}", [128, NHC * 128], BF16) for i in range(2)]
    tmpb = [sb(p1, f"tmp{i}", [128, 512], F32) for i in range(3)]
    sqb = [sb(p1, f"sq{i}", [128, 512], BF16) for i in range(3)]
    stage = [sb(p1, f"stage{i}", [128, TOK], BF16) for i in range(2)]
    vtmp = [sb(p1, f"vtmp{i}", [128, 512], BF16) for i in range(3)]
    vstage = [sb(p1, f"vstage{i}", [128, TOK], BF16) for i in range(2)]
    cosb = sb(p1, "cosb", [128, TOK], F32)
    sinb = sb(p1, "sinb", [128, TOK], F32)
    rp1 = [sb(p1, f"rp1_{i}", [128, 512], F32) for i in range(3)]
    rp2 = [sb(p1, f"rp2_{i}", [128, 512], BF16) for i in range(3)]
    psP = [p1.enter_context(nc.psum_tensor(f"psP{i}", [128, 512], F32)) for i in range(3)]
    psN = p1.enter_context(nc.psum_tensor("psN", [1, 512], F32))
    psSw = [p1.enter_context(nc.psum_tensor(f"psSw{i}", [128, 512], F32)) for i in range(3)]
    psVt1 = p1.enter_context(nc.psum_tensor("psVt1", [128, 512], BF16))

    b_Xp = Buf("Xp")
    b_wcol = [Buf(f"wcol{i}") for i in range(2)]
    b_tmp = [Buf(f"tmp{i}") for i in range(3)]
    b_sq = [Buf(f"sq{i}") for i in range(3)]
    b_stage = [Buf(f"stage{i}") for i in range(2)]
    b_vtmp = [Buf(f"vtmp{i}") for i in range(3)]
    b_vstage = [Buf(f"vstage{i}") for i in range(2)]
    b_psVt1 = Buf("psVt1")
    b_cos = Buf("cos")
    b_rp1 = [Buf(f"rp1_{i}") for i in range(3)]
    b_rp2 = [Buf(f"rp2_{i}") for i in range(3)]
    b_psSw = [Buf(f"psSw{i}") for i in range(3)]
    b_psP = [Buf(f"psP{i}") for i in range(3)]
    b_psN = Buf("psN")

    cnt1 = [0]
    pending1 = []

    def p1_panel(p):
        for f in pending1:
            f()
        pending1.clear()
        for hc in range(NHC):
            P.step("gpsimd",
                   lambda E, hc=hc, p=p: E.dma_start(
                       out=Xp[:, hc * TOK:(hc + 1) * TOK],
                       in_=xT[hc * 128:(hc + 1) * 128, p * TOK:(p + 1) * TOK]),
                   writes=[b_Xp] if hc == NHC - 1 else [],
                   extra_waits=tuple(b_Xp.free.items()) if hc == 0 else ())
        P.step("sync", lambda E, p=p: E.dma_start(
            out=cosb[:], in_=cosD[:, p * TOK:(p + 1) * TOK]),
            writes=[b_cos], extra_waits=tuple(b_cos.free.items()))
        P.step("sync", lambda E, p=p: E.dma_start(
            out=sinb[:], in_=sinD[:, p * TOK:(p + 1) * TOK]),
            writes=[b_cos])

        tensors = [("k", wk, 1), ("v", wv, 2)] + ([("q", wq, 0)] if p == 0 else [])

        def flush():
            for f in pending1:
                f()
            pending1.clear()

        for tname, wsrc, bidx in tensors:
            for dc in range(NHC):
                wb = dc % 2
                sbf = stage[dc % 2]
                b_sbf = b_stage[dc % 2]
                P.step("gpsimd",
                       lambda E, wsrc=wsrc, dc=dc, wb=wb: E.dma_start(
                           out=wcol[wb][:],
                           in_=wsrc[dc].rearrange("p h c -> p (h c)")),
                       writes=[b_wcol[wb]])
                for qt in range(NQT):
                    pp = cnt1[0] % 3
                    tb = cnt1[0] % 3
                    cnt1[0] += 1

                    def mm_group(E, dc=dc, qt=qt, wb=wb, pp=pp):
                        last = None
                        for hc in range(NHC):
                            last = E.matmul(
                                psP[pp][:],
                                wcol[wb][:, hc * 128:(hc + 1) * 128],
                                Xp[:, hc * TOK + qt * 512: hc * TOK + qt * 512 + 512],
                                start=(hc == 0), stop=(hc == NHC - 1))
                        return last
                    P.step("tensor", mm_group,
                           reads=[b_wcol[wb], b_Xp], writes=[b_psP[pp]])
                    flush()

                    def tail(tname=tname, dc=dc, qt=qt, pp=pp, tb=tb, p=p,
                             bidx=bidx, sbf=sbf, b_sbf=b_sbf):
                        if tname == "v":
                            vb = tb
                            vs = stage[dc % 2]
                            P.step("scalar",
                                   lambda E: E.add(
                                       vtmp[vb][:], psP[pp][:],
                                       biases[:, bidx * 16 + dc: bidx * 16 + dc + 1]),
                                   reads=[b_psP[pp], b_biases], writes=[b_vtmp[vb]])

                            def vtr(E, vb=vb):
                                last = None
                                for s4 in range(4):
                                    last = E.matmul(
                                        psVt1[:, s4 * 128:(s4 + 1) * 128],
                                        vtmp[vb][:, s4 * 128:(s4 + 1) * 128],
                                        id_b[:], start=(s4 == 0), stop=(s4 == 3),
                                        is_transpose=True)
                                return last
                            P.step("tensor", vtr, reads=[b_vtmp[vb], b_id],
                                   writes=[b_psVt1])
                            P.step("vector", lambda E, qt=qt, dc=dc: E.tensor_copy(
                                vstage[dc % 2][:, qt * 512:qt * 512 + 512], psVt1[:]),
                                reads=[b_psVt1],
                                writes=[b_vstage[dc % 2]] if qt == NQT - 1 else [],
                                extra_waits=tuple(b_vstage[dc % 2].free.items()) if qt == 0 else ())
                            return
                        P.step("scalar",
                               lambda E: E.add(
                                   tmpb[tb][:], psP[pp][:],
                                   biases[:, bidx * 16 + dc: bidx * 16 + dc + 1]),
                               reads=[b_psP[pp], b_biases], writes=[b_tmp[tb]])
                        P.step("scalar",
                               lambda E: E.activation(
                                   sqb[tb][:], tmpb[tb][:], AF.Square),
                               reads=[b_tmp[tb]], writes=[b_sq[tb]])
                        if tname == "q":
                            dst, dof, dstbuf = Qres, dc * TOK + qt * 512, b_Qres
                        else:
                            dst, dof, dstbuf = sbf, qt * 512, b_sbf
                        cs = qt * 512
                        rb = tb
                        P.step("vector", lambda E: E.tensor_mul(
                            rp1[rb][:], tmpb[tb][:], cosb[:, cs:cs + 512]),
                            reads=[b_tmp[tb], b_cos], writes=[b_rp1[rb]])
                        P.step("vector", lambda E: E.tensor_mul(
                            rp2[rb][:], tmpb[tb][:], sinb[:, cs:cs + 512]),
                            reads=[b_tmp[tb], b_cos], writes=[b_rp2[rb]])
                        P.step("tensor",
                               lambda E: E.matmul(
                                   psN[:], ones_c_b[:], sqb[tb][:], start=True, stop=True),
                               reads=[b_sq[tb], b_ones], writes=[b_psN])
                        P.step("tensor", lambda E: E.matmul(
                            psSw[rb][:], swp_b[:], rp2[rb][:], start=True, stop=True),
                            reads=[b_rp2[rb], b_id], writes=[b_psSw[rb]])
                        P.step("vector", lambda E, dst=dst, dof=dof: E.tensor_add(
                            dst[:, dof:dof + 512], rp1[rb][:], psSw[rb][:]),
                            reads=[b_rp1[rb], b_psSw[rb]],
                            writes=[dstbuf] if qt == NQT - 1 else [],
                            extra_waits=tuple(dstbuf.free.items()) if qt == 0 else ())
                        slot = (dc * 16 + p * 4 + qt) if tname == "k" else (256 + dc * 4 + qt)
                        P.step("vector",
                               lambda E, slot=slot: E.tensor_reduce(
                                   nslots[0:1, slot:slot + 1], psN[:], AX.X, ALU.max),
                               reads=[b_psN], writes=[b_nslots])
                    pending1.append(tail)
                if tname == "k":
                    def kdma(dc=dc, p=p, sbf=sbf, b_sbf=b_sbf):
                        P.step("sync",
                               lambda E: E.dma_start(
                                   out=Ks[dc][:, p * TOK:(p + 1) * TOK], in_=sbf[:]),
                               reads=[b_sbf], writes=[b_Ks])
                    pending1.append(kdma)
                elif tname == "v":
                    def vdma(dc=dc, p=p):
                        P.step("sync",
                               lambda E: E.dma_start(
                                   out=Vs[dc][:, p * TOK:(p + 1) * TOK],
                                   in_=vstage[dc % 2][:]),
                               reads=[b_vstage[dc % 2]], writes=[b_Vs])
                    pending1.append(vdma)
        flush()

    for p in range(NP):
        p1_panel(p)
    p1.close()
    P.phase_barrier(ALLSEMS)

    if phases >= 2:
        pnc = ExitStack()
        psNC = pnc.enter_context(nc.psum_tensor("psNC", [128, NH], F32))
        b_psNC = Buf("psNC")
        P.step("vector", lambda E: E.tensor_reduce(
            crow[:], nslots[0:1, 0:256].rearrange("a (d s) -> a d s", s=16),
            AX.X, ALU.max), reads=[b_nslots], writes=[b_negC])
        P.step("vector", lambda E: E.tensor_reduce(
            crow2[:], nslots[0:1, 256:320].rearrange("a (d s) -> a d s", s=4),
            AX.X, ALU.max), reads=[b_nslots], writes=[b_negC])
        P.step("vector", lambda E: E.tensor_mul(crow[:], crow[:], crow2[:]),
               reads=[b_negC], writes=[b_negC])
        P.step("scalar", lambda E: E.activation(crow2[:], crow[:], AF.Sqrt, scale=1.0 / 128.0),
               reads=[b_negC], writes=[b_negC])
        P.step("vector", lambda E: E.tensor_scalar_mul(negrow[:], crow2[:], -1.0),
               reads=[b_negC], writes=[b_negC])
        P.step("tensor", lambda E: E.matmul(psNC[:], ones_r_f[:], negrow[:],
                                            start=True, stop=True),
               reads=[b_negC, b_ones], writes=[b_psNC])
        P.step("vector", lambda E: E.tensor_copy(negC[:], psNC[:]),
               reads=[b_psNC], writes=[b_negC])
        P.phase_barrier(ALLSEMS)
        pnc.close()

        p2 = ExitStack()
        Kh = [sb(p2, f"Kh{i}", [128, S], BF16) for i in range(2)]
        Vtb = [sb(p2, f"Vtb{i}", [128, S], BF16) for i in range(2)]
        Pt = [sb(p2, f"Pt{i}", [128, 1024], BF16) for i in range(6)]
        linv = sb(p2, "linv", [1, 512], F32)
        lnrow = sb(p2, "lnrow", [1, 512], F32)
        lbc = sb(p2, "lbc", [128, 512], F32)
        ls2 = [sb(p2, f"ls2_{i}", [128, 512], BF16) for i in range(2)]
        ls4 = sb(p2, "ls4", [128, 512], BF16)
        laccf = [sb(p2, f"laccf{i}", [128, 512], F32) for i in range(2)]
        ast = [sb(p2, f"ast{i}", [128, TOK], BF16) for i in range(2)]
        psS = [p2.enter_context(nc.psum_tensor(f"psS{i}", [128, 1024], F32)) for i in range(2)]
        psO = [p2.enter_context(nc.psum_tensor(f"psO{i}", [128, 512], F32)) for i in range(2)]
        psLB = p2.enter_context(nc.psum_tensor("psLB", [128, 512], F32))

        b_Kh = [Buf(f"Kh{i}") for i in range(2)]
        b_Vt2 = [Buf(f"Vtb{i}") for i in range(2)]
        b_Pt = [Buf(f"Pt{i}") for i in range(6)]
        b_linv = Buf("linv")
        b_lbc = Buf("lbc")
        b_ls2 = [Buf(f"ls2_{i}") for i in range(2)]
        b_ls4 = Buf("ls4")
        b_laccf = [Buf(f"laccf{i}") for i in range(2)]
        b_ast = [Buf(f"ast{i}") for i in range(2)]
        b_psS = [Buf(f"psS{i}") for i in range(2)]
        b_psO = [Buf(f"psO{i}") for i in range(2)]
        b_psLB = Buf("psLB")

        pend2 = []
        for h in range(NH):
            hb = h % 2
            P.step("sync", lambda E, h=h, hb=hb: E.dma_start(out=Kh[hb][:], in_=Ks[h]),
                   reads=[b_Ks], writes=[b_Kh[hb]])
            P.step("sync", lambda E, h=h, hb=hb: E.dma_start(out=Vtb[hb][:], in_=Vs[h]),
                   reads=[b_Vs], writes=[b_Vt2[hb]])
            for qt in range(NQT):
                qoff = h * TOK + qt * 512
                npair = NKC // 2

                def plan_smm(ip, hb=hb, qoff=qoff):
                    pp = ip % 2
                    pt = ip % 6
                    kc0, kc1 = 2 * ip, 2 * ip + 1

                    def smm(E, kc0=kc0, kc1=kc1, pp=pp, hb=hb, qoff=qoff):
                        E.matmul(psS[pp][:, 0:512],
                                 Kh[hb][:, kc0 * 128:(kc0 + 1) * 128],
                                 Qres[:, qoff:qoff + 512], start=True, stop=True)
                        return E.matmul(psS[pp][:, 512:1024],
                                        Kh[hb][:, kc1 * 128:(kc1 + 1) * 128],
                                        Qres[:, qoff:qoff + 512], start=True, stop=True)
                    P.step("tensor", smm, reads=[b_Kh[hb], b_Qres], writes=[b_psS[pp]])
                    P.step("scalar", lambda E, pp=pp, pt=pt, h=h: E.activation(
                        Pt[pt][:], psS[pp][:], AF.Exp,
                        bias=negC[:, h:h + 1], scale=ISQ),
                        reads=[b_psS[pp], b_negC], writes=[b_Pt[pt]])

                qo = qt % 2

                def plan_pvl(ip, npair=npair, qo=qo):
                    pt = ip % 6
                    kc0, kc1 = 2 * ip, 2 * ip + 1

                    def pvl(E, kc0=kc0, kc1=kc1, pt=pt, ip=ip, npair=npair, hb=hb, qo=qo):
                        E.matmul(psO[qo][:], Vtb[hb][:, kc0 * 128:(kc0 + 1) * 128],
                                 Pt[pt][:, 0:512], start=(ip == 0), stop=False)
                        return E.matmul(psO[qo][:], Vtb[hb][:, kc1 * 128:(kc1 + 1) * 128],
                                        Pt[pt][:, 512:1024], start=False,
                                        stop=(ip == npair - 1))
                    P.step("tensor", pvl, reads=[b_Pt[pt], b_Vt2[hb]],
                           writes=[b_psO[qo]])
                    lsp = ip % 2
                    P.step("vector" if ip % 2 == 0 else "gpsimd",
                           lambda E, pt=pt, lsp=lsp: E.tensor_add(
                               ls2[lsp][:], Pt[pt][:, 0:512], Pt[pt][:, 512:1024]),
                           reads=[b_Pt[pt]], writes=[b_ls2[lsp]])
                    if ip == 0:
                        P.step("vector", lambda E, lsp=lsp, qo=qo: E.tensor_copy(
                            laccf[qo][:], ls2[lsp][:]),
                            reads=[b_ls2[lsp]], writes=[b_laccf[qo]])
                    else:
                        P.step("vector", lambda E, lsp=lsp, qo=qo: E.tensor_add(
                            laccf[qo][:], ls2[lsp][:], laccf[qo][:]),
                            reads=[b_ls2[lsp], b_laccf[qo]], writes=[b_laccf[qo]])

                for ip in range(npair):
                    plan_smm(ip)
                    if ip in (6, 14, 22) and pend2:
                        pend2.pop(0)()
                    if ip >= 2:
                        plan_pvl(ip - 2)
                plan_pvl(npair - 2)
                plan_pvl(npair - 1)

                def st1(hb=hb, qt=qt, qo=qo):
                    P.step("tensor", lambda E, qo=qo: E.matmul(
                        psLB[0:1, :], ones_c_f[:], laccf[qo][:], start=True, stop=True),
                        reads=[b_laccf[qo], b_ones], writes=[b_psLB])
                    P.step("scalar", lambda E: E.activation(
                        lnrow[:], psLB[0:1, :], AF.Ln),
                        reads=[b_psLB], writes=[b_linv])
                    P.step("scalar", lambda E: E.activation(
                        linv[:], lnrow[:], AF.Exp, scale=-1.0),
                        reads=[b_linv], writes=[b_linv])

                def st2():
                    P.step("tensor", lambda E: E.matmul(
                        psLB[:], ones_r_f[:], linv[:], start=True, stop=True),
                        reads=[b_linv, b_ones], writes=[b_psLB])
                    P.step("vector", lambda E: E.tensor_copy(lbc[:], psLB[:]),
                           reads=[b_psLB], writes=[b_lbc])

                def st3(hb=hb, qt=qt, qo=qo):
                    P.step("vector", lambda E, hb=hb, qt=qt, qo=qo: E.tensor_mul(
                        ast[hb][:, qt * 512:qt * 512 + 512], psO[qo][:], lbc[:]),
                        reads=[b_psO[qo], b_lbc],
                        writes=[b_ast[hb]] if qt == NQT - 1 else [],
                        extra_waits=tuple(b_ast[hb].free.items()) if qt == 0 else ())
                pend2.extend([st1, st2, st3])
            for f in pend2:
                f()
            pend2.clear()
            P.step("sync", lambda E, h=h, hb=hb: E.dma_start(out=As[h], in_=ast[hb][:]),
                   reads=[b_ast[hb]], writes=[b_As])

        P.phase_barrier(ALLSEMS)
        p2.close()
        right_q.close()

    if phases >= 3:
        right_x = ExitStack()
        x1n_sb = sb(right_x, "x1n_sb", [128, NHC * TOK], BF16, side="right")

        p3 = ExitStack()
        ah = sb(p3, "ah", [128, NHC * 1024], BF16)
        wob = [sb(p3, f"wob{i}", [128, NHC * 128], BF16) for i in range(2)]
        rtile = [sb(p3, f"rtile{i}", [128, 512], F32) for i in range(2)]
        x1t = [sb(p3, f"x1t{i}", [128, 512], F32) for i in range(2)]
        x1b = [sb(p3, f"x1b{i}", [128, 512], BF16) for i in range(2)]
        sq3 = [sb(p3, f"sq3{i}", [128, 512], BF16) for i in range(2)]
        rows_mu = sb(p3, "rows_mu3", [1, TOK], F32)
        rows_e2 = sb(p3, "rows_e23", [1, TOK], F32)
        rows_m2 = sb(p3, "rows_m23", [1, TOK], F32)
        rows_rs = sb(p3, "rows_rs3", [1, TOK], F32)
        rows_s = rows_e2
        psOut = [p3.enter_context(nc.psum_tensor(f"psOut{i}", [128, 512], F32)) for i in range(2)]
        psSum = p3.enter_context(nc.psum_tensor("psSum3", [1, 1024], F32))
        psSq = p3.enter_context(nc.psum_tensor("psSq3", [1, 1024], F32))

        b_ah = Buf("ah")
        b_wob = [Buf(f"wob{i}") for i in range(2)]
        b_rtile = [Buf(f"rt{i}") for i in range(2)]
        b_x1t = [Buf(f"x1t{i}") for i in range(2)]
        b_x1b = [Buf(f"x1b{i}") for i in range(2)]
        b_sq3 = [Buf(f"sq3{i}") for i in range(2)]
        b_rows = Buf("rows3")
        b_psOut = [Buf(f"psOut{i}") for i in range(2)]
        b_psSum = Buf("psSum3")
        b_psSq = Buf("psSq3")

        cnt3 = [0]
        pend3 = []
        for half in range(2):
            for dc in range(NH):
                P.step("sync", lambda E, dc=dc, half=half: E.dma_start(
                    out=ah[:, dc * 1024:(dc + 1) * 1024],
                    in_=As[dc][:, half * 1024:(half + 1) * 1024]),
                    reads=[b_As], writes=[b_ah] if dc == NH - 1 else [],
                    extra_waits=tuple(b_ah.free.items()) if dc == 0 else ())
            for hoc in range(NHC):
                wb = hoc % 2
                P.step("gpsimd", lambda E, hoc=hoc, wb=wb: E.dma_start(
                    out=wob[wb][:], in_=wo[hoc].rearrange("p h c -> p (h c)")),
                    writes=[b_wob[wb]])
                for sqt in range(2):
                    qt = half * 2 + sqt
                    pp = cnt3[0] % 2
                    cnt3[0] += 1

                    def omm(E, wb=wb, sqt=sqt, pp=pp):
                        last = None
                        for dc in range(NHC):
                            last = E.matmul(
                                psOut[pp][:], wob[wb][:, dc * 128:(dc + 1) * 128],
                                ah[:, dc * 1024 + sqt * 512: dc * 1024 + sqt * 512 + 512],
                                start=(dc == 0), stop=(dc == NHC - 1))
                        return last
                    P.step("tensor", omm, reads=[b_wob[wb], b_ah], writes=[b_psOut[pp]])
                    for f in pend3:
                        f()
                    pend3.clear()

                    def tail3(hoc=hoc, qt=qt, pp=pp, sqt=sqt):
                        P.step("sync", lambda E: E.dma_start(
                            out=rtile[pp][:],
                            in_=xT[hoc * 128:(hoc + 1) * 128, qt * 512:qt * 512 + 512]),
                            writes=[b_rtile[pp]])
                        P.step("vector", lambda E: E.tensor_add(
                            x1t[pp][:], psOut[pp][:], rtile[pp][:]),
                            reads=[b_psOut[pp], b_rtile[pp]], writes=[b_x1t[pp]])
                        P.step("sync", lambda E: E.dma_start(
                            out=X1[hoc][:, qt * 512:qt * 512 + 512], in_=x1t[pp][:]),
                            reads=[b_x1t[pp]], writes=[b_X1])
                        P.step("vector", lambda E: E.tensor_copy(
                            x1b[pp][:], x1t[pp][:]),
                            reads=[b_x1t[pp]], writes=[b_x1b[pp]])
                        P.step("scalar", lambda E: E.activation(
                            sq3[pp][:], x1t[pp][:], AF.Square),
                            reads=[b_x1t[pp]], writes=[b_sq3[pp]])
                        P.step("tensor", lambda E: E.matmul(
                            psSum[0:1, sqt * 512:sqt * 512 + 512], ones_c_b[:], x1b[pp][:],
                            start=(hoc == 0), stop=(hoc == NHC - 1)),
                            reads=[b_x1b[pp], b_ones], writes=[b_psSum])
                        P.step("tensor", lambda E: E.matmul(
                            psSq[0:1, sqt * 512:sqt * 512 + 512], ones_c_b[:], sq3[pp][:],
                            start=(hoc == 0), stop=(hoc == NHC - 1)),
                            reads=[b_sq3[pp], b_ones], writes=[b_psSq])
                    pend3.append(tail3)
            for f in pend3:
                f()
            pend3.clear()
            P.step("vector", lambda E, half=half: E.tensor_scalar_mul(
                rows_mu[0:1, half * 1024:(half + 1) * 1024], psSum[:], 1.0 / H),
                reads=[b_psSum], writes=[b_rows])
            P.step("vector", lambda E, half=half: E.tensor_scalar_mul(
                rows_e2[0:1, half * 1024:(half + 1) * 1024], psSq[:], 1.0 / H),
                reads=[b_psSq], writes=[b_rows])
        P.step("scalar", lambda E: E.activation(rows_m2[:], rows_mu[:], AF.Square),
               reads=[b_rows], writes=[b_rows])
        P.step("vector", lambda E: E.tensor_sub(rows_rs[:], rows_e2[:], rows_m2[:]),
               reads=[b_rows], writes=[b_rows])
        P.step("scalar", lambda E: E.activation(rows_m2[:], rows_rs[:], AF.Ln, bias=epsrow[:]),
               reads=[b_rows], writes=[b_rows])
        P.step("scalar", lambda E: E.activation(rows_rs[:], rows_m2[:], AF.Exp, scale=-0.5),
               reads=[b_rows], writes=[b_rows])
        P.step("vector", lambda E: E.tensor_mul(rows_s[:], rows_mu[:], rows_rs[:]),
               reads=[b_rows], writes=[b_rows])
        bcR = sb(p3, "bcR3", [128, TOK], F32)
        bcS = sb(p3, "bcS3", [128, TOK], F32)
        b_bc = Buf("bc3")
        psBc = [p3.enter_context(nc.psum_tensor(f"psBc3{i}", [128, 512], F32)) for i in range(2)]
        b_psBc = [Buf(f"psBc3{i}") for i in range(2)]
        for qt in range(NQT):
            P.step("tensor", lambda E, qt=qt: E.matmul(
                psBc[0][:], ones_r_f[:], rows_rs[0:1, qt * 512:qt * 512 + 512],
                start=True, stop=True),
                reads=[b_rows, b_ones], writes=[b_psBc[0]])
            P.step("vector", lambda E, qt=qt: E.tensor_copy(
                bcR[:, qt * 512:qt * 512 + 512], psBc[0][:]),
                reads=[b_psBc[0]], writes=[b_bc] if qt == NQT - 1 else [])
            P.step("tensor", lambda E, qt=qt: E.matmul(
                psBc[1][:], ones_r_f[:], rows_s[0:1, qt * 512:qt * 512 + 512],
                start=True, stop=True),
                reads=[b_rows, b_ones], writes=[b_psBc[1]])
            P.step("vector", lambda E, qt=qt: E.tensor_copy(
                bcS[:, qt * 512:qt * 512 + 512], psBc[1][:]),
                reads=[b_psBc[1]], writes=[b_bc] if qt == NQT - 1 else [])
        if phases >= 4:
            P.step("gpsimd", lambda E: E.dma_start(out=glAb[:], in_=glA),
                   writes=[b_glA])
            P.step("gpsimd", lambda E: E.dma_start(out=glBTb[:], in_=glBT),
                   writes=[b_glBT])
            P.step("sync", lambda E: E.dma_start(
                out=gwTb[:], in_=gwT.rearrange("p h e -> p (h e)")), writes=[b_gwT])
            for hc in range(NHC):
                P.step("tensor", lambda E, hc=hc: E.matmul(
                    psBc[0][:, 0:NE], glAb[:, hc * 128:(hc + 1) * 128], glBTb[:],
                    start=True, stop=True),
                    reads=[b_glA, b_glBT], writes=[b_psBc[0]])
                P.step("vector", lambda E, hc=hc: E.scalar_tensor_tensor(
                    geff[:, hc * NE:(hc + 1) * NE], psBc[0][:, 0:NE], 2.0,
                    gwTb[:, hc * NE:(hc + 1) * NE], ALU.mult, ALU.add),
                    reads=[b_psBc[0], b_gwT], writes=[b_geff] if hc == NHC - 1 else [])
        x1L = [sb(p3, f"x1L{i}", [128, TOK], F32) for i in range(2)]
        x1M = [sb(p3, f"x1M{i}", [128, TOK], F32) for i in range(2)]
        b_x1L = [Buf(f"x1L{i}") for i in range(2)]
        b_x1M = [Buf(f"x1M{i}") for i in range(2)]
        for hoc in range(NHC):
            pp = hoc % 2
            mule = "gpsimd" if hoc % 3 == 2 else "vector"
            P.step("sync", lambda E, hoc=hoc, pp=pp: E.dma_start(
                out=x1L[pp][:], in_=X1[hoc]),
                reads=[b_X1], writes=[b_x1L[pp]])
            P.step(mule, lambda E, pp=pp: E.tensor_mul(
                x1M[pp][:], x1L[pp][:], bcR[:]),
                reads=[b_x1L[pp], b_bc], writes=[b_x1M[pp]])
            P.step(mule, lambda E, pp=pp: E.tensor_sub(
                x1L[pp][:], x1M[pp][:], bcS[:]),
                reads=[b_x1M[pp]], writes=[b_x1L[pp]])
            P.step("vector", lambda E, pp=pp, hoc=hoc: E.tensor_copy(
                x1n_sb[:, hoc * TOK:(hoc + 1) * TOK], x1L[pp][:]),
                reads=[b_x1L[pp]],
                writes=[b_x1n_sb] if hoc == NHC - 1 else [],
                extra_waits=tuple(b_x1n_sb.free.items()) if hoc == 0 else ())
            P.step("sync", lambda E, pp=pp, hoc=hoc: E.dma_start(
                out=X1n[hoc], in_=x1L[pp][:]),
                reads=[b_x1L[pp]], writes=[b_X1n])
        P.phase_barrier(ALLSEMS)
        p3.close()

    if phases >= 4:
        p4 = ExitStack()
        LTa = sb(p4, "LTa", [128, 128], F32)
        suba = sb(p4, "suba", [128, 128], F32)
        Ea = sb(p4, "Ea", [128, 128], F32)
        m1a = sb(p4, "m1a", [128, 128], F32)
        m2a = sb(p4, "m2a", [128, 128], F32)
        Lma = sb(p4, "Lma", [128, 128], F32)
        nm1a = sb(p4, "nm1a", [128, 16], F32)
        nm2a = sb(p4, "nm2a", [128, 16], F32)
        dena = sb(p4, "dena", [128, 16], F32)
        dinva = sb(p4, "dinva", [128, 16], F32)
        w_all = sb(p4, "w_all", [128, 16 * NE], F32)
        wtmp = sb(p4, "wtmp", [1, TOK], F32)
        wbc = sb(p4, "wbc", [128, TOK], F32)
        gAb = sb(p4, "gAb", [128, NHC * R], BF16)
        gBTb = sb(p4, "gBTb", [R, I2], BF16)
        t2sb = sb(p4, "t2sb", [R, TOK], BF16)
        wg = [sb(p4, f"wg{i}", [128, NHC * 128], BF16) for i in range(2)]
        wu = [sb(p4, f"wu{i}", [128, NHC * 128], BF16) for i in range(2)]
        sg4 = [sb(p4, f"sg4{i}", [128, 512], BF16) for i in range(2)]
        ub4 = [sb(p4, f"ub4{i}", [128, 512], BF16) for i in range(2)]
        act4 = [sb(p4, f"act4{i}", [128, 512], BF16) for i in range(2)]

        b_r = Buf("router")
        b_w_all = Buf("w_all")
        b_wtmp = Buf("wtmp")
        b_wbc = Buf("wbc")
        b_gA = Buf("gA")
        b_gBT = Buf("gBT")
        b_t2 = Buf("t2")
        b_wg = [Buf(f"wg{i}") for i in range(2)]
        b_wu = [Buf(f"wu{i}") for i in range(2)]
        b_sg4 = [Buf(f"sg4{i}") for i in range(2)]
        b_ub4 = [Buf(f"ub4{i}") for i in range(2)]
        b_act4 = [Buf(f"act4{i}") for i in range(2)]

        pr = ExitStack()
        psLt = [pr.enter_context(nc.psum_tensor(f"psLt{i}", [128, NE], F32)) for i in range(2)]
        b_psLt = [Buf(f"psLt{i}") for i in range(2)]
        for tc in range(16):
            lp = tc % 2

            def rmm(E, tc=tc, lp=lp):
                last = None
                for hc in range(NHC):
                    last = E.matmul(
                        psLt[lp][:],
                        x1n_sb[:, hc * TOK + tc * 128: hc * TOK + tc * 128 + 128],
                        geff[:, hc * NE:(hc + 1) * NE],
                        start=(hc == 0), stop=(hc == NHC - 1))
                return last
            P.step("tensor", rmm, reads=[b_x1n_sb, b_geff], writes=[b_psLt[lp]])
            P.step("vector", lambda E, tc=tc, lp=lp: E.tensor_copy(
                LTa[:, tc * NE:(tc + 1) * NE], psLt[lp][:]),
                reads=[b_psLt[lp]], writes=[b_r] if tc == 15 else [])
        seg = lambda t: t.rearrange("p (a b) -> p a b", b=NE)
        bc3 = lambda t: t.unsqueeze(2).broadcast_to([128, 16, NE])
        P.step("vector", lambda E: E.tensor_reduce(
            nm1a[:], seg(LTa[:, :]), AX.X, ALU.max, negate=True),
            reads=[b_r], writes=[b_r])
        P.step("vector", lambda E: E.tensor_tensor(
            seg(suba[:, :]), seg(LTa[:, :]), bc3(nm1a[:, :]), ALU.add),
            reads=[b_r], writes=[b_r])
        P.step("scalar", lambda E: E.activation(Ea[:], suba[:], AF.Exp),
               reads=[b_r], writes=[b_r])
        P.step("vector", lambda E: E.tensor_single_scalar(
            m1a[:], suba[:], 0.0, ALU.is_ge),
            reads=[b_r], writes=[b_r])
        P.step("vector", lambda E: E.scalar_tensor_tensor(
            Lma[:], m1a[:], -BIG, suba[:], ALU.mult, ALU.add),
            reads=[b_r], writes=[b_r])
        P.step("vector", lambda E: E.tensor_reduce(
            nm2a[:], seg(Lma[:, :]), AX.X, ALU.max, negate=True),
            reads=[b_r], writes=[b_r])
        P.step("vector", lambda E: E.tensor_tensor(
            seg(Lma[:, :]), seg(Lma[:, :]), bc3(nm2a[:, :]), ALU.add),
            reads=[b_r], writes=[b_r])
        P.step("vector", lambda E: E.tensor_single_scalar(
            m2a[:], Lma[:], 0.0, ALU.is_ge),
            reads=[b_r], writes=[b_r])
        P.step("vector", lambda E: E.tensor_add(m1a[:], m1a[:], m2a[:]),
               reads=[b_r], writes=[b_r])
        P.step("vector", lambda E: E.tensor_mul(Ea[:], Ea[:], m1a[:]),
               reads=[b_r], writes=[b_r])
        P.step("vector", lambda E: E.tensor_reduce(
            dena[:], seg(Ea[:, :]), AX.X, ALU.add),
            reads=[b_r], writes=[b_r])
        P.step("vector", lambda E: E.reciprocal(dinva[:], dena[:]),
               reads=[b_r], writes=[b_r])
        P.step("vector", lambda E: E.tensor_tensor(
            seg(w_all[:, :]), seg(Ea[:, :]), bc3(dinva[:, :]), ALU.mult),
            reads=[b_r], writes=[b_w_all])
        if RDBG is not None:
            P.step("sync", lambda E: E.dma_start(out=RDBG[:, 0:128], in_=w_all[:]),
                   reads=[b_w_all])
            P.step("sync", lambda E: E.dma_start(out=RDBG[:, 128:256], in_=gwTb[:]),
                   reads=[b_gwT])
            P.step("vector", lambda E: E.tensor_copy(wbc[:, 0:128], geff[:]),
                   reads=[b_geff], writes=[b_wbc])
            P.step("sync", lambda E: E.dma_start(out=RDBG[:, 256:384], in_=wbc[:, 0:128]),
                   reads=[b_wbc])
            P.step("sync", lambda E: E.dma_start(out=RDBG[:, 384:392], in_=LT[:]),
                   reads=[b_r])
            P.step("sync", lambda E: E.dma_start(out=RDBG[:, 392:400], in_=Ee[:]),
                   reads=[b_r])
            P.step("sync", lambda E: E.dma_start(out=RDBG[:, 400:408], in_=msks[:]),
                   reads=[b_r])
            P.step("sync", lambda E: E.dma_start(out=RDBG[:, 408:409], in_=den[:], allow_slow_non_contiguous=True),
                   reads=[b_r])
        P.phase_barrier(ALLSEMS)
        pr.close()

        pg = ExitStack()
        psTr = pg.enter_context(nc.psum_tensor("psTr", [1, 512], F32))
        psW = pg.enter_context(nc.psum_tensor("psW", [128, 512], F32))
        psT2 = pg.enter_context(nc.psum_tensor("psT2", [R, 512], F32))
        psG = [pg.enter_context(nc.psum_tensor(f"psG{i}", [128, 512], F32)) for i in range(2)]
        psU = [pg.enter_context(nc.psum_tensor(f"psU{i}", [128, 512], F32)) for i in range(2)]
        b_psTr = Buf("psTr")
        b_psW = Buf("psW")
        b_psT2 = Buf("psT2")
        b_psG = [Buf(f"psG{i}") for i in range(2)]
        b_psU = [Buf(f"psU{i}") for i in range(2)]

        for e in range(NE):
            for tg in range(4):
                def trw(E, tg=tg, e=e):
                    last = None
                    for j4 in range(4):
                        tc = tg * 4 + j4
                        last = E.matmul(
                            psTr[0:1, j4 * 128:(j4 + 1) * 128],
                            w_all[:, tc * NE + e: tc * NE + e + 1], id_f[:],
                            start=(j4 == 0), stop=(j4 == 3), is_transpose=True)
                    return last
                P.step("tensor", trw, reads=[b_w_all, b_id], writes=[b_psTr])
                P.step("vector", lambda E, tg=tg: E.tensor_copy(
                    wtmp[0:1, tg * 512:(tg + 1) * 512], psTr[0:1, :]),
                    reads=[b_psTr], writes=[b_wtmp] if tg == 3 else [],
                    extra_waits=tuple(b_wtmp.free.items()) if tg == 0 else ())
            for qt in range(NQT):
                P.step("tensor", lambda E, qt=qt: E.matmul(
                    psW[:], ones_r_f[:], wtmp[0:1, qt * 512:qt * 512 + 512],
                    start=True, stop=True),
                    reads=[b_wtmp, b_ones], writes=[b_psW])
                P.step("vector", lambda E, qt=qt: E.tensor_copy(
                    wbc[:, qt * 512:qt * 512 + 512], psW[:]),
                    reads=[b_psW], writes=[b_wbc] if qt == NQT - 1 else [],
                    extra_waits=tuple(b_wbc.free.items()) if qt == 0 else ())
            P.step("gpsimd", lambda E, e=e: E.dma_start(
                out=gAb[:], in_=gupA[e].rearrange("p h r -> p (h r)")),
                writes=[b_gA], extra_waits=tuple(b_gA.free.items()))
            P.step("gpsimd", lambda E, e=e: E.dma_start(out=gBTb[:], in_=gupBT[e]),
                   writes=[b_gBT], extra_waits=tuple(b_gBT.free.items()))
            for qt in range(NQT):
                def t2mm(E, qt=qt):
                    last = None
                    for hc in range(NHC):
                        last = E.matmul(
                            psT2[:], gAb[:, hc * R:(hc + 1) * R],
                            x1n_sb[:, hc * TOK + qt * 512: hc * TOK + qt * 512 + 512],
                            start=(hc == 0), stop=(hc == NHC - 1))
                    return last
                P.step("tensor", t2mm, reads=[b_gA, b_x1n_sb], writes=[b_psT2])
                P.step("vector", lambda E, qt=qt: E.tensor_scalar_mul(
                    t2sb[:, qt * 512:qt * 512 + 512], psT2[:], 2.0),
                    reads=[b_psT2], writes=[b_t2] if qt == NQT - 1 else [],
                    extra_waits=tuple(b_t2.free.items()) if qt == 0 else ())
            for j in range(NIC):
                jb = j % 2
                P.step("gpsimd", lambda E, e=e, j=j, jb=jb: E.dma_start(
                    out=wg[jb][:], in_=gupw[e, j].rearrange("p h c -> p (h c)")),
                    writes=[b_wg[jb]])
                P.step("gpsimd", lambda E, e=e, j=j, jb=jb: E.dma_start(
                    out=wu[jb][:], in_=gupw[e, j + 8].rearrange("p h c -> p (h c)")),
                    writes=[b_wu[jb]])
                for qt in range(NQT):
                    gp = qt % 2

                    def gmm(E, jb=jb, qt=qt, gp=gp, j=j):
                        for hc in range(NHC):
                            E.matmul(psG[gp][:], wg[jb][:, hc * 128:(hc + 1) * 128],
                                     x1n_sb[:, hc * TOK + qt * 512: hc * TOK + qt * 512 + 512],
                                     start=(hc == 0), stop=False)
                        return E.matmul(psG[gp][:], gBTb[:, j * 128:(j + 1) * 128],
                                        t2sb[:, qt * 512:qt * 512 + 512],
                                        start=False, stop=True)
                    P.step("tensor", gmm, reads=[b_wg[jb], b_x1n_sb, b_gBT, b_t2],
                           writes=[b_psG[gp]])

                    def umm(E, jb=jb, qt=qt, gp=gp, j=j):
                        for hc in range(NHC):
                            E.matmul(psU[gp][:], wu[jb][:, hc * 128:(hc + 1) * 128],
                                     x1n_sb[:, hc * TOK + qt * 512: hc * TOK + qt * 512 + 512],
                                     start=(hc == 0), stop=False)
                        return E.matmul(psU[gp][:], gBTb[:, (j + 8) * 128:(j + 9) * 128],
                                        t2sb[:, qt * 512:qt * 512 + 512],
                                        start=False, stop=True)
                    P.step("tensor", umm, reads=[b_wu[jb], b_x1n_sb, b_gBT, b_t2],
                           writes=[b_psU[gp]])
                    P.step("scalar", lambda E, gp=gp: E.activation(
                        sg4[gp][:], psG[gp][:], AF.Silu),
                        reads=[b_psG[gp]], writes=[b_sg4[gp]])
                    P.step("vector", lambda E, gp=gp, qt=qt: E.tensor_mul(
                        ub4[gp][:], psU[gp][:], wbc[:, qt * 512:qt * 512 + 512]),
                        reads=[b_psU[gp], b_wbc], writes=[b_ub4[gp]])
                    P.step("vector", lambda E, gp=gp: E.tensor_mul(
                        act4[gp][:], sg4[gp][:], ub4[gp][:]),
                        reads=[b_sg4[gp], b_ub4[gp]], writes=[b_act4[gp]])
                    P.step("sync", lambda E, e=e, j=j, qt=qt, gp=gp: E.dma_start(
                        out=ACTs[e, j][:, qt * 512:qt * 512 + 512], in_=act4[gp][:]),
                        reads=[b_act4[gp]], writes=[b_ACTs])
        P.phase_barrier(ALLSEMS)
        pg.close()
        p4.close()
        right_x.close()

    if phases >= 5:
        yst = ExitStack()
        yacc = sb(yst, "yacc", [128, NHC * TOK], F32)

        p5 = ExitStack()
        actb = sb(p5, "actb", [128, NIC * TOK], BF16)
        wd = [sb(p5, f"wd{i}", [128, NIC * 128], BF16) for i in range(2)]
        dAb = sb(p5, "dAb", [128, NIC * R], BF16)
        dBTb = sb(p5, "dBTb", [R, H], BF16)
        t3sb = sb(p5, "t3sb", [R, TOK], BF16)
        psT3 = p5.enter_context(nc.psum_tensor("psT3", [R, 512], F32))
        psY = [p5.enter_context(nc.psum_tensor(f"psY{i}", [128, 512], F32)) for i in range(2)]

        b_actb = Buf("actb")
        b_wd = [Buf(f"wd{i}") for i in range(2)]
        b_dA = Buf("dA")
        b_dBT = Buf("dBT")
        b_t3 = Buf("t3")
        b_psT3 = Buf("psT3")
        b_psY = [Buf(f"psY{i}") for i in range(2)]

        for hoc in range(NHC):
            P.step("sync", lambda E, hoc=hoc: E.dma_start(
                out=yacc[:, hoc * TOK:(hoc + 1) * TOK], in_=X1n[hoc]),
                reads=[b_X1n], writes=[b_yacc] if hoc == NHC - 1 else [])

        cnt5 = [0]
        for e in range(NE):
            P.step("gpsimd", lambda E, e=e: E.dma_start(
                out=dAb[:], in_=downA[e].rearrange("p i r -> p (i r)")),
                writes=[b_dA], extra_waits=tuple(b_dA.free.items()))
            P.step("gpsimd", lambda E, e=e: E.dma_start(out=dBTb[:], in_=downBT[e]),
                   writes=[b_dBT], extra_waits=tuple(b_dBT.free.items()))
            for ic in range(NIC):
                P.step("sync", lambda E, e=e, ic=ic: E.dma_start(
                    out=actb[:, ic * TOK:(ic + 1) * TOK], in_=ACTs[e, ic]),
                    reads=[b_ACTs], writes=[b_actb] if ic == NIC - 1 else [],
                    extra_waits=tuple(b_actb.free.items()) if ic == 0 else ())
            for qt in range(NQT):
                def t3mm(E, qt=qt):
                    last = None
                    for ic in range(NIC):
                        last = E.matmul(
                            psT3[:], dAb[:, ic * R:(ic + 1) * R],
                            actb[:, ic * TOK + qt * 512: ic * TOK + qt * 512 + 512],
                            start=(ic == 0), stop=(ic == NIC - 1))
                    return last
                P.step("tensor", t3mm, reads=[b_dA, b_actb], writes=[b_psT3])
                P.step("vector", lambda E, qt=qt: E.tensor_scalar_mul(
                    t3sb[:, qt * 512:qt * 512 + 512], psT3[:], 2.0),
                    reads=[b_psT3], writes=[b_t3] if qt == NQT - 1 else [],
                    extra_waits=tuple(b_t3.free.items()) if qt == 0 else ())
            for hoc in range(NHC):
                wb = hoc % 2
                P.step("gpsimd", lambda E, e=e, hoc=hoc, wb=wb: E.dma_start(
                    out=wd[wb][:], in_=downw[e, hoc].rearrange("p i c -> p (i c)")),
                    writes=[b_wd[wb]])
                for qt in range(NQT):
                    yp = cnt5[0] % 2
                    cnt5[0] += 1

                    def ymm(E, wb=wb, qt=qt, yp=yp, hoc=hoc):
                        for ic in range(NIC):
                            E.matmul(psY[yp][:], wd[wb][:, ic * 128:(ic + 1) * 128],
                                     actb[:, ic * TOK + qt * 512: ic * TOK + qt * 512 + 512],
                                     start=(ic == 0), stop=False)
                        return E.matmul(psY[yp][:], dBTb[:, hoc * 128:(hoc + 1) * 128],
                                        t3sb[:, qt * 512:qt * 512 + 512],
                                        start=False, stop=True)
                    P.step("tensor", ymm, reads=[b_wd[wb], b_actb, b_dBT, b_t3],
                           writes=[b_psY[yp]])
                    P.step("vector", lambda E, yp=yp, hoc=hoc, qt=qt: E.tensor_add(
                        yacc[:, hoc * TOK + qt * 512: hoc * TOK + qt * 512 + 512],
                        psY[yp][:],
                        yacc[:, hoc * TOK + qt * 512: hoc * TOK + qt * 512 + 512]),
                        reads=[b_psY[yp], b_yacc], writes=[b_yacc])
        P.phase_barrier(ALLSEMS)
        p5.close()

        p6 = ExitStack()
        x6 = [sb(p6, f"x6{i}", [128, 1024], F32) for i in range(2)]
        y6b = [sb(p6, f"y6b{i}", [128, 1024], BF16) for i in range(2)]
        sq6b = [sb(p6, f"sq6b{i}", [128, 1024], BF16) for i in range(2)]
        rows_mu6 = sb(p6, "rows_mu6", [1, TOK], F32)
        rows_e26 = sb(p6, "rows_e26", [1, TOK], F32)
        rows_m26 = sb(p6, "rows_m26", [1, TOK], F32)
        rows_rs6 = sb(p6, "rows_rs6", [1, TOK], F32)
        rows_s6 = rows_e26
        bcR6 = sb(p6, "bcR6", [128, TOK], F32)
        bcS6 = sb(p6, "bcS6", [128, TOK], F32)
        b_x6 = [Buf(f"x6{i}") for i in range(2)]
        b_y6b = [Buf(f"y6b{i}") for i in range(2)]
        b_sq6b = [Buf(f"sq6b{i}") for i in range(2)]

        b_rows6 = Buf("rows6")
        b_bc6 = Buf("bc6")

        st6 = ExitStack()
        psSum6 = st6.enter_context(nc.psum_tensor("psSum6", [1, TOK], F32))
        psSq6 = st6.enter_context(nc.psum_tensor("psSq6", [1, TOK], F32))
        b_psSum6 = Buf("psSum6")
        b_psSq6 = Buf("psSq6")
        for ck in range(2 * NHC):
            hoc, hf = ck // 2, ck % 2
            pp = ck % 2
            P.step("vector", lambda E, hoc=hoc, hf=hf, pp=pp: E.tensor_copy(
                y6b[pp][:],
                yacc[:, hoc * TOK + hf * 1024: hoc * TOK + hf * 1024 + 1024]),
                reads=[b_yacc], writes=[b_y6b[pp]])
            P.step("scalar", lambda E, hoc=hoc, hf=hf, pp=pp: E.activation(
                sq6b[pp][:],
                yacc[:, hoc * TOK + hf * 1024: hoc * TOK + hf * 1024 + 1024],
                AF.Square),
                reads=[b_yacc], writes=[b_sq6b[pp]])
            for sq_i in range(2):
                qt = hf * 2 + sq_i
                P.step("tensor", lambda E, pp=pp, hoc=hoc, qt=qt, sq_i=sq_i: E.matmul(
                    psSum6[0:1, qt * 512:qt * 512 + 512], ones_c_b[:],
                    y6b[pp][:, sq_i * 512:sq_i * 512 + 512],
                    start=(hoc == 0), stop=(hoc == NHC - 1)),
                    reads=[b_y6b[pp], b_ones], writes=[b_psSum6])
                P.step("tensor", lambda E, pp=pp, hoc=hoc, qt=qt, sq_i=sq_i: E.matmul(
                    psSq6[0:1, qt * 512:qt * 512 + 512], ones_c_b[:],
                    sq6b[pp][:, sq_i * 512:sq_i * 512 + 512],
                    start=(hoc == 0), stop=(hoc == NHC - 1)),
                    reads=[b_sq6b[pp], b_ones], writes=[b_psSq6])
        P.step("vector", lambda E: E.tensor_scalar_mul(rows_mu6[:], psSum6[:], 1.0 / H),
               reads=[b_psSum6], writes=[b_rows6])
        P.step("vector", lambda E: E.tensor_scalar_mul(rows_e26[:], psSq6[:], 1.0 / H),
               reads=[b_psSq6], writes=[b_rows6])
        P.step("scalar", lambda E: E.activation(rows_m26[:], rows_mu6[:], AF.Square),
               reads=[b_rows6], writes=[b_rows6])
        P.step("vector", lambda E: E.tensor_sub(rows_rs6[:], rows_e26[:], rows_m26[:]),
               reads=[b_rows6], writes=[b_rows6])
        P.step("scalar", lambda E: E.activation(rows_m26[:], rows_rs6[:], AF.Ln, bias=epsrow[:]),
               reads=[b_rows6], writes=[b_rows6])
        P.step("scalar", lambda E: E.activation(rows_rs6[:], rows_m26[:], AF.Exp, scale=-0.5),
               reads=[b_rows6], writes=[b_rows6])
        P.step("vector", lambda E: E.tensor_mul(rows_s6[:], rows_mu6[:], rows_rs6[:]),
               reads=[b_rows6], writes=[b_rows6])
        P.phase_barrier(ALLSEMS)
        st6.close()

        stB6 = ExitStack()
        psBc6 = [stB6.enter_context(nc.psum_tensor(f"psBc6{i}", [128, 512], F32)) for i in range(2)]
        b_psBc6 = [Buf(f"psBc6{i}") for i in range(2)]
        for qt in range(NQT):
            P.step("tensor", lambda E, qt=qt: E.matmul(
                psBc6[0][:], ones_r_f[:], rows_rs6[0:1, qt * 512:qt * 512 + 512],
                start=True, stop=True),
                reads=[b_rows6, b_ones], writes=[b_psBc6[0]])
            P.step("vector", lambda E, qt=qt: E.tensor_copy(
                bcR6[:, qt * 512:qt * 512 + 512], psBc6[0][:]),
                reads=[b_psBc6[0]], writes=[b_bc6] if qt == NQT - 1 else [])
            P.step("tensor", lambda E, qt=qt: E.matmul(
                psBc6[1][:], ones_r_f[:], rows_s6[0:1, qt * 512:qt * 512 + 512],
                start=True, stop=True),
                reads=[b_rows6, b_ones], writes=[b_psBc6[1]])
            P.step("vector", lambda E, qt=qt: E.tensor_copy(
                bcS6[:, qt * 512:qt * 512 + 512], psBc6[1][:]),
                reads=[b_psBc6[1]], writes=[b_bc6] if qt == NQT - 1 else [])
        for ck in range(2 * NHC):
            hoc, hf = ck // 2, ck % 2
            pp = ck % 2
            P.step("vector", lambda E, pp=pp, hoc=hoc, hf=hf: E.tensor_mul(
                x6[pp][:],
                yacc[:, hoc * TOK + hf * 1024: hoc * TOK + hf * 1024 + 1024],
                bcR6[:, hf * 1024:(hf + 1) * 1024]),
                reads=[b_yacc, b_bc6], writes=[b_x6[pp]])
            P.step("vector", lambda E, pp=pp, hf=hf: E.tensor_sub(
                x6[pp][:], x6[pp][:], bcS6[:, hf * 1024:(hf + 1) * 1024]),
                reads=[b_x6[pp]], writes=[b_x6[pp]])
            P.step("sync", lambda E, pp=pp, hoc=hoc, hf=hf: E.dma_start(
                out=OUT[hoc * 128:(hoc + 1) * 128, hf * 1024:(hf + 1) * 1024],
                in_=x6[pp][:]),
                reads=[b_x6[pp]], writes=[])
        stB6.close()
        p6.close()
        yst.close()

    P.raw_wait("sync", ss, ss.n)
    P.raw_wait("sync", sg, sg.n)
    P.raw_wait("sync", st, st.n)
    P.raw_wait("sync", sv, sv.n)
    P.raw_wait("sync", sa, sa.n)

    nops = {e: len(P.prog[e]) for e in ENGINES}

    with nc.Block() as block:
        @block.tensor
        def _(Eng):
            for fn in P.prog["tensor"]:
                fn(Eng)

        @block.vector
        def _(Eng):
            for fn in P.prog["vector"]:
                fn(Eng)

        @block.scalar
        def _(Eng):
            for fn in P.prog["scalar"]:
                fn(Eng)

        @block.gpsimd
        def _(Eng):
            for fn in P.prog["gpsimd"]:
                fn(Eng)

        @block.sync
        def _(Eng):
            for fn in P.prog["sync"]:
                fn(Eng)

    return nc, nops


def prep_inputs(inp):
    f32 = np.float32
    x = np.asarray(inp["L0_attn_hidden_states"], f32)
    Wq = np.asarray(inp["L0_attn_Wq"], f32)
    Wk = np.asarray(inp["L0_attn_Wk"], f32)
    Wv = np.asarray(inp["L0_attn_Wv"], f32)
    Wo = np.asarray(inp["L0_attn_Wo"], f32)
    bq = np.asarray(inp["L0_attn_bq"], f32)
    bk = np.asarray(inp["L0_attn_bk"], f32)
    bv = np.asarray(inp["L0_attn_bv"], f32)
    gw = np.asarray(inp["L0_ffn_gate_weight"], f32)
    glA_ = np.asarray(inp["L0_ffn_gate_lora_A"], f32)
    glB_ = np.asarray(inp["L0_ffn_gate_lora_B"], f32)
    gup = np.asarray(inp["L0_ffn_gate_up_proj"], f32)
    dwn = np.asarray(inp["L0_ffn_down_proj"], f32)
    gA_ = np.asarray(inp["L0_ffn_gup_lora_A"], f32)
    gB_ = np.asarray(inp["L0_ffn_gup_lora_B"], f32)
    dA_ = np.asarray(inp["L0_ffn_down_lora_A"], f32)
    dB_ = np.asarray(inp["L0_ffn_down_lora_B"], f32)

    def blockT(WT):
        return np.ascontiguousarray(
            WT.reshape(NHC, 128, NHC, 128).transpose(2, 1, 0, 3))

    shared = {
        "wq": blockT(Wq.T), "wk": blockT(Wk.T), "wv": blockT(Wv.T),
        "wo": blockT(Wo.T),
        "bias3": np.ascontiguousarray(
            np.stack([bq, bk, bv]).reshape(3, 16, 128).transpose(2, 0, 1)),
        "gwT": np.ascontiguousarray(gw.reshape(NE, NHC, 128).transpose(2, 1, 0)),
        "glA": np.ascontiguousarray(glA_),
        "glBT": np.ascontiguousarray(glB_.T),
        "gupw": np.ascontiguousarray(
            gup.reshape(NE, NHC, 128, NHC, 128).transpose(0, 3, 2, 1, 4)),
        "downw": np.ascontiguousarray(
            dwn.reshape(NE, NIC, 128, NHC, 128).transpose(0, 3, 2, 1, 4)),
        "gupA": np.ascontiguousarray(
            gA_.reshape(NE, R, NHC, 128).transpose(0, 3, 2, 1)),
        "gupBT": np.ascontiguousarray(gB_.transpose(0, 2, 1)),
        "downA": np.ascontiguousarray(
            dA_.reshape(NE, R, NIC, 128).transpose(0, 3, 2, 1)),
        "downBT": np.ascontiguousarray(dB_.transpose(0, 2, 1)),
        "ident": np.eye(128, dtype=f32),
    }

    inv_freq = 1.0 / (1000000.0 ** (np.arange(0, HD, 2, dtype=np.float64) / HD))
    pos = np.arange(S, dtype=np.float64)
    fr = np.outer(inv_freq, pos)
    cosf = np.cos(fr).astype(f32)
    sinf = np.sin(fr).astype(f32)
    cosd = np.ascontiguousarray(np.vstack([cosf, cosf]))
    sind = np.ascontiguousarray(np.vstack([sinf, -sinf]))
    swap = np.zeros((128, 128), f32)
    swap[0:64, 64:128] = np.eye(64)
    swap[64:128, 0:64] = np.eye(64)
    shared["swapm"] = swap

    percore = []
    for c in range(NCORES):
        b, q = c // 4, c % 4
        percore.append({
            "xT": np.ascontiguousarray(np.roll(x[b].T, -q * TOK, axis=1)),
            "cosD": np.ascontiguousarray(np.roll(cosd, -q * TOK, axis=1)),
            "sinD": np.ascontiguousarray(np.roll(sind, -q * TOK, axis=1)),
        })
    return shared, percore


_CACHED = {}


def kernel(**inputs):
    shared, percore = prep_inputs(inputs)
    if "full" not in _CACHED:
        _CACHED["full"] = build_program()
    nc, nops = _CACHED["full"]
    in_maps = [dict(shared, **m) for m in percore]
    res = run_bass_kernel_spmd(nc, in_maps, core_ids=list(range(NCORES)))
    out = np.empty((B, S, H), np.float32)
    for c in range(NCORES):
        b, q = c // 4, c % 4
        out[b, q * TOK:(q + 1) * TOK, :] = np.asarray(
            res.results[c]["out"], np.float32).T
    return out


# version 24
# speedup vs baseline: 1.1621x; 1.0700x over previous
import numpy as np
import ml_dtypes
from contextlib import ExitStack

import concourse.bass as bass
import concourse.mybir as mybir
from concourse.bass_utils import run_bass_kernel_spmd

F32 = mybir.dt.float32
BF16 = mybir.dt.bfloat16
U32 = mybir.dt.uint32
AF = mybir.ActivationFunctionType
ALU = mybir.AluOpType
AX = mybir.AxisListType
IOA = bass.IndirectOffsetOnAxis

B, S, H = 2, 8192, 2048
NCORES = 8
TOK = 2048
NH = 16
HD = 128
NHC = H // 128
NP = S // TOK
NQT = TOK // 512
NKC = S // 128
NE, I2, I, R = 8, 2048, 1024, 16
NIC = I // 128
CAP = 1024
EPS = 1e-5
ISQ = float(1.0 / np.sqrt(128.0))
BIG = 1.0e9

ENGINES = ("tensor", "vector", "scalar", "gpsimd", "sync")


class Sem:
    def __init__(self, handle, name):
        self.h = handle
        self.name = name
        self.n = 0

    def inc(self, k):
        self.n += k
        return self.n


class Buf:
    __slots__ = ("ready", "free", "name")

    def __init__(self, name=""):
        self.ready = {}
        self.free = {}
        self.name = name


def _merge(d, sem, val):
    if d.get(sem, -1) < val:
        d[sem] = val


class Plan:
    def __init__(self, nc, sems):
        self.nc = nc
        self.prog = {e: [] for e in ENGINES}
        self.sems = sems
        self.hwm = {e: {} for e in ENGINES}

    def step(self, eng, build, reads=(), writes=(), inc=True, extra_waits=()):
        waits = {}
        for b in reads:
            for s, v in b.ready.items():
                _merge(waits, s, v)
        for b in writes:
            for s, v in b.free.items():
                _merge(waits, s, v)
        for s, v in extra_waits:
            _merge(waits, s, v)
        sem = self.sems[eng]
        hw = self.hwm[eng]
        wl = []
        for s, v in waits.items():
            assert v <= s.n, f"wait on future {s.name}:{v}>{s.n} (eng {eng})"
            if hw.get(s, -1) < v:
                wl.append((s, v))
                hw[s] = v
        amt = 16 if eng in ("gpsimd", "sync") else 1
        tgt = sem.inc(amt) if inc else None

        def fn(Eng, build=build, wl=tuple(wl), sem=sem, doinc=inc, amt=amt):
            for s, v in wl:
                Eng.wait_ge(s.h, v)
            ins = build(Eng)
            if doinc:
                ins.then_inc(sem.h, amt)
        self.prog[eng].append(fn)
        if inc:
            for b in reads:
                _merge(b.free, sem, tgt)
            for b in writes:
                b.ready = {sem: tgt}
        return tgt

    def raw_wait(self, eng, sem, val):
        if val <= 0:
            return
        hw = self.hwm[eng]
        assert val <= sem.n
        if hw.get(sem, -1) >= val:
            return
        hw[sem] = val
        self.prog[eng].append(lambda Eng, s=sem, v=val: Eng.wait_ge(s.h, v))

    def phase_barrier(self, all_sems):
        snap = [(s, s.n) for s in all_sems]
        for e in ENGINES:
            for s, v in snap:
                self.raw_wait(e, s, v)


def build_program(dbg=(), phases=6):
    nc = bass.Bass("TRN2", target_bir_lowering=False, debug=False)

    def dram_in(name, shape, dtype=F32):
        return nc.dram_tensor(name, shape, dtype, kind="ExternalInput").ap()

    def scratch(name, shape, dtype):
        kind = "ExternalOutput" if name in dbg else "Internal"
        return nc.dram_tensor(name, shape, dtype, kind=kind).ap()

    xT = dram_in("xT", [H, S])
    cosD = dram_in("cosD", [128, S])
    sinD = dram_in("sinD", [128, S])
    swapm = dram_in("swapm", [128, 128])
    wq = dram_in("wq", [NHC, 128, NHC, 128])
    wk = dram_in("wk", [NHC, 128, NHC, 128])
    wv = dram_in("wv", [NHC, 128, NHC, 128])
    wo = dram_in("wo", [NHC, 128, NHC, 128])
    bias3 = dram_in("bias3", [128, 3, 16])
    gwT = dram_in("gwT", [128, NHC, NE])
    glA = dram_in("glA", [R, H])
    glBT = dram_in("glBT", [R, NE])
    gupw = dram_in("gupw", [NE, NHC, 128, NHC, 128])
    downw = dram_in("downw", [NE, NIC, 128, H])
    gupA = dram_in("gupA", [NE, 128, NHC, R])
    gupBT = dram_in("gupBT", [NE, R, I2])
    downA = dram_in("downA", [NE, 128, NIC, R])
    downBT = dram_in("downBT", [NE, R, H])
    ident = dram_in("ident", [128, 128])
    ltri = dram_in("ltri", [128, 128])
    mpre = dram_in("mpre", [128, 128])
    ecmat = dram_in("ecmat", [128, 128])
    tcc2 = dram_in("tcc2", [128, 32])

    OUT = nc.dram_tensor("out", [TOK, H], F32, kind="ExternalOutput").ap()

    Ks = scratch("Ks", [NH, 128, S], BF16)
    Vs = scratch("Vs", [NH, 128, S], BF16)
    As = scratch("As", [NH, 128, TOK], BF16)
    X1 = scratch("X1", [NHC, 128, TOK], F32)
    X1n = scratch("X1n", [NHC, 128, TOK], F32)
    X1tok = scratch("X1tok", [TOK + 128, H], F32)
    idxbuf = scratch("idxbuf", [NE * CAP, 2], U32)
    Dall = scratch("Dall", [NE * CAP, H], F32)
    RDBG = (nc.dram_tensor("RDBG", [128, 512], F32, kind="ExternalOutput").ap()
            if "RDBG" in dbg else None)

    es = ExitStack()
    sem_handles = {}
    for sname in ("st", "sv", "sa", "sg", "ss"):
        sem_handles[sname] = Sem(es.enter_context(nc.semaphore(sname)), sname)
    st, sv, sa, sg, ss = (sem_handles[k] for k in ("st", "sv", "sa", "sg", "ss"))
    ALLSEMS = [st, sv, sa, sg, ss]
    sems = {"tensor": st, "vector": sv, "scalar": sa, "gpsimd": sg, "sync": ss}
    P = Plan(nc, sems)

    def sb(stack, name, shape, dtype, side="left"):
        return stack.enter_context(nc.sbuf_tensor(name, shape, dtype, side=side))

    ones_c_f = sb(es, "ones_c_f", [128, 1], F32)
    ones_c_b = sb(es, "ones_c_b", [128, 1], BF16)
    ones_r_f = sb(es, "ones_r_f", [1, 128], F32)
    id_f = sb(es, "id_f", [128, 128], F32)
    id_b = sb(es, "id_b", [128, 128], BF16)
    swp_f = sb(es, "swp_f", [128, 128], F32)
    swp_b = sb(es, "swp_b", [128, 128], BF16)
    biases = sb(es, "biases", [128, 48], F32)
    negC = sb(es, "negC", [128, NH], F32)
    nslots = sb(es, "nslots", [1, 320], F32)
    crow = sb(es, "crow", [1, NH], F32)
    glAb = sb(es, "glAb", [R, H], BF16)
    glBTb = sb(es, "glBTb", [R, NE], BF16)
    gwTb = sb(es, "gwTb", [128, NHC * NE], F32)
    geff = sb(es, "geff", [128, NHC * NE], BF16)
    crow2 = sb(es, "crow2", [1, NH], F32)
    negrow = sb(es, "negrow", [1, NH], F32)
    epsrow = sb(es, "epsrow", [1, 1], F32)
    epscol = sb(es, "epscol", [128, 1], F32)
    sidxu = [sb(es, f"sidxu{r}", [128, 16], U32) for r in range(2)]
    wsel = [sb(es, f"wsel{r}", [128, 16], F32) for r in range(2)]

    b_ones = Buf("ones")
    b_id = Buf("id")
    b_biases = Buf("biases")
    b_negC = Buf("negC")
    b_nslots = Buf("nslots")
    b_Qres = Buf("Qres")
    b_glA = Buf("glA")
    b_glBT = Buf("glBT")
    b_gwT = Buf("gwT")
    b_geff = Buf("geff")
    b_Ks = Buf("Ks")
    b_Vs = Buf("Vs")
    b_As = Buf("As")
    b_X1 = Buf("X1")
    b_X1n = Buf("X1n")
    b_x1n_sb = Buf("x1n_sb")
    b_X1tok = Buf("X1tok")
    b_idxbuf = Buf("idxbuf")
    b_Dall = Buf("Dall")
    b_sidxu = Buf("sidxu")
    b_wsel = Buf("wsel")

    right_q = ExitStack()
    Qres = sb(right_q, "Qres", [128, NH * TOK], BF16, side="right")

    P.step("vector", lambda E: E.memset(ones_c_f[:], 1.0), writes=[b_ones])
    P.step("vector", lambda E: E.memset(ones_c_b[:], 1.0), writes=[b_ones])
    P.step("vector", lambda E: E.memset(ones_r_f[:], 1.0), writes=[b_ones])
    P.step("vector", lambda E: E.memset(epsrow[:], EPS), writes=[b_ones])
    P.step("vector", lambda E: E.memset(epscol[:], EPS), writes=[b_ones])
    P.step("sync", lambda E: E.dma_start(out=id_f[:], in_=ident), writes=[b_id])
    P.step("vector", lambda E: E.tensor_copy(id_b[:], id_f[:]), reads=[b_id], writes=[b_id])
    P.step("sync", lambda E: E.dma_start(out=swp_f[:], in_=swapm), writes=[b_id])
    P.step("vector", lambda E: E.tensor_copy(swp_b[:], swp_f[:]), reads=[b_id], writes=[b_id])
    P.step("sync", lambda E: E.dma_start(
        out=biases[:], in_=bias3.rearrange("p a b -> p (a b)")), writes=[b_biases])

    REGS = {}

    def _mkregs(E):
        REGS["tok"] = E.to_reg(TOK)
        REGS["slots"] = E.to_reg(NE * CAP - 1)
        return None
    P.step("gpsimd", _mkregs, inc=False)

    p1 = ExitStack()
    Xp = sb(p1, "Xp", [128, NHC * TOK], BF16)
    wcol = [sb(p1, f"wcol{i}", [128, NHC * 128], BF16) for i in range(2)]
    tmpb = [sb(p1, f"tmp{i}", [128, 512], F32) for i in range(3)]
    sqb = [sb(p1, f"sq{i}", [128, 512], BF16) for i in range(3)]
    stage = [sb(p1, f"stage{i}", [128, TOK], BF16) for i in range(2)]
    vtmp = [sb(p1, f"vtmp{i}", [128, 512], BF16) for i in range(3)]
    vstage = [sb(p1, f"vstage{i}", [128, TOK], BF16) for i in range(2)]
    cosb = sb(p1, "cosb", [128, TOK], F32)
    sinb = sb(p1, "sinb", [128, TOK], F32)
    rp1 = [sb(p1, f"rp1_{i}", [128, 512], F32) for i in range(3)]
    rp2 = [sb(p1, f"rp2_{i}", [128, 512], BF16) for i in range(3)]
    psP = [p1.enter_context(nc.psum_tensor(f"psP{i}", [128, 512], F32)) for i in range(3)]
    psN = p1.enter_context(nc.psum_tensor("psN", [1, 512], F32))
    psSw = [p1.enter_context(nc.psum_tensor(f"psSw{i}", [128, 512], F32)) for i in range(3)]
    psVt1 = p1.enter_context(nc.psum_tensor("psVt1", [128, 512], BF16))

    b_Xp = Buf("Xp")
    b_wcol = [Buf(f"wcol{i}") for i in range(2)]
    b_tmp = [Buf(f"tmp{i}") for i in range(3)]
    b_sq = [Buf(f"sq{i}") for i in range(3)]
    b_stage = [Buf(f"stage{i}") for i in range(2)]
    b_vtmp = [Buf(f"vtmp{i}") for i in range(3)]
    b_vstage = [Buf(f"vstage{i}") for i in range(2)]
    b_psVt1 = Buf("psVt1")
    b_cos = Buf("cos")
    b_rp1 = [Buf(f"rp1_{i}") for i in range(3)]
    b_rp2 = [Buf(f"rp2_{i}") for i in range(3)]
    b_psSw = [Buf(f"psSw{i}") for i in range(3)]
    b_psP = [Buf(f"psP{i}") for i in range(3)]
    b_psN = Buf("psN")

    cnt1 = [0]
    pending1 = []

    def p1_panel(p):
        for f in pending1:
            f()
        pending1.clear()
        for hc in range(NHC):
            P.step("gpsimd",
                   lambda E, hc=hc, p=p: E.dma_start(
                       out=Xp[:, hc * TOK:(hc + 1) * TOK],
                       in_=xT[hc * 128:(hc + 1) * 128, p * TOK:(p + 1) * TOK]),
                   writes=[b_Xp] if hc == NHC - 1 else [],
                   extra_waits=tuple(b_Xp.free.items()) if hc == 0 else ())
        P.step("sync", lambda E, p=p: E.dma_start(
            out=cosb[:], in_=cosD[:, p * TOK:(p + 1) * TOK]),
            writes=[b_cos], extra_waits=tuple(b_cos.free.items()))
        P.step("sync", lambda E, p=p: E.dma_start(
            out=sinb[:], in_=sinD[:, p * TOK:(p + 1) * TOK]),
            writes=[b_cos])

        tensors = [("k", wk, 1), ("v", wv, 2)] + ([("q", wq, 0)] if p == 0 else [])

        def flush():
            for f in pending1:
                f()
            pending1.clear()

        for tname, wsrc, bidx in tensors:
            for dc in range(NHC):
                wb = dc % 2
                sbf = stage[dc % 2]
                b_sbf = b_stage[dc % 2]
                P.step("gpsimd",
                       lambda E, wsrc=wsrc, dc=dc, wb=wb: E.dma_start(
                           out=wcol[wb][:],
                           in_=wsrc[dc].rearrange("p h c -> p (h c)")),
                       writes=[b_wcol[wb]])
                for qt in range(NQT):
                    pp = cnt1[0] % 3
                    tb = cnt1[0] % 3
                    cnt1[0] += 1

                    def mm_group(E, dc=dc, qt=qt, wb=wb, pp=pp):
                        last = None
                        for hc in range(NHC):
                            last = E.matmul(
                                psP[pp][:],
                                wcol[wb][:, hc * 128:(hc + 1) * 128],
                                Xp[:, hc * TOK + qt * 512: hc * TOK + qt * 512 + 512],
                                start=(hc == 0), stop=(hc == NHC - 1))
                        return last
                    P.step("tensor", mm_group,
                           reads=[b_wcol[wb], b_Xp], writes=[b_psP[pp]])
                    flush()

                    def tail(tname=tname, dc=dc, qt=qt, pp=pp, tb=tb, p=p,
                             bidx=bidx, sbf=sbf, b_sbf=b_sbf):
                        if tname == "v":
                            vb = tb
                            vs = stage[dc % 2]
                            P.step("scalar",
                                   lambda E: E.add(
                                       vtmp[vb][:], psP[pp][:],
                                       biases[:, bidx * 16 + dc: bidx * 16 + dc + 1]),
                                   reads=[b_psP[pp], b_biases], writes=[b_vtmp[vb]])

                            def vtr(E, vb=vb):
                                last = None
                                for s4 in range(4):
                                    last = E.matmul(
                                        psVt1[:, s4 * 128:(s4 + 1) * 128],
                                        vtmp[vb][:, s4 * 128:(s4 + 1) * 128],
                                        id_b[:], start=(s4 == 0), stop=(s4 == 3),
                                        is_transpose=True)
                                return last
                            P.step("tensor", vtr, reads=[b_vtmp[vb], b_id],
                                   writes=[b_psVt1])
                            P.step("vector", lambda E, qt=qt, dc=dc: E.tensor_copy(
                                vstage[dc % 2][:, qt * 512:qt * 512 + 512], psVt1[:]),
                                reads=[b_psVt1],
                                writes=[b_vstage[dc % 2]] if qt == NQT - 1 else [],
                                extra_waits=tuple(b_vstage[dc % 2].free.items()) if qt == 0 else ())
                            return
                        P.step("scalar",
                               lambda E: E.add(
                                   tmpb[tb][:], psP[pp][:],
                                   biases[:, bidx * 16 + dc: bidx * 16 + dc + 1]),
                               reads=[b_psP[pp], b_biases], writes=[b_tmp[tb]])
                        P.step("scalar",
                               lambda E: E.activation(
                                   sqb[tb][:], tmpb[tb][:], AF.Square),
                               reads=[b_tmp[tb]], writes=[b_sq[tb]])
                        if tname == "q":
                            dst, dof, dstbuf = Qres, dc * TOK + qt * 512, b_Qres
                        else:
                            dst, dof, dstbuf = sbf, qt * 512, b_sbf
                        cs = qt * 512
                        rb = tb
                        P.step("vector", lambda E: E.tensor_mul(
                            rp1[rb][:], tmpb[tb][:], cosb[:, cs:cs + 512]),
                            reads=[b_tmp[tb], b_cos], writes=[b_rp1[rb]])
                        P.step("vector", lambda E: E.tensor_mul(
                            rp2[rb][:], tmpb[tb][:], sinb[:, cs:cs + 512]),
                            reads=[b_tmp[tb], b_cos], writes=[b_rp2[rb]])
                        P.step("tensor",
                               lambda E: E.matmul(
                                   psN[:], ones_c_b[:], sqb[tb][:], start=True, stop=True),
                               reads=[b_sq[tb], b_ones], writes=[b_psN])
                        P.step("tensor", lambda E: E.matmul(
                            psSw[rb][:], swp_b[:], rp2[rb][:], start=True, stop=True),
                            reads=[b_rp2[rb], b_id], writes=[b_psSw[rb]])
                        P.step("vector", lambda E, dst=dst, dof=dof: E.tensor_add(
                            dst[:, dof:dof + 512], rp1[rb][:], psSw[rb][:]),
                            reads=[b_rp1[rb], b_psSw[rb]],
                            writes=[dstbuf] if qt == NQT - 1 else [],
                            extra_waits=tuple(dstbuf.free.items()) if qt == 0 else ())
                        slot = (dc * 16 + p * 4 + qt) if tname == "k" else (256 + dc * 4 + qt)
                        P.step("vector",
                               lambda E, slot=slot: E.tensor_reduce(
                                   nslots[0:1, slot:slot + 1], psN[:], AX.X, ALU.max),
                               reads=[b_psN], writes=[b_nslots])
                    pending1.append(tail)
                if tname == "k":
                    def kdma(dc=dc, p=p, sbf=sbf, b_sbf=b_sbf):
                        P.step("sync",
                               lambda E: E.dma_start(
                                   out=Ks[dc][:, p * TOK:(p + 1) * TOK], in_=sbf[:]),
                               reads=[b_sbf], writes=[b_Ks])
                    pending1.append(kdma)
                elif tname == "v":
                    def vdma(dc=dc, p=p):
                        P.step("sync",
                               lambda E: E.dma_start(
                                   out=Vs[dc][:, p * TOK:(p + 1) * TOK],
                                   in_=vstage[dc % 2][:]),
                               reads=[b_vstage[dc % 2]], writes=[b_Vs])
                    pending1.append(vdma)
        flush()

    for p in range(NP):
        p1_panel(p)
    p1.close()
    P.phase_barrier(ALLSEMS)

    if phases >= 2:
        pnc = ExitStack()
        psNC = pnc.enter_context(nc.psum_tensor("psNC", [128, NH], F32))
        b_psNC = Buf("psNC")
        P.step("vector", lambda E: E.tensor_reduce(
            crow[:], nslots[0:1, 0:256].rearrange("a (d s) -> a d s", s=16),
            AX.X, ALU.max), reads=[b_nslots], writes=[b_negC])
        P.step("vector", lambda E: E.tensor_reduce(
            crow2[:], nslots[0:1, 256:320].rearrange("a (d s) -> a d s", s=4),
            AX.X, ALU.max), reads=[b_nslots], writes=[b_negC])
        P.step("vector", lambda E: E.tensor_mul(crow[:], crow[:], crow2[:]),
               reads=[b_negC], writes=[b_negC])
        P.step("scalar", lambda E: E.activation(crow2[:], crow[:], AF.Sqrt, scale=1.0 / 128.0),
               reads=[b_negC], writes=[b_negC])
        P.step("vector", lambda E: E.tensor_scalar_mul(negrow[:], crow2[:], -1.0),
               reads=[b_negC], writes=[b_negC])
        P.step("tensor", lambda E: E.matmul(psNC[:], ones_r_f[:], negrow[:],
                                            start=True, stop=True),
               reads=[b_negC, b_ones], writes=[b_psNC])
        P.step("vector", lambda E: E.tensor_copy(negC[:], psNC[:]),
               reads=[b_psNC], writes=[b_negC])
        P.phase_barrier(ALLSEMS)
        pnc.close()

        p2 = ExitStack()
        Kh = [sb(p2, f"Kh{i}", [128, S], BF16) for i in range(2)]
        Vtb = [sb(p2, f"Vtb{i}", [128, S], BF16) for i in range(2)]
        Pt = [sb(p2, f"Pt{i}", [128, 1024], BF16) for i in range(6)]
        linv = sb(p2, "linv", [1, 512], F32)
        lnrow = sb(p2, "lnrow", [1, 512], F32)
        lbc = sb(p2, "lbc", [128, 512], F32)
        ls2 = [sb(p2, f"ls2_{i}", [128, 512], BF16) for i in range(2)]
        ls4 = sb(p2, "ls4", [128, 512], BF16)
        laccf = [sb(p2, f"laccf{i}", [128, 512], F32) for i in range(2)]
        ast = [sb(p2, f"ast{i}", [128, TOK], BF16) for i in range(2)]
        psS = [p2.enter_context(nc.psum_tensor(f"psS{i}", [128, 1024], F32)) for i in range(2)]
        psO = [p2.enter_context(nc.psum_tensor(f"psO{i}", [128, 512], F32)) for i in range(2)]
        psLB = p2.enter_context(nc.psum_tensor("psLB", [128, 512], F32))

        b_Kh = [Buf(f"Kh{i}") for i in range(2)]
        b_Vt2 = [Buf(f"Vtb{i}") for i in range(2)]
        b_Pt = [Buf(f"Pt{i}") for i in range(6)]
        b_linv = Buf("linv")
        b_lbc = Buf("lbc")
        b_ls2 = [Buf(f"ls2_{i}") for i in range(2)]
        b_ls4 = Buf("ls4")
        b_laccf = [Buf(f"laccf{i}") for i in range(2)]
        b_ast = [Buf(f"ast{i}") for i in range(2)]
        b_psS = [Buf(f"psS{i}") for i in range(2)]
        b_psO = [Buf(f"psO{i}") for i in range(2)]
        b_psLB = Buf("psLB")

        pend2 = []
        for h in range(NH):
            hb = h % 2
            P.step("sync", lambda E, h=h, hb=hb: E.dma_start(out=Kh[hb][:], in_=Ks[h]),
                   reads=[b_Ks], writes=[b_Kh[hb]])
            P.step("sync", lambda E, h=h, hb=hb: E.dma_start(out=Vtb[hb][:], in_=Vs[h]),
                   reads=[b_Vs], writes=[b_Vt2[hb]])
            for qt in range(NQT):
                qoff = h * TOK + qt * 512
                npair = NKC // 2

                def plan_smm(ip, hb=hb, qoff=qoff):
                    pp = ip % 2
                    pt = ip % 6
                    kc0, kc1 = 2 * ip, 2 * ip + 1

                    def smm(E, kc0=kc0, kc1=kc1, pp=pp, hb=hb, qoff=qoff):
                        E.matmul(psS[pp][:, 0:512],
                                 Kh[hb][:, kc0 * 128:(kc0 + 1) * 128],
                                 Qres[:, qoff:qoff + 512], start=True, stop=True)
                        return E.matmul(psS[pp][:, 512:1024],
                                        Kh[hb][:, kc1 * 128:(kc1 + 1) * 128],
                                        Qres[:, qoff:qoff + 512], start=True, stop=True)
                    P.step("tensor", smm, reads=[b_Kh[hb], b_Qres], writes=[b_psS[pp]])
                    P.step("scalar", lambda E, pp=pp, pt=pt, h=h: E.activation(
                        Pt[pt][:], psS[pp][:], AF.Exp,
                        bias=negC[:, h:h + 1], scale=ISQ),
                        reads=[b_psS[pp], b_negC], writes=[b_Pt[pt]])

                qo = qt % 2

                def plan_pvl(ip, npair=npair, qo=qo):
                    pt = ip % 6
                    kc0, kc1 = 2 * ip, 2 * ip + 1

                    def pvl(E, kc0=kc0, kc1=kc1, pt=pt, ip=ip, npair=npair, hb=hb, qo=qo):
                        E.matmul(psO[qo][:], Vtb[hb][:, kc0 * 128:(kc0 + 1) * 128],
                                 Pt[pt][:, 0:512], start=(ip == 0), stop=False)
                        return E.matmul(psO[qo][:], Vtb[hb][:, kc1 * 128:(kc1 + 1) * 128],
                                        Pt[pt][:, 512:1024], start=False,
                                        stop=(ip == npair - 1))
                    P.step("tensor", pvl, reads=[b_Pt[pt], b_Vt2[hb]],
                           writes=[b_psO[qo]])
                    lsp = ip % 2
                    P.step("vector" if ip % 2 == 0 else "gpsimd",
                           lambda E, pt=pt, lsp=lsp: E.tensor_add(
                               ls2[lsp][:], Pt[pt][:, 0:512], Pt[pt][:, 512:1024]),
                           reads=[b_Pt[pt]], writes=[b_ls2[lsp]])
                    if ip == 0:
                        P.step("vector", lambda E, lsp=lsp, qo=qo: E.tensor_copy(
                            laccf[qo][:], ls2[lsp][:]),
                            reads=[b_ls2[lsp]], writes=[b_laccf[qo]])
                    else:
                        P.step("vector", lambda E, lsp=lsp, qo=qo: E.tensor_add(
                            laccf[qo][:], ls2[lsp][:], laccf[qo][:]),
                            reads=[b_ls2[lsp], b_laccf[qo]], writes=[b_laccf[qo]])

                for ip in range(npair):
                    plan_smm(ip)
                    if ip in (6, 14, 22) and pend2:
                        pend2.pop(0)()
                    if ip >= 2:
                        plan_pvl(ip - 2)
                plan_pvl(npair - 2)
                plan_pvl(npair - 1)

                def st1(hb=hb, qt=qt, qo=qo):
                    P.step("tensor", lambda E, qo=qo: E.matmul(
                        psLB[0:1, :], ones_c_f[:], laccf[qo][:], start=True, stop=True),
                        reads=[b_laccf[qo], b_ones], writes=[b_psLB])
                    P.step("scalar", lambda E: E.activation(
                        lnrow[:], psLB[0:1, :], AF.Ln),
                        reads=[b_psLB], writes=[b_linv])
                    P.step("scalar", lambda E: E.activation(
                        linv[:], lnrow[:], AF.Exp, scale=-1.0),
                        reads=[b_linv], writes=[b_linv])

                def st2():
                    P.step("tensor", lambda E: E.matmul(
                        psLB[:], ones_r_f[:], linv[:], start=True, stop=True),
                        reads=[b_linv, b_ones], writes=[b_psLB])
                    P.step("vector", lambda E: E.tensor_copy(lbc[:], psLB[:]),
                           reads=[b_psLB], writes=[b_lbc])

                def st3(hb=hb, qt=qt, qo=qo):
                    P.step("vector", lambda E, hb=hb, qt=qt, qo=qo: E.tensor_mul(
                        ast[hb][:, qt * 512:qt * 512 + 512], psO[qo][:], lbc[:]),
                        reads=[b_psO[qo], b_lbc],
                        writes=[b_ast[hb]] if qt == NQT - 1 else [],
                        extra_waits=tuple(b_ast[hb].free.items()) if qt == 0 else ())
                pend2.extend([st1, st2, st3])
            for f in pend2:
                f()
            pend2.clear()
            P.step("sync", lambda E, h=h, hb=hb: E.dma_start(out=As[h], in_=ast[hb][:]),
                   reads=[b_ast[hb]], writes=[b_As])

        P.phase_barrier(ALLSEMS)
        p2.close()
        right_q.close()

    if phases >= 3:
        right_x = ExitStack()
        x1n_sb = sb(right_x, "x1n_sb", [128, NHC * TOK], BF16, side="right")

        p3 = ExitStack()
        ah = sb(p3, "ah", [128, NHC * 1024], BF16)
        wob = [sb(p3, f"wob{i}", [128, NHC * 128], BF16) for i in range(2)]
        rtile = [sb(p3, f"rtile{i}", [128, 512], F32) for i in range(2)]
        x1t = [sb(p3, f"x1t{i}", [128, 512], F32) for i in range(2)]
        x1b = [sb(p3, f"x1b{i}", [128, 512], BF16) for i in range(2)]
        sq3 = [sb(p3, f"sq3{i}", [128, 512], BF16) for i in range(2)]
        rows_mu = sb(p3, "rows_mu3", [1, TOK], F32)
        rows_e2 = sb(p3, "rows_e23", [1, TOK], F32)
        rows_m2 = sb(p3, "rows_m23", [1, TOK], F32)
        rows_rs = sb(p3, "rows_rs3", [1, TOK], F32)
        rows_s = rows_e2
        psOut = [p3.enter_context(nc.psum_tensor(f"psOut{i}", [128, 512], F32)) for i in range(2)]
        psSum = p3.enter_context(nc.psum_tensor("psSum3", [1, 1024], F32))
        psSq = p3.enter_context(nc.psum_tensor("psSq3", [1, 1024], F32))

        b_ah = Buf("ah")
        b_wob = [Buf(f"wob{i}") for i in range(2)]
        b_rtile = [Buf(f"rt{i}") for i in range(2)]
        b_x1t = [Buf(f"x1t{i}") for i in range(2)]
        b_x1b = [Buf(f"x1b{i}") for i in range(2)]
        b_sq3 = [Buf(f"sq3{i}") for i in range(2)]
        b_rows = Buf("rows3")
        b_psOut = [Buf(f"psOut{i}") for i in range(2)]
        b_psSum = Buf("psSum3")
        b_psSq = Buf("psSq3")

        cnt3 = [0]
        pend3 = []
        for half in range(2):
            for dc in range(NH):
                P.step("sync", lambda E, dc=dc, half=half: E.dma_start(
                    out=ah[:, dc * 1024:(dc + 1) * 1024],
                    in_=As[dc][:, half * 1024:(half + 1) * 1024]),
                    reads=[b_As], writes=[b_ah] if dc == NH - 1 else [],
                    extra_waits=tuple(b_ah.free.items()) if dc == 0 else ())
            for hoc in range(NHC):
                wb = hoc % 2
                P.step("gpsimd", lambda E, hoc=hoc, wb=wb: E.dma_start(
                    out=wob[wb][:], in_=wo[hoc].rearrange("p h c -> p (h c)")),
                    writes=[b_wob[wb]])
                for sqt in range(2):
                    qt = half * 2 + sqt
                    pp = cnt3[0] % 2
                    cnt3[0] += 1

                    def omm(E, wb=wb, sqt=sqt, pp=pp):
                        last = None
                        for dc in range(NHC):
                            last = E.matmul(
                                psOut[pp][:], wob[wb][:, dc * 128:(dc + 1) * 128],
                                ah[:, dc * 1024 + sqt * 512: dc * 1024 + sqt * 512 + 512],
                                start=(dc == 0), stop=(dc == NHC - 1))
                        return last
                    P.step("tensor", omm, reads=[b_wob[wb], b_ah], writes=[b_psOut[pp]])
                    for f in pend3:
                        f()
                    pend3.clear()

                    def tail3(hoc=hoc, qt=qt, pp=pp, sqt=sqt):
                        P.step("sync", lambda E: E.dma_start(
                            out=rtile[pp][:],
                            in_=xT[hoc * 128:(hoc + 1) * 128, qt * 512:qt * 512 + 512]),
                            writes=[b_rtile[pp]])
                        P.step("vector", lambda E: E.tensor_add(
                            x1t[pp][:], psOut[pp][:], rtile[pp][:]),
                            reads=[b_psOut[pp], b_rtile[pp]], writes=[b_x1t[pp]])
                        P.step("sync", lambda E: E.dma_start(
                            out=X1[hoc][:, qt * 512:qt * 512 + 512], in_=x1t[pp][:]),
                            reads=[b_x1t[pp]], writes=[b_X1])
                        P.step("vector", lambda E: E.tensor_copy(
                            x1b[pp][:], x1t[pp][:]),
                            reads=[b_x1t[pp]], writes=[b_x1b[pp]])
                        P.step("scalar", lambda E: E.activation(
                            sq3[pp][:], x1t[pp][:], AF.Square),
                            reads=[b_x1t[pp]], writes=[b_sq3[pp]])
                        P.step("tensor", lambda E: E.matmul(
                            psSum[0:1, sqt * 512:sqt * 512 + 512], ones_c_b[:], x1b[pp][:],
                            start=(hoc == 0), stop=(hoc == NHC - 1)),
                            reads=[b_x1b[pp], b_ones], writes=[b_psSum])
                        P.step("tensor", lambda E: E.matmul(
                            psSq[0:1, sqt * 512:sqt * 512 + 512], ones_c_b[:], sq3[pp][:],
                            start=(hoc == 0), stop=(hoc == NHC - 1)),
                            reads=[b_sq3[pp], b_ones], writes=[b_psSq])
                    pend3.append(tail3)
            for f in pend3:
                f()
            pend3.clear()
            P.step("vector", lambda E, half=half: E.tensor_scalar_mul(
                rows_mu[0:1, half * 1024:(half + 1) * 1024], psSum[:], 1.0 / H),
                reads=[b_psSum], writes=[b_rows])
            P.step("vector", lambda E, half=half: E.tensor_scalar_mul(
                rows_e2[0:1, half * 1024:(half + 1) * 1024], psSq[:], 1.0 / H),
                reads=[b_psSq], writes=[b_rows])
        P.step("scalar", lambda E: E.activation(rows_m2[:], rows_mu[:], AF.Square),
               reads=[b_rows], writes=[b_rows])
        P.step("vector", lambda E: E.tensor_sub(rows_rs[:], rows_e2[:], rows_m2[:]),
               reads=[b_rows], writes=[b_rows])
        P.step("scalar", lambda E: E.activation(rows_m2[:], rows_rs[:], AF.Ln, bias=epsrow[:]),
               reads=[b_rows], writes=[b_rows])
        P.step("scalar", lambda E: E.activation(rows_rs[:], rows_m2[:], AF.Exp, scale=-0.5),
               reads=[b_rows], writes=[b_rows])
        P.step("vector", lambda E: E.tensor_mul(rows_s[:], rows_mu[:], rows_rs[:]),
               reads=[b_rows], writes=[b_rows])
        bcR = sb(p3, "bcR3", [128, TOK], F32)
        bcS = sb(p3, "bcS3", [128, TOK], F32)
        b_bc = Buf("bc3")
        psBc = [p3.enter_context(nc.psum_tensor(f"psBc3{i}", [128, 512], F32)) for i in range(2)]
        b_psBc = [Buf(f"psBc3{i}") for i in range(2)]
        for qt in range(NQT):
            P.step("tensor", lambda E, qt=qt: E.matmul(
                psBc[0][:], ones_r_f[:], rows_rs[0:1, qt * 512:qt * 512 + 512],
                start=True, stop=True),
                reads=[b_rows, b_ones], writes=[b_psBc[0]])
            P.step("vector", lambda E, qt=qt: E.tensor_copy(
                bcR[:, qt * 512:qt * 512 + 512], psBc[0][:]),
                reads=[b_psBc[0]], writes=[b_bc] if qt == NQT - 1 else [])
            P.step("tensor", lambda E, qt=qt: E.matmul(
                psBc[1][:], ones_r_f[:], rows_s[0:1, qt * 512:qt * 512 + 512],
                start=True, stop=True),
                reads=[b_rows, b_ones], writes=[b_psBc[1]])
            P.step("vector", lambda E, qt=qt: E.tensor_copy(
                bcS[:, qt * 512:qt * 512 + 512], psBc[1][:]),
                reads=[b_psBc[1]], writes=[b_bc] if qt == NQT - 1 else [])
        if phases >= 4:
            P.step("gpsimd", lambda E: E.dma_start(out=glAb[:], in_=glA),
                   writes=[b_glA])
            P.step("gpsimd", lambda E: E.dma_start(out=glBTb[:], in_=glBT),
                   writes=[b_glBT])
            P.step("sync", lambda E: E.dma_start(
                out=gwTb[:], in_=gwT.rearrange("p h e -> p (h e)")), writes=[b_gwT])
            for hc in range(NHC):
                P.step("tensor", lambda E, hc=hc: E.matmul(
                    psBc[0][:, 0:NE], glAb[:, hc * 128:(hc + 1) * 128], glBTb[:],
                    start=True, stop=True),
                    reads=[b_glA, b_glBT], writes=[b_psBc[0]])
                P.step("vector", lambda E, hc=hc: E.scalar_tensor_tensor(
                    geff[:, hc * NE:(hc + 1) * NE], psBc[0][:, 0:NE], 2.0,
                    gwTb[:, hc * NE:(hc + 1) * NE], ALU.mult, ALU.add),
                    reads=[b_psBc[0], b_gwT], writes=[b_geff] if hc == NHC - 1 else [])
        x1L = [sb(p3, f"x1L{i}", [128, TOK], F32) for i in range(2)]
        x1M = [sb(p3, f"x1M{i}", [128, TOK], F32) for i in range(2)]
        b_x1L = [Buf(f"x1L{i}") for i in range(2)]
        b_x1M = [Buf(f"x1M{i}") for i in range(2)]
        for hoc in range(NHC):
            pp = hoc % 2
            mule = "gpsimd" if hoc % 3 == 2 else "vector"
            P.step("sync", lambda E, hoc=hoc, pp=pp: E.dma_start(
                out=x1L[pp][:], in_=X1[hoc]),
                reads=[b_X1], writes=[b_x1L[pp]])
            P.step(mule, lambda E, pp=pp: E.tensor_mul(
                x1M[pp][:], x1L[pp][:], bcR[:]),
                reads=[b_x1L[pp], b_bc], writes=[b_x1M[pp]])
            P.step(mule, lambda E, pp=pp: E.tensor_sub(
                x1L[pp][:], x1M[pp][:], bcS[:]),
                reads=[b_x1M[pp]], writes=[b_x1L[pp]])
            P.step("vector", lambda E, pp=pp, hoc=hoc: E.tensor_copy(
                x1n_sb[:, hoc * TOK:(hoc + 1) * TOK], x1L[pp][:]),
                reads=[b_x1L[pp]],
                writes=[b_x1n_sb] if hoc == NHC - 1 else [],
                extra_waits=tuple(b_x1n_sb.free.items()) if hoc == 0 else ())
            P.step("sync", lambda E, pp=pp, hoc=hoc: E.dma_start(
                out=X1n[hoc], in_=x1L[pp][:]),
                reads=[b_x1L[pp]], writes=[b_X1n])
        P.phase_barrier(ALLSEMS)
        p3.close()

    if phases >= 4:
        p3t = ExitStack()
        xt_in = [sb(p3t, f"xt_in{i}", [128, NHC * 128], F32) for i in range(2)]
        xrow = [sb(p3t, f"xrow{i}", [128, H], F32) for i in range(2)]
        psXT = [p3t.enter_context(nc.psum_tensor(f"psXT{i}", [128, 512], F32))
                for i in range(2)]
        b_xt_in = [Buf(f"xt_in{i}") for i in range(2)]
        b_xrow = [Buf(f"xrow{i}") for i in range(2)]
        b_psXT = [Buf(f"psXT{i}") for i in range(2)]

        P.step("vector", lambda E: E.memset(xrow[0][:], 0.0), writes=[b_xrow[0]])
        P.step("sync", lambda E: E.dma_start(
            out=X1tok[TOK:TOK + 128, :], in_=xrow[0][:]),
            reads=[b_xrow[0]], writes=[b_X1tok])
        for tc in range(16):
            tb = tc % 2
            for hc in range(NHC):
                P.step("sync", lambda E, tc=tc, hc=hc, tb=tb: E.dma_start(
                    out=xt_in[tb][:, hc * 128:(hc + 1) * 128],
                    in_=X1n[hc][:, tc * 128:tc * 128 + 128]),
                    reads=[b_X1n],
                    writes=[b_xt_in[tb]] if hc == NHC - 1 else [],
                    extra_waits=tuple(b_xt_in[tb].free.items()) if hc == 0 else ())
            for g4 in range(4):
                pp = g4 % 2

                def xtr(E, tb=tb, g4=g4, pp=pp):
                    last = None
                    for k in range(4):
                        hc = g4 * 4 + k
                        last = E.matmul(
                            psXT[pp][:, k * 128:(k + 1) * 128],
                            xt_in[tb][:, hc * 128:(hc + 1) * 128],
                            id_f[:], start=(k == 0), stop=(k == 3),
                            is_transpose=True)
                    return last
                P.step("tensor", xtr, reads=[b_xt_in[tb], b_id], writes=[b_psXT[pp]])
                P.step("vector", lambda E, tb=tb, g4=g4, pp=pp: E.tensor_copy(
                    xrow[tb][:, g4 * 512:(g4 + 1) * 512], psXT[pp][:]),
                    reads=[b_psXT[pp]],
                    writes=[b_xrow[tb]] if g4 == 3 else [],
                    extra_waits=tuple(b_xrow[tb].free.items()) if g4 == 0 else ())
            P.step("sync", lambda E, tc=tc, tb=tb: E.dma_start(
                out=X1tok[tc * 128:(tc + 1) * 128, :], in_=xrow[tb][:]),
                reads=[b_xrow[tb]], writes=[b_X1tok])

        p4 = ExitStack()
        LTa = sb(p4, "LTa", [128, 128], F32)
        suba = sb(p4, "suba", [128, 128], F32)
        Ea = sb(p4, "Ea", [128, 128], F32)
        m1a = sb(p4, "m1a", [128, 128], F32)
        m2a = sb(p4, "m2a", [128, 128], F32)
        msel = sb(p4, "msel", [128, 128], F32)
        Lma = sb(p4, "Lma", [128, 128], F32)
        nm1a = sb(p4, "nm1a", [128, 16], F32)
        nm2a = sb(p4, "nm2a", [128, 16], F32)
        dena = sb(p4, "dena", [128, 16], F32)
        dinva = sb(p4, "dinva", [128, 16], F32)
        w_all = sb(p4, "w_all", [128, 16 * NE], F32)
        csum = sb(p4, "csum", [128, 128], F32)
        totc = sb(p4, "totc", [128, 1], F32)
        basc = sb(p4, "basc", [128, 1], F32)
        basr = sb(p4, "basr", [1, 128], F32)
        posm = sb(p4, "posm", [128, 128], F32)
        ovf = sb(p4, "ovf", [128, 128], F32)
        slotm = sb(p4, "slotm", [128, 128], F32)
        wrk4 = sb(p4, "wrk4", [128, 128], F32)
        sidxf = [sb(p4, f"sidxf{r}", [128, 16], F32) for r in range(2)]
        ltri_s = sb(p4, "ltri_s", [128, 128], F32)
        mpre_s = sb(p4, "mpre_s", [128, 128], F32)
        ecm_s = sb(p4, "ecm_s", [128, 128], F32)
        tcc_s = sb(p4, "tcc_s", [128, 32], F32)
        tccu = sb(p4, "tccu", [128, 32], U32)
        padf = sb(p4, "padf", [128, 128], F32)
        padu = sb(p4, "padu", [128, 128], U32)

        b_r = Buf("router")
        b_w_all = Buf("w_all")
        b_idxm = Buf("idxm")
        b_c4 = Buf("c4")
        b_tccu = Buf("tccu")
        b_pad = Buf("pad")

        P.step("sync", lambda E: E.dma_start(out=ltri_s[:], in_=ltri), writes=[b_c4])
        P.step("sync", lambda E: E.dma_start(out=mpre_s[:], in_=mpre), writes=[b_c4])
        P.step("sync", lambda E: E.dma_start(out=ecm_s[:], in_=ecmat), writes=[b_c4])
        P.step("sync", lambda E: E.dma_start(out=tcc_s[:], in_=tcc2), writes=[b_c4])
        P.step("vector", lambda E: E.tensor_copy(tccu[:], tcc_s[:]),
               reads=[b_c4], writes=[b_tccu])
        P.step("vector", lambda E: E.memset(padf[:], float(TOK)), writes=[b_pad])
        P.step("vector", lambda E: E.tensor_copy(padu[:], padf[:]),
               reads=[b_pad], writes=[b_pad])
        P.step("sync", lambda E: E.dma_start(
            out=idxbuf.rearrange("(p x) c -> p (x c)", p=128), in_=padu[:]),
            reads=[b_pad], writes=[b_idxbuf])

        pr = ExitStack()
        psLt = [pr.enter_context(nc.psum_tensor(f"psLt{i}", [128, NE], F32))
                for i in range(2)]
        psIC = pr.enter_context(nc.psum_tensor("psIC", [128, 128], F32))
        psC1 = pr.enter_context(nc.psum_tensor("psC1", [128, 1], F32))
        psR1 = pr.enter_context(nc.psum_tensor("psR1", [1, 128], F32))
        psBB = pr.enter_context(nc.psum_tensor("psBB", [128, 128], F32))
        b_psLt = [Buf(f"psLt{i}") for i in range(2)]
        b_psIC = Buf("psIC")
        b_psC1 = Buf("psC1")
        b_psR1 = Buf("psR1")
        b_psBB = Buf("psBB")
        for tc in range(16):
            lp = tc % 2

            def rmm(E, tc=tc, lp=lp):
                last = None
                for hc in range(NHC):
                    last = E.matmul(
                        psLt[lp][:],
                        x1n_sb[:, hc * TOK + tc * 128: hc * TOK + tc * 128 + 128],
                        geff[:, hc * NE:(hc + 1) * NE],
                        start=(hc == 0), stop=(hc == NHC - 1))
                return last
            P.step("tensor", rmm, reads=[b_x1n_sb, b_geff], writes=[b_psLt[lp]])
            P.step("vector", lambda E, tc=tc, lp=lp: E.tensor_copy(
                LTa[:, tc * NE:(tc + 1) * NE], psLt[lp][:]),
                reads=[b_psLt[lp]], writes=[b_r] if tc == 15 else [])
        seg = lambda t: t.rearrange("p (a b) -> p a b", b=NE)
        bc3 = lambda t: t.unsqueeze(2).broadcast_to([128, 16, NE])
        P.step("vector", lambda E: E.tensor_reduce(
            nm1a[:], seg(LTa[:, :]), AX.X, ALU.max, negate=True),
            reads=[b_r], writes=[b_r])
        P.step("vector", lambda E: E.tensor_tensor(
            seg(suba[:, :]), seg(LTa[:, :]), bc3(nm1a[:, :]), ALU.add),
            reads=[b_r], writes=[b_r])
        P.step("scalar", lambda E: E.activation(Ea[:], suba[:], AF.Exp),
               reads=[b_r], writes=[b_r])
        P.step("vector", lambda E: E.tensor_single_scalar(
            m1a[:], suba[:], 0.0, ALU.is_ge),
            reads=[b_r], writes=[b_r])
        P.step("vector", lambda E: E.scalar_tensor_tensor(
            Lma[:], m1a[:], -BIG, suba[:], ALU.mult, ALU.add),
            reads=[b_r], writes=[b_r])
        P.step("vector", lambda E: E.tensor_reduce(
            nm2a[:], seg(Lma[:, :]), AX.X, ALU.max, negate=True),
            reads=[b_r], writes=[b_r])
        P.step("vector", lambda E: E.tensor_tensor(
            seg(Lma[:, :]), seg(Lma[:, :]), bc3(nm2a[:, :]), ALU.add),
            reads=[b_r], writes=[b_r])
        P.step("vector", lambda E: E.tensor_single_scalar(
            m2a[:], Lma[:], 0.0, ALU.is_ge),
            reads=[b_r], writes=[b_r])
        P.step("vector", lambda E: E.tensor_add(msel[:], m1a[:], m2a[:]),
               reads=[b_r], writes=[b_r])
        P.step("vector", lambda E: E.tensor_mul(Ea[:], Ea[:], msel[:]),
               reads=[b_r], writes=[b_r])
        P.step("vector", lambda E: E.tensor_reduce(
            dena[:], seg(Ea[:, :]), AX.X, ALU.add),
            reads=[b_r], writes=[b_r])
        P.step("vector", lambda E: E.reciprocal(dinva[:], dena[:]),
               reads=[b_r], writes=[b_r])
        P.step("vector", lambda E: E.tensor_tensor(
            seg(w_all[:, :]), seg(Ea[:, :]), bc3(dinva[:, :]), ALU.mult),
            reads=[b_r], writes=[b_w_all])

        P.step("tensor", lambda E: E.matmul(
            psIC[:], ltri_s[:], msel[:], start=True, stop=True),
            reads=[b_c4, b_r], writes=[b_psIC])
        P.step("vector", lambda E: E.tensor_copy(csum[:], psIC[:]),
               reads=[b_psIC], writes=[b_idxm])
        P.step("tensor", lambda E: E.matmul(
            psC1[:], msel[:], ones_c_f[:], start=True, stop=True),
            reads=[b_r, b_ones], writes=[b_psC1])
        P.step("vector", lambda E: E.tensor_copy(totc[:], psC1[:]),
               reads=[b_psC1], writes=[b_idxm])
        P.step("tensor", lambda E: E.matmul(
            psC1[:], mpre_s[:], totc[:], start=True, stop=True),
            reads=[b_c4, b_idxm], writes=[b_psC1])
        P.step("vector", lambda E: E.tensor_copy(basc[:], psC1[:]),
               reads=[b_psC1], writes=[b_idxm])
        P.step("tensor", lambda E: E.matmul(
            psR1[:], basc[:], id_f[:], start=True, stop=True, is_transpose=True),
            reads=[b_idxm, b_id], writes=[b_psR1])
        P.step("vector", lambda E: E.tensor_copy(basr[:], psR1[0:1, :]),
               reads=[b_psR1], writes=[b_idxm])
        P.step("tensor", lambda E: E.matmul(
            psBB[:], ones_r_f[:], basr[:], start=True, stop=True),
            reads=[b_idxm, b_ones], writes=[b_psBB])
        P.step("vector", lambda E: E.scalar_tensor_tensor(
            posm[:], csum[:], -1.0, psBB[:], ALU.add, ALU.add),
            reads=[b_idxm, b_psBB], writes=[b_idxm])
        P.step("vector", lambda E: E.tensor_single_scalar(
            ovf[:], posm[:], float(CAP), ALU.is_ge),
            reads=[b_idxm], writes=[b_idxm])
        P.step("vector", lambda E: E.tensor_tensor(
            slotm[:], posm[:], ecm_s[:], ALU.add),
            reads=[b_idxm, b_c4], writes=[b_idxm])
        P.step("vector", lambda E: E.scalar_tensor_tensor(
            slotm[:], ovf[:], BIG, slotm[:], ALU.mult, ALU.add),
            reads=[b_idxm], writes=[b_idxm])
        for r in range(2):
            mr = m1a if r == 0 else m2a
            P.step("vector", lambda E, mr=mr: E.tensor_tensor(
                wrk4[:], mr[:], slotm[:], ALU.mult),
                reads=[b_idxm, b_r], writes=[b_idxm])
            P.step("vector", lambda E, r=r: E.tensor_reduce(
                sidxf[r][:], seg(wrk4[:, :]), AX.X, ALU.add),
                reads=[b_idxm], writes=[b_idxm])
            P.step("vector", lambda E, r=r: E.tensor_copy(
                sidxu[r][:], sidxf[r][:]),
                reads=[b_idxm], writes=[b_sidxu])
            P.step("vector", lambda E, mr=mr: E.tensor_tensor(
                wrk4[:], mr[:], w_all[:], ALU.mult),
                reads=[b_idxm, b_r, b_w_all], writes=[b_idxm])
            P.step("vector", lambda E, r=r: E.tensor_reduce(
                wsel[r][:], seg(wrk4[:, :]), AX.X, ALU.add),
                reads=[b_idxm], writes=[b_wsel])
        for tc in range(16):
            for r in range(2):
                P.step("gpsimd", lambda E, tc=tc, r=r: E.indirect_dma_start(
                    out=idxbuf,
                    out_offset=IOA(ap=sidxu[r][:, tc:tc + 1], axis=0),
                    in_=tccu[:, 2 * tc:2 * tc + 2], in_offset=None,
                    bounds_check=REGS["slots"], oob_is_err=False),
                    reads=[b_idxbuf, b_sidxu, b_tccu], writes=[b_idxbuf])
        P.phase_barrier(ALLSEMS)
        pr.close()
        p4.close()
        p3t.close()
        right_x.close()

    if phases >= 5:
        NQC = CAP // 512
        p5 = ExitStack()
        Xf = sb(p5, "Xf", [128, NHC * CAP], BF16)
        Hf = sb(p5, "Hf", [128, NIC * CAP], BF16)
        Xg = [sb(p5, f"Xg{i}", [128, H], BF16) for i in range(2)]
        eidx = [sb(p5, f"eidx{i}", [128, 16], U32) for i in range(2)]
        gAb = sb(p5, "gAb", [128, NHC * R], BF16)
        gBTb = sb(p5, "gBTb", [R, I2], BF16)
        dAb = sb(p5, "dAb", [128, NIC * R], BF16)
        dBTb = sb(p5, "dBTb", [R, H], BF16)
        t2sb = sb(p5, "t2sb", [R, CAP], BF16)
        t3sb = sb(p5, "t3sb", [R, CAP], BF16)
        wg5 = [sb(p5, f"wg5{i}", [128, NHC * 128], BF16) for i in range(2)]
        wu5 = [sb(p5, f"wu5{i}", [128, NHC * 128], BF16) for i in range(2)]
        wd5 = [sb(p5, f"wd5{i}", [128, NIC * 512], BF16) for i in range(2)]
        sg5 = [sb(p5, f"sg5{i}", [128, 512], BF16) for i in range(2)]
        dt5 = [sb(p5, f"dt5{i}", [128, 512], F32) for i in range(2)]

        b_Xf = Buf("Xf")
        b_Hf = Buf("Hf")
        b_Xg = [Buf(f"Xg{i}") for i in range(2)]
        b_eidx = [Buf(f"eidx{i}") for i in range(2)]
        b_gA = Buf("gA")
        b_gBT = Buf("gBT")
        b_dA = Buf("dA")
        b_dBT = Buf("dBT")
        b_t2 = Buf("t2")
        b_t3 = Buf("t3")
        b_wg5 = [Buf(f"wg5{i}") for i in range(2)]
        b_wu5 = [Buf(f"wu5{i}") for i in range(2)]
        b_wd5 = [Buf(f"wd5{i}") for i in range(2)]
        b_sg5 = [Buf(f"sg5{i}") for i in range(2)]
        b_dt5 = [Buf(f"dt5{i}") for i in range(2)]

        psTR = p5.enter_context(nc.psum_tensor("psTR", [128, 512], BF16))
        psT2 = p5.enter_context(nc.psum_tensor("psT2", [R, 512], F32))
        psG = [p5.enter_context(nc.psum_tensor(f"psG{i}", [128, 512], F32))
               for i in range(2)]
        psU = [p5.enter_context(nc.psum_tensor(f"psU{i}", [128, 512], F32))
               for i in range(2)]
        psD = [p5.enter_context(nc.psum_tensor(f"psD{i}", [128, 512], F32))
               for i in range(2)]
        b_psTR = Buf("psTR")
        b_psT2 = Buf("psT2")
        b_psG = [Buf(f"psG{i}") for i in range(2)]
        b_psU = [Buf(f"psU{i}") for i in range(2)]
        b_psD = [Buf(f"psD{i}") for i in range(2)]

        for e in range(NE):
            eb = e % 2
            for sc in range(NIC):
                P.step("sync", lambda E, e=e, sc=sc, eb=eb: E.dma_start(
                    out=eidx[eb][:, 2 * sc:2 * sc + 2],
                    in_=idxbuf[e * CAP + sc * 128: e * CAP + sc * 128 + 128, :]),
                    reads=[b_idxbuf],
                    writes=[b_eidx[eb]] if sc == NIC - 1 else [],
                    extra_waits=tuple(b_eidx[eb].free.items()) if sc == 0 else ())
            P.step("gpsimd", lambda E, e=e: E.dma_start(
                out=gAb[:], in_=gupA[e].rearrange("p h r -> p (h r)")),
                writes=[b_gA], extra_waits=tuple(b_gA.free.items()))
            P.step("gpsimd", lambda E, e=e: E.dma_start(out=gBTb[:], in_=gupBT[e]),
                   writes=[b_gBT], extra_waits=tuple(b_gBT.free.items()))
            P.step("gpsimd", lambda E, e=e: E.dma_start(
                out=dAb[:], in_=downA[e].rearrange("p i r -> p (i r)")),
                writes=[b_dA], extra_waits=tuple(b_dA.free.items()))
            P.step("gpsimd", lambda E, e=e: E.dma_start(out=dBTb[:], in_=downBT[e]),
                   writes=[b_dBT], extra_waits=tuple(b_dBT.free.items()))
            for sc in range(NIC):
                gb = sc % 2
                P.step("gpsimd", lambda E, sc=sc, gb=gb, eb=eb: E.indirect_dma_start(
                    out=Xg[gb][:], out_offset=None,
                    in_=X1tok,
                    in_offset=IOA(ap=eidx[eb][:, 2 * sc:2 * sc + 1], axis=0),
                    bounds_check=REGS["tok"], oob_is_err=False),
                    reads=[b_X1tok, b_eidx[eb]], writes=[b_Xg[gb]])
                for g4 in range(4):
                    def xtr5(E, gb=gb, g4=g4):
                        last = None
                        for k in range(4):
                            hc = g4 * 4 + k
                            last = E.matmul(
                                psTR[:, k * 128:(k + 1) * 128],
                                Xg[gb][:, hc * 128:(hc + 1) * 128],
                                id_b[:], start=(k == 0), stop=(k == 3),
                                is_transpose=True)
                        return last
                    P.step("tensor", xtr5, reads=[b_Xg[gb], b_id], writes=[b_psTR])

                    def xev5(E, sc=sc, g4=g4):
                        last = None
                        for k in range(4):
                            hc = g4 * 4 + k
                            last = E.tensor_copy(
                                Xf[:, hc * CAP + sc * 128: hc * CAP + sc * 128 + 128],
                                psTR[:, k * 128:(k + 1) * 128])
                        return last
                    P.step("vector", xev5, reads=[b_psTR],
                           writes=[b_Xf] if (sc == NIC - 1 and g4 == 3) else [],
                           extra_waits=tuple(b_Xf.free.items())
                           if (sc == 0 and g4 == 0) else ())
            for qt in range(NQC):
                def t2mm(E, qt=qt):
                    last = None
                    for hc in range(NHC):
                        last = E.matmul(
                            psT2[:], gAb[:, hc * R:(hc + 1) * R],
                            Xf[:, hc * CAP + qt * 512: hc * CAP + qt * 512 + 512],
                            start=(hc == 0), stop=(hc == NHC - 1))
                    return last
                P.step("tensor", t2mm, reads=[b_gA, b_Xf], writes=[b_psT2])
                P.step("vector", lambda E, qt=qt: E.tensor_scalar_mul(
                    t2sb[:, qt * 512:qt * 512 + 512], psT2[:], 2.0),
                    reads=[b_psT2], writes=[b_t2] if qt == NQC - 1 else [],
                    extra_waits=tuple(b_t2.free.items()) if qt == 0 else ())
            for j in range(NIC):
                jb = j % 2
                P.step("gpsimd", lambda E, e=e, j=j, jb=jb: E.dma_start(
                    out=wg5[jb][:], in_=gupw[e, j].rearrange("p h c -> p (h c)")),
                    writes=[b_wg5[jb]])
                P.step("gpsimd", lambda E, e=e, j=j, jb=jb: E.dma_start(
                    out=wu5[jb][:], in_=gupw[e, j + 8].rearrange("p h c -> p (h c)")),
                    writes=[b_wu5[jb]])
                for qt in range(NQC):
                    gp = qt % 2

                    def gmm(E, jb=jb, qt=qt, gp=gp, j=j):
                        for hc in range(NHC):
                            E.matmul(psG[gp][:], wg5[jb][:, hc * 128:(hc + 1) * 128],
                                     Xf[:, hc * CAP + qt * 512: hc * CAP + qt * 512 + 512],
                                     start=(hc == 0), stop=False)
                        return E.matmul(psG[gp][:], gBTb[:, j * 128:(j + 1) * 128],
                                        t2sb[:, qt * 512:qt * 512 + 512],
                                        start=False, stop=True)
                    P.step("tensor", gmm, reads=[b_wg5[jb], b_Xf, b_gBT, b_t2],
                           writes=[b_psG[gp]])

                    def umm(E, jb=jb, qt=qt, gp=gp, j=j):
                        for hc in range(NHC):
                            E.matmul(psU[gp][:], wu5[jb][:, hc * 128:(hc + 1) * 128],
                                     Xf[:, hc * CAP + qt * 512: hc * CAP + qt * 512 + 512],
                                     start=(hc == 0), stop=False)
                        return E.matmul(psU[gp][:], gBTb[:, (j + 8) * 128:(j + 9) * 128],
                                        t2sb[:, qt * 512:qt * 512 + 512],
                                        start=False, stop=True)
                    P.step("tensor", umm, reads=[b_wu5[jb], b_Xf, b_gBT, b_t2],
                           writes=[b_psU[gp]])
                    P.step("scalar", lambda E, gp=gp: E.activation(
                        sg5[gp][:], psG[gp][:], AF.Silu),
                        reads=[b_psG[gp]], writes=[b_sg5[gp]])
                    P.step("vector", lambda E, gp=gp, j=j, qt=qt: E.tensor_mul(
                        Hf[:, j * CAP + qt * 512: j * CAP + qt * 512 + 512],
                        sg5[gp][:], psU[gp][:]),
                        reads=[b_sg5[gp], b_psU[gp]],
                        writes=[b_Hf] if (j == NIC - 1 and qt == NQC - 1) else [],
                        extra_waits=tuple(b_Hf.free.items())
                        if (j == 0 and qt == 0) else ())
            for qt in range(NQC):
                def t3mm(E, qt=qt):
                    last = None
                    for ic in range(NIC):
                        last = E.matmul(
                            psT2[:], dAb[:, ic * R:(ic + 1) * R],
                            Hf[:, ic * CAP + qt * 512: ic * CAP + qt * 512 + 512],
                            start=(ic == 0), stop=(ic == NIC - 1))
                    return last
                P.step("tensor", t3mm, reads=[b_dA, b_Hf], writes=[b_psT2])
                P.step("vector", lambda E, qt=qt: E.tensor_scalar_mul(
                    t3sb[:, qt * 512:qt * 512 + 512], psT2[:], 2.0),
                    reads=[b_psT2], writes=[b_t3] if qt == NQC - 1 else [],
                    extra_waits=tuple(b_t3.free.items()) if qt == 0 else ())
            for fg in range(4):
                wb = fg % 2
                for ic in range(NIC):
                    P.step("gpsimd", lambda E, e=e, fg=fg, ic=ic, wb=wb: E.dma_start(
                        out=wd5[wb][:, ic * 512:(ic + 1) * 512],
                        in_=downw[e, ic][:, fg * 512:fg * 512 + 512]),
                        writes=[b_wd5[wb]] if ic == NIC - 1 else [],
                        extra_waits=tuple(b_wd5[wb].free.items()) if ic == 0 else ())
                for sc in range(NIC):
                    dp = sc % 2

                    def dmm(E, wb=wb, sc=sc, dp=dp, fg=fg):
                        for ic in range(NIC):
                            E.matmul(psD[dp][:],
                                     Hf[:, ic * CAP + sc * 128: ic * CAP + sc * 128 + 128],
                                     wd5[wb][:, ic * 512:(ic + 1) * 512],
                                     start=(ic == 0), stop=False)
                        return E.matmul(psD[dp][:], t3sb[:, sc * 128:sc * 128 + 128],
                                        dBTb[:, fg * 512:fg * 512 + 512],
                                        start=False, stop=True)
                    P.step("tensor", dmm, reads=[b_Hf, b_wd5[wb], b_t3, b_dBT],
                           writes=[b_psD[dp]])
                    if sc % 2 == 0:
                        P.step("vector", lambda E, dp=dp: E.tensor_copy(
                            dt5[dp][:], psD[dp][:]),
                            reads=[b_psD[dp]], writes=[b_dt5[dp]])
                    else:
                        P.step("scalar", lambda E, dp=dp: E.copy(
                            dt5[dp][:], psD[dp][:]),
                            reads=[b_psD[dp]], writes=[b_dt5[dp]])
                    P.step("sync", lambda E, e=e, sc=sc, fg=fg, dp=dp: E.dma_start(
                        out=Dall[e * CAP + sc * 128: e * CAP + sc * 128 + 128,
                                 fg * 512:fg * 512 + 512],
                        in_=dt5[dp][:]),
                        reads=[b_dt5[dp]], writes=[b_Dall])
        P.phase_barrier(ALLSEMS)
        p5.close()

        p6 = ExitStack()
        d1t = [sb(p6, f"d1t{i}", [128, H], F32) for i in range(2)]
        d2t = [sb(p6, f"d2t{i}", [128, H], F32) for i in range(2)]
        x1r = [sb(p6, f"x1r{i}", [128, H], F32) for i in range(2)]
        y2t = [sb(p6, f"y2t{i}", [128, H], F32) for i in range(2)]
        sqt = [sb(p6, f"sqt{i}", [128, H], F32) for i in range(2)]
        ot6 = [sb(p6, f"ot6{i}", [128, H], F32) for i in range(2)]
        cst = [sb(p6, f"cst{i}", [128, 8], F32) for i in range(2)]
        b_d1 = [Buf(f"d1{i}") for i in range(2)]
        b_d2 = [Buf(f"d2{i}") for i in range(2)]
        b_x1r = [Buf(f"x1r{i}") for i in range(2)]
        b_y2 = [Buf(f"y2{i}") for i in range(2)]
        b_sq6 = [Buf(f"sq6{i}") for i in range(2)]
        b_ot = [Buf(f"ot{i}") for i in range(2)]
        b_cst = [Buf(f"cst{i}") for i in range(2)]
        for tc in range(16):
            tb = tc % 2
            P.step("gpsimd", lambda E, tc=tc, tb=tb: E.indirect_dma_start(
                out=d1t[tb][:], out_offset=None,
                in_=Dall, in_offset=IOA(ap=sidxu[0][:, tc:tc + 1], axis=0),
                bounds_check=REGS["slots"], oob_is_err=False),
                reads=[b_Dall, b_sidxu], writes=[b_d1[tb]])
            P.step("gpsimd", lambda E, tc=tc, tb=tb: E.indirect_dma_start(
                out=d2t[tb][:], out_offset=None,
                in_=Dall, in_offset=IOA(ap=sidxu[1][:, tc:tc + 1], axis=0),
                bounds_check=REGS["slots"], oob_is_err=False),
                reads=[b_Dall, b_sidxu], writes=[b_d2[tb]])
            P.step("sync", lambda E, tc=tc, tb=tb: E.dma_start(
                out=x1r[tb][:], in_=X1tok[tc * 128:(tc + 1) * 128, :]),
                reads=[b_X1tok], writes=[b_x1r[tb]])
            P.step("scalar", lambda E, tc=tc, tb=tb: E.activation(
                y2t[tb][:], d1t[tb][:], AF.Copy, scale=wsel[0][:, tc:tc + 1]),
                reads=[b_d1[tb], b_wsel], writes=[b_y2[tb]])
            P.step("scalar", lambda E, tc=tc, tb=tb: E.activation(
                sqt[tb][:], d2t[tb][:], AF.Copy, scale=wsel[1][:, tc:tc + 1]),
                reads=[b_d2[tb], b_wsel], writes=[b_sq6[tb]])
            P.step("vector", lambda E, tb=tb: E.tensor_add(
                y2t[tb][:], y2t[tb][:], sqt[tb][:]),
                reads=[b_y2[tb], b_sq6[tb]], writes=[b_y2[tb]])
            P.step("vector", lambda E, tb=tb: E.tensor_add(
                y2t[tb][:], y2t[tb][:], x1r[tb][:]),
                reads=[b_y2[tb], b_x1r[tb]], writes=[b_y2[tb]])
            P.step("vector", lambda E, tb=tb: E.tensor_reduce(
                cst[tb][:, 0:1], y2t[tb][:], AX.X, ALU.add),
                reads=[b_y2[tb]], writes=[b_cst[tb]])
            P.step("scalar", lambda E, tb=tb: E.activation(
                sqt[tb][:], y2t[tb][:], AF.Square),
                reads=[b_y2[tb]], writes=[b_sq6[tb]])
            P.step("vector", lambda E, tb=tb: E.tensor_reduce(
                cst[tb][:, 1:2], sqt[tb][:], AX.X, ALU.add),
                reads=[b_sq6[tb]], writes=[b_cst[tb]])
            P.step("vector", lambda E, tb=tb: E.tensor_scalar_mul(
                cst[tb][:, 2:3], cst[tb][:, 0:1], 1.0 / H),
                reads=[b_cst[tb]], writes=[b_cst[tb]])
            P.step("vector", lambda E, tb=tb: E.tensor_scalar_mul(
                cst[tb][:, 3:4], cst[tb][:, 1:2], 1.0 / H),
                reads=[b_cst[tb]], writes=[b_cst[tb]])
            P.step("scalar", lambda E, tb=tb: E.activation(
                cst[tb][:, 4:5], cst[tb][:, 2:3], AF.Square),
                reads=[b_cst[tb]], writes=[b_cst[tb]])
            P.step("vector", lambda E, tb=tb: E.tensor_sub(
                cst[tb][:, 5:6], cst[tb][:, 3:4], cst[tb][:, 4:5]),
                reads=[b_cst[tb]], writes=[b_cst[tb]])
            P.step("scalar", lambda E, tb=tb: E.activation(
                cst[tb][:, 6:7], cst[tb][:, 5:6], AF.Ln, bias=epscol[:]),
                reads=[b_cst[tb], b_ones], writes=[b_cst[tb]])
            P.step("scalar", lambda E, tb=tb: E.activation(
                cst[tb][:, 7:8], cst[tb][:, 6:7], AF.Exp, scale=-0.5),
                reads=[b_cst[tb]], writes=[b_cst[tb]])
            P.step("vector", lambda E, tb=tb: E.tensor_mul(
                cst[tb][:, 4:5], cst[tb][:, 2:3], cst[tb][:, 7:8]),
                reads=[b_cst[tb]], writes=[b_cst[tb]])
            P.step("vector", lambda E, tb=tb: E.tensor_scalar_mul(
                cst[tb][:, 5:6], cst[tb][:, 4:5], -1.0),
                reads=[b_cst[tb]], writes=[b_cst[tb]])
            P.step("scalar", lambda E, tb=tb: E.activation(
                ot6[tb][:], y2t[tb][:], AF.Copy, scale=cst[tb][:, 7:8]),
                reads=[b_y2[tb], b_cst[tb]], writes=[b_ot[tb]])
            P.step("scalar", lambda E, tb=tb: E.add(
                x1r[tb][:], ot6[tb][:], cst[tb][:, 5:6]),
                reads=[b_ot[tb], b_cst[tb]], writes=[b_x1r[tb]])
            P.step("sync", lambda E, tc=tc, tb=tb: E.dma_start(
                out=OUT[tc * 128:(tc + 1) * 128, :], in_=x1r[tb][:]),
                reads=[b_x1r[tb]], writes=[])
        p6.close()

    P.raw_wait("sync", ss, ss.n)
    P.raw_wait("sync", sg, sg.n)
    P.raw_wait("sync", st, st.n)
    P.raw_wait("sync", sv, sv.n)
    P.raw_wait("sync", sa, sa.n)

    nops = {e: len(P.prog[e]) for e in ENGINES}

    with nc.Block() as block:
        @block.tensor
        def _(Eng):
            for fn in P.prog["tensor"]:
                fn(Eng)

        @block.vector
        def _(Eng):
            for fn in P.prog["vector"]:
                fn(Eng)

        @block.scalar
        def _(Eng):
            for fn in P.prog["scalar"]:
                fn(Eng)

        @block.gpsimd
        def _(Eng):
            for fn in P.prog["gpsimd"]:
                fn(Eng)

        @block.sync
        def _(Eng):
            for fn in P.prog["sync"]:
                fn(Eng)

    return nc, nops


def prep_inputs(inp):
    f32 = np.float32
    x = np.asarray(inp["L0_attn_hidden_states"], f32)
    Wq = np.asarray(inp["L0_attn_Wq"], f32)
    Wk = np.asarray(inp["L0_attn_Wk"], f32)
    Wv = np.asarray(inp["L0_attn_Wv"], f32)
    Wo = np.asarray(inp["L0_attn_Wo"], f32)
    bq = np.asarray(inp["L0_attn_bq"], f32)
    bk = np.asarray(inp["L0_attn_bk"], f32)
    bv = np.asarray(inp["L0_attn_bv"], f32)
    gw = np.asarray(inp["L0_ffn_gate_weight"], f32)
    glA_ = np.asarray(inp["L0_ffn_gate_lora_A"], f32)
    glB_ = np.asarray(inp["L0_ffn_gate_lora_B"], f32)
    gup = np.asarray(inp["L0_ffn_gate_up_proj"], f32)
    dwn = np.asarray(inp["L0_ffn_down_proj"], f32)
    gA_ = np.asarray(inp["L0_ffn_gup_lora_A"], f32)
    gB_ = np.asarray(inp["L0_ffn_gup_lora_B"], f32)
    dA_ = np.asarray(inp["L0_ffn_down_lora_A"], f32)
    dB_ = np.asarray(inp["L0_ffn_down_lora_B"], f32)

    def blockT(WT):
        return np.ascontiguousarray(
            WT.reshape(NHC, 128, NHC, 128).transpose(2, 1, 0, 3))

    shared = {
        "wq": blockT(Wq.T), "wk": blockT(Wk.T), "wv": blockT(Wv.T),
        "wo": blockT(Wo.T),
        "bias3": np.ascontiguousarray(
            np.stack([bq, bk, bv]).reshape(3, 16, 128).transpose(2, 0, 1)),
        "gwT": np.ascontiguousarray(gw.reshape(NE, NHC, 128).transpose(2, 1, 0)),
        "glA": np.ascontiguousarray(glA_),
        "glBT": np.ascontiguousarray(glB_.T),
        "gupw": np.ascontiguousarray(
            gup.reshape(NE, NHC, 128, NHC, 128).transpose(0, 3, 2, 1, 4)),
        "downw": np.ascontiguousarray(dwn.reshape(NE, NIC, 128, H)),
        "gupA": np.ascontiguousarray(
            gA_.reshape(NE, R, NHC, 128).transpose(0, 3, 2, 1)),
        "gupBT": np.ascontiguousarray(gB_.transpose(0, 2, 1)),
        "downA": np.ascontiguousarray(
            dA_.reshape(NE, R, NIC, 128).transpose(0, 3, 2, 1)),
        "downBT": np.ascontiguousarray(dB_.transpose(0, 2, 1)),
        "ident": np.eye(128, dtype=f32),
    }
    kk = np.arange(128)
    shared["ltri"] = np.triu(np.ones((128, 128), f32))
    shared["mpre"] = (((kk // 8)[:, None] < (kk // 8)[None, :])
                      & ((kk % 8)[:, None] == (kk % 8)[None, :])).astype(f32)
    shared["ecmat"] = np.broadcast_to(
        ((kk % 8) * CAP).astype(f32), (128, 128)).copy()
    tccm = np.zeros((128, 32), f32)
    for t in range(16):
        tccm[:, 2 * t] = t * 128 + kk
        tccm[:, 2 * t + 1] = t * 128 + kk
    shared["tcc2"] = tccm

    inv_freq = 1.0 / (1000000.0 ** (np.arange(0, HD, 2, dtype=np.float64) / HD))
    pos = np.arange(S, dtype=np.float64)
    fr = np.outer(inv_freq, pos)
    cosf = np.cos(fr).astype(f32)
    sinf = np.sin(fr).astype(f32)
    cosd = np.ascontiguousarray(np.vstack([cosf, cosf]))
    sind = np.ascontiguousarray(np.vstack([sinf, -sinf]))
    swap = np.zeros((128, 128), f32)
    swap[0:64, 64:128] = np.eye(64)
    swap[64:128, 0:64] = np.eye(64)
    shared["swapm"] = swap

    percore = []
    for c in range(NCORES):
        b, q = c // 4, c % 4
        percore.append({
            "xT": np.ascontiguousarray(np.roll(x[b].T, -q * TOK, axis=1)),
            "cosD": np.ascontiguousarray(np.roll(cosd, -q * TOK, axis=1)),
            "sinD": np.ascontiguousarray(np.roll(sind, -q * TOK, axis=1)),
        })
    return shared, percore


_CACHED = {}


def kernel(**inputs):
    shared, percore = prep_inputs(inputs)
    if "full" not in _CACHED:
        _CACHED["full"] = build_program()
    nc, nops = _CACHED["full"]
    in_maps = [dict(shared, **m) for m in percore]
    res = run_bass_kernel_spmd(nc, in_maps, core_ids=list(range(NCORES)))
    out = np.empty((B, S, H), np.float32)
    for c in range(NCORES):
        b, q = c // 4, c % 4
        out[b, q * TOK:(q + 1) * TOK, :] = np.asarray(
            res.results[c]["out"], np.float32)
    return out


# version 31
# speedup vs baseline: 1.2193x; 1.0492x over previous
import numpy as np
import ml_dtypes
from contextlib import ExitStack

import concourse.bass as bass
import concourse.mybir as mybir
from concourse.bass_utils import run_bass_kernel_spmd

F32 = mybir.dt.float32
BF16 = mybir.dt.bfloat16
U32 = mybir.dt.uint32
AF = mybir.ActivationFunctionType
ALU = mybir.AluOpType
AX = mybir.AxisListType
IOA = bass.IndirectOffsetOnAxis

B, S, H = 2, 8192, 2048
NCORES = 8
TOK = 2048
NH = 16
HD = 128
NHC = H // 128
NP = S // TOK
NQT = TOK // 512
NKC = S // 128
NE, I2, I, R = 8, 2048, 1024, 16
NIC = I // 128
CAP = 1024
EPS = 1e-5
ISQ = float(1.0 / np.sqrt(128.0))
BIG = 1.0e9

ENGINES = ("tensor", "vector", "scalar", "gpsimd", "sync")


class Sem:
    def __init__(self, handle, name):
        self.h = handle
        self.name = name
        self.n = 0

    def inc(self, k):
        self.n += k
        return self.n


class Buf:
    __slots__ = ("ready", "free", "name")

    def __init__(self, name=""):
        self.ready = {}
        self.free = {}
        self.name = name


def _merge(d, sem, val):
    if d.get(sem, -1) < val:
        d[sem] = val


class Plan:
    def __init__(self, nc, sems):
        self.nc = nc
        self.prog = {e: [] for e in ENGINES}
        self.sems = sems
        self.hwm = {e: {} for e in ENGINES}

    def step(self, eng, build, reads=(), writes=(), inc=True, extra_waits=()):
        waits = {}
        for b in reads:
            for s, v in b.ready.items():
                _merge(waits, s, v)
        for b in writes:
            for s, v in b.free.items():
                _merge(waits, s, v)
        for s, v in extra_waits:
            _merge(waits, s, v)
        sem = self.sems[eng]
        hw = self.hwm[eng]
        wl = []
        for s, v in waits.items():
            assert v <= s.n, f"wait on future {s.name}:{v}>{s.n} (eng {eng})"
            if hw.get(s, -1) < v:
                wl.append((s, v))
                hw[s] = v
        amt = 16 if eng in ("gpsimd", "sync") else 1
        tgt = sem.inc(amt) if inc else None

        def fn(Eng, build=build, wl=tuple(wl), sem=sem, doinc=inc, amt=amt):
            for s, v in wl:
                Eng.wait_ge(s.h, v)
            ins = build(Eng)
            if doinc:
                ins.then_inc(sem.h, amt)
        self.prog[eng].append(fn)
        if inc:
            for b in reads:
                _merge(b.free, sem, tgt)
            for b in writes:
                b.ready = {sem: tgt}
        return tgt

    def raw_wait(self, eng, sem, val):
        if val <= 0:
            return
        hw = self.hwm[eng]
        assert val <= sem.n
        if hw.get(sem, -1) >= val:
            return
        hw[sem] = val
        self.prog[eng].append(lambda Eng, s=sem, v=val: Eng.wait_ge(s.h, v))

    def phase_barrier(self, all_sems):
        snap = [(s, s.n) for s in all_sems]
        for e in ENGINES:
            for s, v in snap:
                self.raw_wait(e, s, v)


def build_program(dbg=(), phases=6):
    nc = bass.Bass("TRN2", target_bir_lowering=False, debug=False)

    def dram_in(name, shape, dtype=F32):
        return nc.dram_tensor(name, shape, dtype, kind="ExternalInput").ap()

    def scratch(name, shape, dtype):
        kind = "ExternalOutput" if name in dbg else "Internal"
        return nc.dram_tensor(name, shape, dtype, kind=kind).ap()

    xT = dram_in("xT", [H, S])
    cosD = dram_in("cosD", [128, S])
    sinD = dram_in("sinD", [128, S])
    swapm = dram_in("swapm", [128, 128])
    wq = dram_in("wq", [NHC, 128, NHC, 128])
    wk = dram_in("wk", [NHC, 128, NHC, 128])
    wv = dram_in("wv", [NHC, 128, NHC, 128])
    wo = dram_in("wo", [NHC, 128, NHC, 128])
    bias3 = dram_in("bias3", [128, 3, 16])
    gwT = dram_in("gwT", [128, NHC, NE])
    glA = dram_in("glA", [R, H])
    glBT = dram_in("glBT", [R, NE])
    gupw = dram_in("gupw", [NE, NHC, 128, NHC, 128])
    downw = dram_in("downw", [NE, NIC, 128, H])
    gupA = dram_in("gupA", [NE, 128, NHC, R])
    gupBT = dram_in("gupBT", [NE, R, I2])
    downA = dram_in("downA", [NE, 128, NIC, R])
    downBT = dram_in("downBT", [NE, R, H])
    ident = dram_in("ident", [128, 128])
    ltri = dram_in("ltri", [128, 128])
    mpre = dram_in("mpre", [128, 128])
    ecmat = dram_in("ecmat", [128, 128])
    tokidf = dram_in("tokidf", [128, 16])

    OUT = nc.dram_tensor("out", [TOK, H], F32, kind="ExternalOutput").ap()

    Ks = scratch("Ks", [NH, 128, S], BF16)
    Vs = scratch("Vs", [NH, 128, S], BF16)
    As = scratch("As", [NH, 128, TOK], BF16)
    X1 = scratch("X1", [NHC, 128, TOK], F32)
    X1tok = scratch("X1tok", [TOK + 128, H], BF16)
    Dall = scratch("Dall", [NE * CAP, H], F32)
    EidxD = scratch("EidxD", [NE * CAP, 1], F32)
    RDBG = (nc.dram_tensor("RDBG", [128, 512], F32, kind="ExternalOutput").ap()
            if "RDBG" in dbg else None)

    es = ExitStack()
    sem_handles = {}
    for sname in ("st", "sv", "sa", "sg", "ss"):
        sem_handles[sname] = Sem(es.enter_context(nc.semaphore(sname)), sname)
    st, sv, sa, sg, ss = (sem_handles[k] for k in ("st", "sv", "sa", "sg", "ss"))
    ALLSEMS = [st, sv, sa, sg, ss]
    sems = {"tensor": st, "vector": sv, "scalar": sa, "gpsimd": sg, "sync": ss}
    P = Plan(nc, sems)

    def sb(stack, name, shape, dtype, side="left"):
        return stack.enter_context(nc.sbuf_tensor(name, shape, dtype, side=side))

    ones_c_f = sb(es, "ones_c_f", [128, 1], F32)
    ones_c_b = sb(es, "ones_c_b", [128, 1], BF16)
    ones_r_f = sb(es, "ones_r_f", [1, 128], F32)
    id_f = sb(es, "id_f", [128, 128], F32)
    id_b = sb(es, "id_b", [128, 128], BF16)
    swp_f = sb(es, "swp_f", [128, 128], F32)
    swp_b = sb(es, "swp_b", [128, 128], BF16)
    biases = sb(es, "biases", [128, 48], F32)
    negC = sb(es, "negC", [128, NH], F32)
    nslots = sb(es, "nslots", [1, 320], F32)
    crow = sb(es, "crow", [1, NH], F32)
    glAb = sb(es, "glAb", [R, H], BF16)
    glBTb = sb(es, "glBTb", [R, NE], BF16)
    gwTb = sb(es, "gwTb", [128, NHC * NE], F32)
    geff = sb(es, "geff", [128, NHC * NE], BF16)
    crow2 = sb(es, "crow2", [1, NH], F32)
    negrow = sb(es, "negrow", [1, NH], F32)
    epsrow = sb(es, "epsrow", [1, 1], F32)
    epscol = sb(es, "epscol", [128, 1], F32)
    sidxu = [sb(es, f"sidxu{r}", [128, 16], U32) for r in range(2)]
    wsel = [sb(es, f"wsel{r}", [128, 16], F32) for r in range(2)]
    eidxall = sb(es, "eidxall", [128, NE * NIC], U32)

    b_ones = Buf("ones")
    b_id = Buf("id")
    b_biases = Buf("biases")
    b_negC = Buf("negC")
    b_nslots = Buf("nslots")
    b_Qres = Buf("Qres")
    b_glA = Buf("glA")
    b_glBT = Buf("glBT")
    b_gwT = Buf("gwT")
    b_geff = Buf("geff")
    b_Ks = Buf("Ks")
    b_Vs = Buf("Vs")
    b_As = Buf("As")
    b_X1 = Buf("X1")
    b_X1n = Buf("X1n")
    b_x1n_sb = Buf("x1n_sb")
    b_X1tok = Buf("X1tok")
    b_idxbuf = Buf("idxbuf")
    b_Dall = Buf("Dall")
    b_sidxu = Buf("sidxu")
    b_wsel = Buf("wsel")
    b_eidxall = Buf("eidxall")

    right_q = ExitStack()
    Qres = sb(right_q, "Qres", [128, NH * TOK], BF16, side="right")

    P.step("vector", lambda E: E.memset(ones_c_f[:], 1.0), writes=[b_ones])
    P.step("vector", lambda E: E.memset(ones_c_b[:], 1.0), writes=[b_ones])
    P.step("vector", lambda E: E.memset(ones_r_f[:], 1.0), writes=[b_ones])
    P.step("vector", lambda E: E.memset(epsrow[:], EPS), writes=[b_ones])
    P.step("vector", lambda E: E.memset(epscol[:], EPS), writes=[b_ones])
    P.step("sync", lambda E: E.dma_start(out=id_f[:], in_=ident), writes=[b_id])
    P.step("vector", lambda E: E.tensor_copy(id_b[:], id_f[:]), reads=[b_id], writes=[b_id])
    P.step("sync", lambda E: E.dma_start(out=swp_f[:], in_=swapm), writes=[b_id])
    P.step("vector", lambda E: E.tensor_copy(swp_b[:], swp_f[:]), reads=[b_id], writes=[b_id])
    P.step("sync", lambda E: E.dma_start(
        out=biases[:], in_=bias3.rearrange("p a b -> p (a b)")), writes=[b_biases])

    REGS = {}

    def _mkregs(E):
        REGS["tok"] = E.to_reg(TOK)
        REGS["slots"] = E.to_reg(NE * CAP - 1)
        return None
    P.step("gpsimd", _mkregs, inc=False)

    p1 = ExitStack()
    Xp = sb(p1, "Xp", [128, NHC * TOK], BF16)
    wcol = [sb(p1, f"wcol{i}", [128, NHC * 128], BF16) for i in range(2)]
    tmpb = [sb(p1, f"tmp{i}", [128, 512], F32) for i in range(3)]
    sqb = [sb(p1, f"sq{i}", [128, 512], BF16) for i in range(3)]
    stage = [sb(p1, f"stage{i}", [128, TOK], BF16) for i in range(2)]
    vtmp = [sb(p1, f"vtmp{i}", [128, 512], BF16) for i in range(3)]
    vstage = [sb(p1, f"vstage{i}", [128, TOK], BF16) for i in range(2)]
    cosb = sb(p1, "cosb", [128, TOK], F32)
    sinb = sb(p1, "sinb", [128, TOK], F32)
    rp1 = [sb(p1, f"rp1_{i}", [128, 512], F32) for i in range(3)]
    rp2 = [sb(p1, f"rp2_{i}", [128, 512], BF16) for i in range(3)]
    psP = [p1.enter_context(nc.psum_tensor(f"psP{i}", [128, 512], F32)) for i in range(3)]
    psN = p1.enter_context(nc.psum_tensor("psN", [1, 512], F32))
    psSw = [p1.enter_context(nc.psum_tensor(f"psSw{i}", [128, 512], F32)) for i in range(3)]
    psVt1 = p1.enter_context(nc.psum_tensor("psVt1", [128, 512], BF16))

    b_Xp = Buf("Xp")
    b_wcol = [Buf(f"wcol{i}") for i in range(2)]
    b_tmp = [Buf(f"tmp{i}") for i in range(3)]
    b_sq = [Buf(f"sq{i}") for i in range(3)]
    b_stage = [Buf(f"stage{i}") for i in range(2)]
    b_vtmp = [Buf(f"vtmp{i}") for i in range(3)]
    b_vstage = [Buf(f"vstage{i}") for i in range(2)]
    b_psVt1 = Buf("psVt1")
    b_cos = Buf("cos")
    b_rp1 = [Buf(f"rp1_{i}") for i in range(3)]
    b_rp2 = [Buf(f"rp2_{i}") for i in range(3)]
    b_psSw = [Buf(f"psSw{i}") for i in range(3)]
    b_psP = [Buf(f"psP{i}") for i in range(3)]
    b_psN = Buf("psN")

    cnt1 = [0]
    pending1 = []

    def p1_panel(p):
        for f in pending1:
            f()
        pending1.clear()
        for hc in range(NHC):
            P.step("gpsimd",
                   lambda E, hc=hc, p=p: E.dma_start(
                       out=Xp[:, hc * TOK:(hc + 1) * TOK],
                       in_=xT[hc * 128:(hc + 1) * 128, p * TOK:(p + 1) * TOK]),
                   writes=[b_Xp] if hc == NHC - 1 else [],
                   extra_waits=tuple(b_Xp.free.items()) if hc == 0 else ())
        P.step("sync", lambda E, p=p: E.dma_start(
            out=cosb[:], in_=cosD[:, p * TOK:(p + 1) * TOK]),
            writes=[b_cos], extra_waits=tuple(b_cos.free.items()))
        P.step("sync", lambda E, p=p: E.dma_start(
            out=sinb[:], in_=sinD[:, p * TOK:(p + 1) * TOK]),
            writes=[b_cos])

        tensors = [("k", wk, 1), ("v", wv, 2)] + ([("q", wq, 0)] if p == 0 else [])

        def flush():
            for f in pending1:
                f()
            pending1.clear()

        for tname, wsrc, bidx in tensors:
            for dc in range(NHC):
                wb = dc % 2
                sbf = stage[dc % 2]
                b_sbf = b_stage[dc % 2]
                P.step("gpsimd",
                       lambda E, wsrc=wsrc, dc=dc, wb=wb: E.dma_start(
                           out=wcol[wb][:],
                           in_=wsrc[dc].rearrange("p h c -> p (h c)")),
                       writes=[b_wcol[wb]])
                for qt in range(NQT):
                    pp = cnt1[0] % 3
                    tb = cnt1[0] % 3
                    cnt1[0] += 1

                    def mm_group(E, dc=dc, qt=qt, wb=wb, pp=pp):
                        last = None
                        for hc in range(NHC):
                            last = E.matmul(
                                psP[pp][:],
                                wcol[wb][:, hc * 128:(hc + 1) * 128],
                                Xp[:, hc * TOK + qt * 512: hc * TOK + qt * 512 + 512],
                                start=(hc == 0), stop=(hc == NHC - 1))
                        return last
                    P.step("tensor", mm_group,
                           reads=[b_wcol[wb], b_Xp], writes=[b_psP[pp]])
                    flush()

                    def tail(tname=tname, dc=dc, qt=qt, pp=pp, tb=tb, p=p,
                             bidx=bidx, sbf=sbf, b_sbf=b_sbf):
                        if tname == "v":
                            vb = tb
                            vs = stage[dc % 2]
                            P.step("scalar",
                                   lambda E: E.add(
                                       vtmp[vb][:], psP[pp][:],
                                       biases[:, bidx * 16 + dc: bidx * 16 + dc + 1]),
                                   reads=[b_psP[pp], b_biases], writes=[b_vtmp[vb]])

                            def vtr(E, vb=vb):
                                last = None
                                for s4 in range(4):
                                    last = E.matmul(
                                        psVt1[:, s4 * 128:(s4 + 1) * 128],
                                        vtmp[vb][:, s4 * 128:(s4 + 1) * 128],
                                        id_b[:], start=(s4 == 0), stop=(s4 == 3),
                                        is_transpose=True)
                                return last
                            P.step("tensor", vtr, reads=[b_vtmp[vb], b_id],
                                   writes=[b_psVt1])
                            P.step("vector", lambda E, qt=qt, dc=dc: E.tensor_copy(
                                vstage[dc % 2][:, qt * 512:qt * 512 + 512], psVt1[:]),
                                reads=[b_psVt1],
                                writes=[b_vstage[dc % 2]] if qt == NQT - 1 else [],
                                extra_waits=tuple(b_vstage[dc % 2].free.items()) if qt == 0 else ())
                            return
                        P.step("scalar",
                               lambda E: E.add(
                                   tmpb[tb][:], psP[pp][:],
                                   biases[:, bidx * 16 + dc: bidx * 16 + dc + 1]),
                               reads=[b_psP[pp], b_biases], writes=[b_tmp[tb]])
                        P.step("scalar",
                               lambda E: E.activation(
                                   sqb[tb][:], tmpb[tb][:], AF.Square),
                               reads=[b_tmp[tb]], writes=[b_sq[tb]])
                        if tname == "q":
                            dst, dof, dstbuf = Qres, dc * TOK + qt * 512, b_Qres
                        else:
                            dst, dof, dstbuf = sbf, qt * 512, b_sbf
                        cs = qt * 512
                        rb = tb
                        P.step("vector", lambda E: E.tensor_mul(
                            rp1[rb][:], tmpb[tb][:], cosb[:, cs:cs + 512]),
                            reads=[b_tmp[tb], b_cos], writes=[b_rp1[rb]])
                        P.step("vector", lambda E: E.tensor_mul(
                            rp2[rb][:], tmpb[tb][:], sinb[:, cs:cs + 512]),
                            reads=[b_tmp[tb], b_cos], writes=[b_rp2[rb]])
                        P.step("tensor",
                               lambda E: E.matmul(
                                   psN[:], ones_c_b[:], sqb[tb][:], start=True, stop=True),
                               reads=[b_sq[tb], b_ones], writes=[b_psN])
                        P.step("tensor", lambda E: E.matmul(
                            psSw[rb][:], swp_b[:], rp2[rb][:], start=True, stop=True),
                            reads=[b_rp2[rb], b_id], writes=[b_psSw[rb]])
                        P.step("vector", lambda E, dst=dst, dof=dof: E.tensor_add(
                            dst[:, dof:dof + 512], rp1[rb][:], psSw[rb][:]),
                            reads=[b_rp1[rb], b_psSw[rb]],
                            writes=[dstbuf] if qt == NQT - 1 else [],
                            extra_waits=tuple(dstbuf.free.items()) if qt == 0 else ())
                        slot = (dc * 16 + p * 4 + qt) if tname == "k" else (256 + dc * 4 + qt)
                        P.step("vector",
                               lambda E, slot=slot: E.tensor_reduce(
                                   nslots[0:1, slot:slot + 1], psN[:], AX.X, ALU.max),
                               reads=[b_psN], writes=[b_nslots])
                    pending1.append(tail)
                if tname == "k":
                    def kdma(dc=dc, p=p, sbf=sbf, b_sbf=b_sbf):
                        P.step("sync",
                               lambda E: E.dma_start(
                                   out=Ks[dc][:, p * TOK:(p + 1) * TOK], in_=sbf[:]),
                               reads=[b_sbf], writes=[b_Ks])
                    pending1.append(kdma)
                elif tname == "v":
                    def vdma(dc=dc, p=p):
                        P.step("sync",
                               lambda E: E.dma_start(
                                   out=Vs[dc][:, p * TOK:(p + 1) * TOK],
                                   in_=vstage[dc % 2][:]),
                               reads=[b_vstage[dc % 2]], writes=[b_Vs])
                    pending1.append(vdma)
        flush()

    for p in range(NP):
        p1_panel(p)
    p1.close()
    P.phase_barrier(ALLSEMS)

    if phases >= 2:
        pnc = ExitStack()
        psNC = pnc.enter_context(nc.psum_tensor("psNC", [128, NH], F32))
        b_psNC = Buf("psNC")
        P.step("vector", lambda E: E.tensor_reduce(
            crow[:], nslots[0:1, 0:256].rearrange("a (d s) -> a d s", s=16),
            AX.X, ALU.max), reads=[b_nslots], writes=[b_negC])
        P.step("vector", lambda E: E.tensor_reduce(
            crow2[:], nslots[0:1, 256:320].rearrange("a (d s) -> a d s", s=4),
            AX.X, ALU.max), reads=[b_nslots], writes=[b_negC])
        P.step("vector", lambda E: E.tensor_mul(crow[:], crow[:], crow2[:]),
               reads=[b_negC], writes=[b_negC])
        P.step("scalar", lambda E: E.activation(crow2[:], crow[:], AF.Sqrt, scale=1.0 / 128.0),
               reads=[b_negC], writes=[b_negC])
        P.step("vector", lambda E: E.tensor_scalar_mul(negrow[:], crow2[:], -1.0),
               reads=[b_negC], writes=[b_negC])
        P.step("tensor", lambda E: E.matmul(psNC[:], ones_r_f[:], negrow[:],
                                            start=True, stop=True),
               reads=[b_negC, b_ones], writes=[b_psNC])
        P.step("vector", lambda E: E.tensor_copy(negC[:], psNC[:]),
               reads=[b_psNC], writes=[b_negC])
        P.phase_barrier(ALLSEMS)
        pnc.close()

        p2 = ExitStack()
        Kh = [sb(p2, f"Kh{i}", [128, S], BF16) for i in range(2)]
        Vtb = [sb(p2, f"Vtb{i}", [128, S], BF16) for i in range(2)]
        Pt = [sb(p2, f"Pt{i}", [128, 1024], BF16) for i in range(6)]
        linv = sb(p2, "linv", [1, 512], F32)
        lnrow = sb(p2, "lnrow", [1, 512], F32)
        lbc = sb(p2, "lbc", [128, 512], F32)
        ls2 = [sb(p2, f"ls2_{i}", [128, 512], BF16) for i in range(2)]
        ls4 = sb(p2, "ls4", [128, 512], BF16)
        laccf = [sb(p2, f"laccf{i}", [128, 512], F32) for i in range(2)]
        ast = [sb(p2, f"ast{i}", [128, TOK], BF16) for i in range(2)]
        psS = [p2.enter_context(nc.psum_tensor(f"psS{i}", [128, 1024], F32)) for i in range(2)]
        psO = [p2.enter_context(nc.psum_tensor(f"psO{i}", [128, 512], F32)) for i in range(2)]
        psLB = p2.enter_context(nc.psum_tensor("psLB", [128, 512], F32))

        b_Kh = [Buf(f"Kh{i}") for i in range(2)]
        b_Vt2 = [Buf(f"Vtb{i}") for i in range(2)]
        b_Pt = [Buf(f"Pt{i}") for i in range(6)]
        b_linv = Buf("linv")
        b_lbc = Buf("lbc")
        b_ls2 = [Buf(f"ls2_{i}") for i in range(2)]
        b_ls4 = Buf("ls4")
        b_laccf = [Buf(f"laccf{i}") for i in range(2)]
        b_ast = [Buf(f"ast{i}") for i in range(2)]
        b_psS = [Buf(f"psS{i}") for i in range(2)]
        b_psO = [Buf(f"psO{i}") for i in range(2)]
        b_psLB = Buf("psLB")

        pend2 = []
        for h in range(NH):
            hb = h % 2
            P.step("sync", lambda E, h=h, hb=hb: E.dma_start(out=Kh[hb][:], in_=Ks[h]),
                   reads=[b_Ks], writes=[b_Kh[hb]])
            P.step("sync", lambda E, h=h, hb=hb: E.dma_start(out=Vtb[hb][:], in_=Vs[h]),
                   reads=[b_Vs], writes=[b_Vt2[hb]])
            for qt in range(NQT):
                qoff = h * TOK + qt * 512
                npair = NKC // 2

                def plan_smm(ip, hb=hb, qoff=qoff):
                    pp = ip % 2
                    pt = ip % 6
                    kc0, kc1 = 2 * ip, 2 * ip + 1

                    def smm(E, kc0=kc0, kc1=kc1, pp=pp, hb=hb, qoff=qoff):
                        E.matmul(psS[pp][:, 0:512],
                                 Kh[hb][:, kc0 * 128:(kc0 + 1) * 128],
                                 Qres[:, qoff:qoff + 512], start=True, stop=True)
                        return E.matmul(psS[pp][:, 512:1024],
                                        Kh[hb][:, kc1 * 128:(kc1 + 1) * 128],
                                        Qres[:, qoff:qoff + 512], start=True, stop=True)
                    P.step("tensor", smm, reads=[b_Kh[hb], b_Qres], writes=[b_psS[pp]])
                    P.step("scalar", lambda E, pp=pp, pt=pt, h=h: E.activation(
                        Pt[pt][:], psS[pp][:], AF.Exp,
                        bias=negC[:, h:h + 1], scale=ISQ),
                        reads=[b_psS[pp], b_negC], writes=[b_Pt[pt]])

                qo = qt % 2

                def plan_pvl(ip, npair=npair, qo=qo):
                    pt = ip % 6
                    kc0, kc1 = 2 * ip, 2 * ip + 1

                    def pvl(E, kc0=kc0, kc1=kc1, pt=pt, ip=ip, npair=npair, hb=hb, qo=qo):
                        E.matmul(psO[qo][:], Vtb[hb][:, kc0 * 128:(kc0 + 1) * 128],
                                 Pt[pt][:, 0:512], start=(ip == 0), stop=False)
                        return E.matmul(psO[qo][:], Vtb[hb][:, kc1 * 128:(kc1 + 1) * 128],
                                        Pt[pt][:, 512:1024], start=False,
                                        stop=(ip == npair - 1))
                    P.step("tensor", pvl, reads=[b_Pt[pt], b_Vt2[hb]],
                           writes=[b_psO[qo]])
                    lsp = ip % 2
                    P.step("vector" if ip % 2 == 0 else "gpsimd",
                           lambda E, pt=pt, lsp=lsp: E.tensor_add(
                               ls2[lsp][:], Pt[pt][:, 0:512], Pt[pt][:, 512:1024]),
                           reads=[b_Pt[pt]], writes=[b_ls2[lsp]])
                    if ip == 0:
                        P.step("vector", lambda E, lsp=lsp, qo=qo: E.tensor_copy(
                            laccf[qo][:], ls2[lsp][:]),
                            reads=[b_ls2[lsp]], writes=[b_laccf[qo]])
                    else:
                        P.step("vector", lambda E, lsp=lsp, qo=qo: E.tensor_add(
                            laccf[qo][:], ls2[lsp][:], laccf[qo][:]),
                            reads=[b_ls2[lsp], b_laccf[qo]], writes=[b_laccf[qo]])

                for ip in range(npair):
                    plan_smm(ip)
                    if ip in (6, 14, 22) and pend2:
                        pend2.pop(0)()
                    if ip >= 2:
                        plan_pvl(ip - 2)
                plan_pvl(npair - 2)
                plan_pvl(npair - 1)

                def st1(hb=hb, qt=qt, qo=qo):
                    P.step("tensor", lambda E, qo=qo: E.matmul(
                        psLB[0:1, :], ones_c_f[:], laccf[qo][:], start=True, stop=True),
                        reads=[b_laccf[qo], b_ones], writes=[b_psLB])
                    P.step("scalar", lambda E: E.activation(
                        lnrow[:], psLB[0:1, :], AF.Ln),
                        reads=[b_psLB], writes=[b_linv])
                    P.step("scalar", lambda E: E.activation(
                        linv[:], lnrow[:], AF.Exp, scale=-1.0),
                        reads=[b_linv], writes=[b_linv])

                def st2():
                    P.step("tensor", lambda E: E.matmul(
                        psLB[:], ones_r_f[:], linv[:], start=True, stop=True),
                        reads=[b_linv, b_ones], writes=[b_psLB])
                    P.step("vector", lambda E: E.tensor_copy(lbc[:], psLB[:]),
                           reads=[b_psLB], writes=[b_lbc])

                def st3(hb=hb, qt=qt, qo=qo):
                    P.step("vector", lambda E, hb=hb, qt=qt, qo=qo: E.tensor_mul(
                        ast[hb][:, qt * 512:qt * 512 + 512], psO[qo][:], lbc[:]),
                        reads=[b_psO[qo], b_lbc],
                        writes=[b_ast[hb]] if qt == NQT - 1 else [],
                        extra_waits=tuple(b_ast[hb].free.items()) if qt == 0 else ())
                pend2.extend([st1, st2, st3])
            for f in pend2:
                f()
            pend2.clear()
            P.step("sync", lambda E, h=h, hb=hb: E.dma_start(out=As[h], in_=ast[hb][:]),
                   reads=[b_ast[hb]], writes=[b_As])

        P.phase_barrier(ALLSEMS)
        p2.close()
        right_q.close()

    if phases >= 3:
        right_x = ExitStack()
        x1n_sb = sb(right_x, "x1n_sb", [128, NHC * TOK], BF16, side="right")

        p3 = ExitStack()
        ah = sb(p3, "ah", [128, NHC * 1024], BF16)
        wob = [sb(p3, f"wob{i}", [128, NHC * 128], BF16) for i in range(2)]
        rtile = [sb(p3, f"rtile{i}", [128, 512], F32) for i in range(2)]
        x1t = [sb(p3, f"x1t{i}", [128, 512], F32) for i in range(2)]
        x1b = [sb(p3, f"x1b{i}", [128, 512], BF16) for i in range(2)]
        sq3 = [sb(p3, f"sq3{i}", [128, 512], BF16) for i in range(2)]
        rows_mu = sb(p3, "rows_mu3", [1, TOK], F32)
        rows_e2 = sb(p3, "rows_e23", [1, TOK], F32)
        rows_m2 = sb(p3, "rows_m23", [1, TOK], F32)
        rows_rs = sb(p3, "rows_rs3", [1, TOK], F32)
        rows_s = rows_e2
        psOut = [p3.enter_context(nc.psum_tensor(f"psOut{i}", [128, 512], F32)) for i in range(2)]
        psSum = p3.enter_context(nc.psum_tensor("psSum3", [1, 1024], F32))
        psSq = p3.enter_context(nc.psum_tensor("psSq3", [1, 1024], F32))

        b_ah = Buf("ah")
        b_wob = [Buf(f"wob{i}") for i in range(2)]
        b_rtile = [Buf(f"rt{i}") for i in range(2)]
        b_x1t = [Buf(f"x1t{i}") for i in range(2)]
        b_x1b = [Buf(f"x1b{i}") for i in range(2)]
        b_sq3 = [Buf(f"sq3{i}") for i in range(2)]
        b_rows = Buf("rows3")
        b_psOut = [Buf(f"psOut{i}") for i in range(2)]
        b_psSum = Buf("psSum3")
        b_psSq = Buf("psSq3")

        cnt3 = [0]
        pend3 = []
        for half in range(2):
            for dc in range(NH):
                P.step("sync", lambda E, dc=dc, half=half: E.dma_start(
                    out=ah[:, dc * 1024:(dc + 1) * 1024],
                    in_=As[dc][:, half * 1024:(half + 1) * 1024]),
                    reads=[b_As], writes=[b_ah] if dc == NH - 1 else [],
                    extra_waits=tuple(b_ah.free.items()) if dc == 0 else ())
            for hoc in range(NHC):
                wb = hoc % 2
                P.step("gpsimd", lambda E, hoc=hoc, wb=wb: E.dma_start(
                    out=wob[wb][:], in_=wo[hoc].rearrange("p h c -> p (h c)")),
                    writes=[b_wob[wb]])
                for sqt in range(2):
                    qt = half * 2 + sqt
                    pp = cnt3[0] % 2
                    cnt3[0] += 1

                    def omm(E, wb=wb, sqt=sqt, pp=pp):
                        last = None
                        for dc in range(NHC):
                            last = E.matmul(
                                psOut[pp][:], wob[wb][:, dc * 128:(dc + 1) * 128],
                                ah[:, dc * 1024 + sqt * 512: dc * 1024 + sqt * 512 + 512],
                                start=(dc == 0), stop=(dc == NHC - 1))
                        return last
                    P.step("tensor", omm, reads=[b_wob[wb], b_ah], writes=[b_psOut[pp]])
                    for f in pend3:
                        f()
                    pend3.clear()

                    def tail3(hoc=hoc, qt=qt, pp=pp, sqt=sqt):
                        P.step("sync", lambda E: E.dma_start(
                            out=rtile[pp][:],
                            in_=xT[hoc * 128:(hoc + 1) * 128, qt * 512:qt * 512 + 512]),
                            writes=[b_rtile[pp]])
                        P.step("vector", lambda E: E.tensor_add(
                            x1t[pp][:], psOut[pp][:], rtile[pp][:]),
                            reads=[b_psOut[pp], b_rtile[pp]], writes=[b_x1t[pp]])
                        P.step("sync", lambda E: E.dma_start(
                            out=X1[hoc][:, qt * 512:qt * 512 + 512], in_=x1t[pp][:]),
                            reads=[b_x1t[pp]], writes=[b_X1])
                        P.step("vector", lambda E: E.tensor_copy(
                            x1b[pp][:], x1t[pp][:]),
                            reads=[b_x1t[pp]], writes=[b_x1b[pp]])
                        P.step("scalar", lambda E: E.activation(
                            sq3[pp][:], x1t[pp][:], AF.Square),
                            reads=[b_x1t[pp]], writes=[b_sq3[pp]])
                        P.step("tensor", lambda E: E.matmul(
                            psSum[0:1, sqt * 512:sqt * 512 + 512], ones_c_b[:], x1b[pp][:],
                            start=(hoc == 0), stop=(hoc == NHC - 1)),
                            reads=[b_x1b[pp], b_ones], writes=[b_psSum])
                        P.step("tensor", lambda E: E.matmul(
                            psSq[0:1, sqt * 512:sqt * 512 + 512], ones_c_b[:], sq3[pp][:],
                            start=(hoc == 0), stop=(hoc == NHC - 1)),
                            reads=[b_sq3[pp], b_ones], writes=[b_psSq])
                    pend3.append(tail3)
            for f in pend3:
                f()
            pend3.clear()
            P.step("vector", lambda E, half=half: E.tensor_scalar_mul(
                rows_mu[0:1, half * 1024:(half + 1) * 1024], psSum[:], 1.0 / H),
                reads=[b_psSum], writes=[b_rows])
            P.step("vector", lambda E, half=half: E.tensor_scalar_mul(
                rows_e2[0:1, half * 1024:(half + 1) * 1024], psSq[:], 1.0 / H),
                reads=[b_psSq], writes=[b_rows])
        P.step("scalar", lambda E: E.activation(rows_m2[:], rows_mu[:], AF.Square),
               reads=[b_rows], writes=[b_rows])
        P.step("vector", lambda E: E.tensor_sub(rows_rs[:], rows_e2[:], rows_m2[:]),
               reads=[b_rows], writes=[b_rows])
        P.step("scalar", lambda E: E.activation(rows_m2[:], rows_rs[:], AF.Ln, bias=epsrow[:]),
               reads=[b_rows], writes=[b_rows])
        P.step("scalar", lambda E: E.activation(rows_rs[:], rows_m2[:], AF.Exp, scale=-0.5),
               reads=[b_rows], writes=[b_rows])
        P.step("vector", lambda E: E.tensor_mul(rows_s[:], rows_mu[:], rows_rs[:]),
               reads=[b_rows], writes=[b_rows])
        bcR = sb(p3, "bcR3", [128, TOK], F32)
        bcS = sb(p3, "bcS3", [128, TOK], F32)
        b_bc = Buf("bc3")
        psBc = [p3.enter_context(nc.psum_tensor(f"psBc3{i}", [128, 512], F32)) for i in range(2)]
        b_psBc = [Buf(f"psBc3{i}") for i in range(2)]
        for qt in range(NQT):
            P.step("tensor", lambda E, qt=qt: E.matmul(
                psBc[0][:], ones_r_f[:], rows_rs[0:1, qt * 512:qt * 512 + 512],
                start=True, stop=True),
                reads=[b_rows, b_ones], writes=[b_psBc[0]])
            P.step("vector", lambda E, qt=qt: E.tensor_copy(
                bcR[:, qt * 512:qt * 512 + 512], psBc[0][:]),
                reads=[b_psBc[0]], writes=[b_bc] if qt == NQT - 1 else [])
            P.step("tensor", lambda E, qt=qt: E.matmul(
                psBc[1][:], ones_r_f[:], rows_s[0:1, qt * 512:qt * 512 + 512],
                start=True, stop=True),
                reads=[b_rows, b_ones], writes=[b_psBc[1]])
            P.step("vector", lambda E, qt=qt: E.tensor_copy(
                bcS[:, qt * 512:qt * 512 + 512], psBc[1][:]),
                reads=[b_psBc[1]], writes=[b_bc] if qt == NQT - 1 else [])
        if phases >= 4:
            P.step("gpsimd", lambda E: E.dma_start(out=glAb[:], in_=glA),
                   writes=[b_glA])
            P.step("gpsimd", lambda E: E.dma_start(out=glBTb[:], in_=glBT),
                   writes=[b_glBT])
            P.step("sync", lambda E: E.dma_start(
                out=gwTb[:], in_=gwT.rearrange("p h e -> p (h e)")), writes=[b_gwT])
            for hc in range(NHC):
                P.step("tensor", lambda E, hc=hc: E.matmul(
                    psBc[0][:, 0:NE], glAb[:, hc * 128:(hc + 1) * 128], glBTb[:],
                    start=True, stop=True),
                    reads=[b_glA, b_glBT], writes=[b_psBc[0]])
                P.step("vector", lambda E, hc=hc: E.scalar_tensor_tensor(
                    geff[:, hc * NE:(hc + 1) * NE], psBc[0][:, 0:NE], 2.0,
                    gwTb[:, hc * NE:(hc + 1) * NE], ALU.mult, ALU.add),
                    reads=[b_psBc[0], b_gwT], writes=[b_geff] if hc == NHC - 1 else [])
        x1L = [sb(p3, f"x1L{i}", [128, TOK], F32) for i in range(2)]
        x1M = [sb(p3, f"x1M{i}", [128, TOK], F32) for i in range(2)]
        b_x1L = [Buf(f"x1L{i}") for i in range(2)]
        b_x1M = [Buf(f"x1M{i}") for i in range(2)]
        for hoc in range(NHC):
            pp = hoc % 2
            mule = "gpsimd" if hoc % 3 == 2 else "vector"
            P.step("sync", lambda E, hoc=hoc, pp=pp: E.dma_start(
                out=x1L[pp][:], in_=X1[hoc]),
                reads=[b_X1], writes=[b_x1L[pp]])
            P.step(mule, lambda E, pp=pp: E.tensor_mul(
                x1M[pp][:], x1L[pp][:], bcR[:]),
                reads=[b_x1L[pp], b_bc], writes=[b_x1M[pp]])
            P.step(mule, lambda E, pp=pp: E.tensor_sub(
                x1L[pp][:], x1M[pp][:], bcS[:]),
                reads=[b_x1M[pp]], writes=[b_x1L[pp]])
            P.step("vector", lambda E, pp=pp, hoc=hoc: E.tensor_copy(
                x1n_sb[:, hoc * TOK:(hoc + 1) * TOK], x1L[pp][:]),
                reads=[b_x1L[pp]],
                writes=[b_x1n_sb] if hoc == NHC - 1 else [],
                extra_waits=tuple(b_x1n_sb.free.items()) if hoc == 0 else ())
        P.phase_barrier(ALLSEMS)
        p3.close()

    if phases >= 4:
        p3t = ExitStack()
        xrow = [sb(p3t, f"xrow{i}", [128, H], BF16) for i in range(2)]
        psXT = [p3t.enter_context(nc.psum_tensor(f"psXT{i}", [128, 512], BF16))
                for i in range(1)]
        b_xrow = [Buf(f"xrow{i}") for i in range(2)]
        b_psXT = [Buf(f"psXT{i}") for i in range(1)]

        P.step("vector", lambda E: E.memset(xrow[0][:], 0.0), writes=[b_xrow[0]])
        P.step("sync", lambda E: E.dma_start(
            out=X1tok[TOK:TOK + 128, :], in_=xrow[0][:]),
            reads=[b_xrow[0]], writes=[b_X1tok])
        for tc in range(16):
            tb = tc % 2
            for g4 in range(4):
                pp = 0

                def xtr(E, tc=tc, g4=g4, pp=pp):
                    last = None
                    for k in range(4):
                        hc = g4 * 4 + k
                        last = E.matmul(
                            psXT[pp][:, k * 128:(k + 1) * 128],
                            x1n_sb[:, hc * TOK + tc * 128: hc * TOK + tc * 128 + 128],
                            id_b[:], start=(k == 0), stop=(k == 3),
                            is_transpose=True)
                    return last
                P.step("tensor", xtr, reads=[b_x1n_sb, b_id], writes=[b_psXT[pp]])
                P.step("vector", lambda E, tb=tb, g4=g4, pp=pp: E.tensor_copy(
                    xrow[tb][:, g4 * 512:(g4 + 1) * 512], psXT[pp][:]),
                    reads=[b_psXT[pp]],
                    writes=[b_xrow[tb]] if g4 == 3 else [],
                    extra_waits=tuple(b_xrow[tb].free.items()) if g4 == 0 else ())
            P.step("sync", lambda E, tc=tc, tb=tb: E.dma_start(
                out=X1tok[tc * 128:(tc + 1) * 128, :], in_=xrow[tb][:]),
                reads=[b_xrow[tb]], writes=[b_X1tok])

        p4 = ExitStack()
        LTa = sb(p4, "LTa", [128, 128], F32)
        suba = sb(p4, "suba", [128, 128], F32)
        Ea = sb(p4, "Ea", [128, 128], F32)
        m1a = sb(p4, "m1a", [128, 128], F32)
        m2a = sb(p4, "m2a", [128, 128], F32)
        msel = sb(p4, "msel", [128, 128], F32)
        Lma = sb(p4, "Lma", [128, 128], F32)
        nm1a = sb(p4, "nm1a", [128, 16], F32)
        nm2a = sb(p4, "nm2a", [128, 16], F32)
        dena = sb(p4, "dena", [128, 16], F32)
        dinva = sb(p4, "dinva", [128, 16], F32)
        w_all = sb(p4, "w_all", [128, 16 * NE], F32)
        csum = sb(p4, "csum", [128, 128], F32)
        totc = sb(p4, "totc", [128, 1], F32)
        basc = sb(p4, "basc", [128, 1], F32)
        basr = sb(p4, "basr", [1, 128], F32)
        posm = sb(p4, "posm", [128, 128], F32)
        ovf = sb(p4, "ovf", [128, 128], F32)
        slotm = sb(p4, "slotm", [128, 128], F32)
        wrk4 = sb(p4, "wrk4", [128, 128], F32)
        sidxf = [sb(p4, f"sidxf{r}", [128, 16], F32) for r in range(2)]
        ltri_s = sb(p4, "ltri_s", [128, 128], F32)
        mpre_s = sb(p4, "mpre_s", [128, 128], F32)
        ecm_s = sb(p4, "ecm_s", [128, 128], F32)
        tokid_s = sb(p4, "tokid_s", [128, 16], F32)
        einit = sb(p4, "einit", [64, 128], F32)
        ebak = sb(p4, "ebak", [64, 128], F32)

        b_r = Buf("router")
        b_w_all = Buf("w_all")
        b_idxm = Buf("idxm")
        b_c4 = Buf("c4")
        b_tokid = Buf("tokid")
        b_einit = Buf("einit")
        b_eidxD = Buf("eidxD")
        b_ebak = Buf("ebak")

        P.step("sync", lambda E: E.dma_start(out=ltri_s[:], in_=ltri), writes=[b_c4])
        P.step("sync", lambda E: E.dma_start(out=mpre_s[:], in_=mpre), writes=[b_c4])
        P.step("sync", lambda E: E.dma_start(out=ecm_s[:], in_=ecmat), writes=[b_c4])
        P.step("sync", lambda E: E.dma_start(out=tokid_s[:], in_=tokidf), writes=[b_tokid])
        P.step("vector", lambda E: E.memset(einit[:], float(TOK)), writes=[b_einit])
        P.step("sync", lambda E: E.dma_start(
            out=EidxD.rearrange("(a b) one -> a (b one)", b=128), in_=einit[:]),
            reads=[b_einit], writes=[b_eidxD])

        pr = ExitStack()
        psLt = [pr.enter_context(nc.psum_tensor(f"psLt{i}", [128, NE], F32))
                for i in range(2)]
        psIC = pr.enter_context(nc.psum_tensor("psIC", [128, 128], F32))
        psC1 = pr.enter_context(nc.psum_tensor("psC1", [128, 1], F32))
        psR1 = pr.enter_context(nc.psum_tensor("psR1", [1, 128], F32))
        psBB = pr.enter_context(nc.psum_tensor("psBB", [128, 128], F32))
        psET = pr.enter_context(nc.psum_tensor("psET", [128, 64], F32))
        b_psLt = [Buf(f"psLt{i}") for i in range(2)]
        b_psIC = Buf("psIC")
        b_psC1 = Buf("psC1")
        b_psR1 = Buf("psR1")
        b_psBB = Buf("psBB")
        b_psET = Buf("psET")
        for tc in range(16):
            lp = tc % 2

            def rmm(E, tc=tc, lp=lp):
                last = None
                for hc in range(NHC):
                    last = E.matmul(
                        psLt[lp][:],
                        x1n_sb[:, hc * TOK + tc * 128: hc * TOK + tc * 128 + 128],
                        geff[:, hc * NE:(hc + 1) * NE],
                        start=(hc == 0), stop=(hc == NHC - 1))
                return last
            P.step("tensor", rmm, reads=[b_x1n_sb, b_geff], writes=[b_psLt[lp]])
            P.step("vector", lambda E, tc=tc, lp=lp: E.tensor_copy(
                LTa[:, tc * NE:(tc + 1) * NE], psLt[lp][:]),
                reads=[b_psLt[lp]], writes=[b_r] if tc == 15 else [])
        seg = lambda t: t.rearrange("p (a b) -> p a b", b=NE)
        bc3 = lambda t: t.unsqueeze(2).broadcast_to([128, 16, NE])
        P.step("vector", lambda E: E.tensor_reduce(
            nm1a[:], seg(LTa[:, :]), AX.X, ALU.max, negate=True),
            reads=[b_r], writes=[b_r])
        P.step("vector", lambda E: E.tensor_tensor(
            seg(suba[:, :]), seg(LTa[:, :]), bc3(nm1a[:, :]), ALU.add),
            reads=[b_r], writes=[b_r])
        P.step("scalar", lambda E: E.activation(Ea[:], suba[:], AF.Exp),
               reads=[b_r], writes=[b_r])
        P.step("vector", lambda E: E.tensor_single_scalar(
            m1a[:], suba[:], 0.0, ALU.is_ge),
            reads=[b_r], writes=[b_r])
        P.step("vector", lambda E: E.scalar_tensor_tensor(
            Lma[:], m1a[:], -BIG, suba[:], ALU.mult, ALU.add),
            reads=[b_r], writes=[b_r])
        P.step("vector", lambda E: E.tensor_reduce(
            nm2a[:], seg(Lma[:, :]), AX.X, ALU.max, negate=True),
            reads=[b_r], writes=[b_r])
        P.step("vector", lambda E: E.tensor_tensor(
            seg(Lma[:, :]), seg(Lma[:, :]), bc3(nm2a[:, :]), ALU.add),
            reads=[b_r], writes=[b_r])
        P.step("vector", lambda E: E.tensor_single_scalar(
            m2a[:], Lma[:], 0.0, ALU.is_ge),
            reads=[b_r], writes=[b_r])
        P.step("vector", lambda E: E.tensor_add(msel[:], m1a[:], m2a[:]),
               reads=[b_r], writes=[b_r])
        P.step("vector", lambda E: E.tensor_mul(Ea[:], Ea[:], msel[:]),
               reads=[b_r], writes=[b_r])
        P.step("vector", lambda E: E.tensor_reduce(
            dena[:], seg(Ea[:, :]), AX.X, ALU.add),
            reads=[b_r], writes=[b_r])
        P.step("vector", lambda E: E.reciprocal(dinva[:], dena[:]),
               reads=[b_r], writes=[b_r])
        P.step("vector", lambda E: E.tensor_tensor(
            seg(w_all[:, :]), seg(Ea[:, :]), bc3(dinva[:, :]), ALU.mult),
            reads=[b_r], writes=[b_w_all])

        P.step("tensor", lambda E: E.matmul(
            psIC[:], ltri_s[:], msel[:], start=True, stop=True),
            reads=[b_c4, b_r], writes=[b_psIC])
        P.step("vector", lambda E: E.tensor_copy(csum[:], psIC[:]),
               reads=[b_psIC], writes=[b_idxm])
        P.step("tensor", lambda E: E.matmul(
            psC1[:], msel[:], ones_c_f[:], start=True, stop=True),
            reads=[b_r, b_ones], writes=[b_psC1])
        P.step("vector", lambda E: E.tensor_copy(totc[:], psC1[:]),
               reads=[b_psC1], writes=[b_idxm])
        P.step("tensor", lambda E: E.matmul(
            psC1[:], mpre_s[:], totc[:], start=True, stop=True),
            reads=[b_c4, b_idxm], writes=[b_psC1])
        P.step("vector", lambda E: E.tensor_copy(basc[:], psC1[:]),
               reads=[b_psC1], writes=[b_idxm])
        P.step("tensor", lambda E: E.matmul(
            psR1[:], basc[:], id_f[:], start=True, stop=True, is_transpose=True),
            reads=[b_idxm, b_id], writes=[b_psR1])
        P.step("vector", lambda E: E.tensor_copy(basr[:], psR1[0:1, :]),
               reads=[b_psR1], writes=[b_idxm])
        P.step("tensor", lambda E: E.matmul(
            psBB[:], ones_r_f[:], basr[:], start=True, stop=True),
            reads=[b_idxm, b_ones], writes=[b_psBB])
        P.step("vector", lambda E: E.scalar_tensor_tensor(
            posm[:], csum[:], -1.0, psBB[:], ALU.add, ALU.add),
            reads=[b_idxm, b_psBB], writes=[b_idxm])
        P.step("vector", lambda E: E.tensor_single_scalar(
            ovf[:], posm[:], float(CAP), ALU.is_ge),
            reads=[b_idxm], writes=[b_idxm])
        P.step("vector", lambda E: E.tensor_tensor(
            slotm[:], posm[:], ecm_s[:], ALU.add),
            reads=[b_idxm, b_c4], writes=[b_idxm])
        P.step("vector", lambda E: E.scalar_tensor_tensor(
            slotm[:], ovf[:], BIG, slotm[:], ALU.mult, ALU.add),
            reads=[b_idxm], writes=[b_idxm])
        P.step("vector", lambda E: E.scalar_tensor_tensor(
            wrk4[:], msel[:], -65536.0, slotm[:], ALU.mult, ALU.add),
            reads=[b_idxm, b_r], writes=[b_idxm])
        P.step("vector", lambda E: E.tensor_single_scalar(
            slotm[:], wrk4[:], 65536.0, ALU.add),
            reads=[b_idxm], writes=[b_idxm])
        for r in range(2):
            mr = m1a if r == 0 else m2a
            P.step("vector", lambda E, mr=mr: E.tensor_tensor(
                wrk4[:], mr[:], slotm[:], ALU.mult),
                reads=[b_idxm, b_r], writes=[b_idxm])
            P.step("vector", lambda E, r=r: E.tensor_reduce(
                sidxf[r][:], seg(wrk4[:, :]), AX.X, ALU.add),
                reads=[b_idxm], writes=[b_idxm])
            P.step("vector", lambda E, r=r: E.tensor_copy(
                sidxu[r][:], sidxf[r][:]),
                reads=[b_idxm], writes=[b_sidxu])
            P.step("vector", lambda E, mr=mr: E.tensor_tensor(
                wrk4[:], mr[:], w_all[:], ALU.mult),
                reads=[b_idxm, b_r, b_w_all], writes=[b_idxm])
            P.step("vector", lambda E, r=r: E.tensor_reduce(
                wsel[r][:], seg(wrk4[:, :]), AX.X, ALU.add),
                reads=[b_idxm], writes=[b_wsel])
        for r in range(2):
            for tc in range(16):
                P.step("gpsimd", lambda E, r=r, tc=tc: E.indirect_dma_start(
                    out=EidxD,
                    out_offset=IOA(ap=sidxu[r][:, tc:tc + 1], axis=0),
                    in_=tokid_s[:, tc:tc + 1], in_offset=None,
                    bounds_check=REGS["slots"], oob_is_err=False),
                    reads=[b_sidxu, b_tokid, b_eidxD], writes=[b_eidxD])
        P.step("sync", lambda E: E.dma_start(
            out=ebak[:], in_=EidxD.rearrange("(a b) one -> a (b one)", b=128)),
            reads=[b_eidxD], writes=[b_ebak])
        P.step("tensor", lambda E: E.matmul(
            psET[:], ebak[:], id_f[0:64, 0:64], start=True, stop=True,
            is_transpose=True),
            reads=[b_ebak, b_id], writes=[b_psET])
        P.step("vector", lambda E: E.tensor_copy(eidxall[:], psET[:]),
               reads=[b_psET], writes=[b_eidxall])
        pr.close()
        p4.close()
        p3t.close()
        right_x.close()

    if phases >= 5:
        NQC = CAP // 512
        p5 = ExitStack()
        Xf = sb(p5, "Xf", [128, NHC * CAP], BF16)
        Hf = sb(p5, "Hf", [128, NIC * CAP], BF16)
        Xg = [sb(p5, f"Xg{i}", [128, H], BF16) for i in range(2)]
        gAb = sb(p5, "gAb", [128, NHC * R], BF16)
        gBTb = sb(p5, "gBTb", [R, I2], BF16)
        dAb = sb(p5, "dAb", [128, NIC * R], BF16)
        dBTb = sb(p5, "dBTb", [R, H], BF16)
        t2sb = sb(p5, "t2sb", [R, CAP], BF16)
        t3sb = sb(p5, "t3sb", [R, CAP], BF16)
        wg5 = [sb(p5, f"wg5{i}", [128, NHC * 128], BF16) for i in range(2)]
        wu5 = [sb(p5, f"wu5{i}", [128, NHC * 128], BF16) for i in range(2)]
        wd5 = [sb(p5, f"wd5{i}", [128, NIC * 512], BF16) for i in range(2)]
        sg5 = [sb(p5, f"sg5{i}", [128, 512], BF16) for i in range(2)]
        dt5 = [sb(p5, f"dt5{i}", [128, 512], F32) for i in range(2)]

        b_Xf = Buf("Xf")
        b_Hf = Buf("Hf")
        b_Xg = [Buf(f"Xg{i}") for i in range(2)]
        b_gA = Buf("gA")
        b_gBT = Buf("gBT")
        b_dA = Buf("dA")
        b_dBT = Buf("dBT")
        b_t2 = Buf("t2")
        b_t3 = Buf("t3")
        b_wg5 = [Buf(f"wg5{i}") for i in range(2)]
        b_wu5 = [Buf(f"wu5{i}") for i in range(2)]
        b_wd5 = [Buf(f"wd5{i}") for i in range(2)]
        b_sg5 = [Buf(f"sg5{i}") for i in range(2)]
        b_dt5 = [Buf(f"dt5{i}") for i in range(2)]

        psTR = p5.enter_context(nc.psum_tensor("psTR", [128, 512], BF16))
        psT2 = p5.enter_context(nc.psum_tensor("psT2", [R, 512], F32))
        psG = [p5.enter_context(nc.psum_tensor(f"psG{i}", [128, 512], F32))
               for i in range(2)]
        psU = [p5.enter_context(nc.psum_tensor(f"psU{i}", [128, 512], F32))
               for i in range(2)]
        psD = [p5.enter_context(nc.psum_tensor(f"psD{i}", [128, 512], F32))
               for i in range(2)]
        b_psTR = Buf("psTR")
        b_psT2 = Buf("psT2")
        b_psG = [Buf(f"psG{i}") for i in range(2)]
        b_psU = [Buf(f"psU{i}") for i in range(2)]
        b_psD = [Buf(f"psD{i}") for i in range(2)]

        def emit_lora(e):
            P.step("gpsimd", lambda E, e=e: E.dma_start(
                out=gAb[:], in_=gupA[e].rearrange("p h r -> p (h r)")),
                writes=[b_gA], extra_waits=tuple(b_gA.free.items()))
            P.step("gpsimd", lambda E, e=e: E.dma_start(out=gBTb[:], in_=gupBT[e]),
                   writes=[b_gBT], extra_waits=tuple(b_gBT.free.items()))
            P.step("gpsimd", lambda E, e=e: E.dma_start(
                out=dAb[:], in_=downA[e].rearrange("p i r -> p (i r)")),
                writes=[b_dA], extra_waits=tuple(b_dA.free.items()))
            P.step("gpsimd", lambda E, e=e: E.dma_start(out=dBTb[:], in_=downBT[e]),
                   writes=[b_dBT], extra_waits=tuple(b_dBT.free.items()))

        def emit_wgwu(e, j):
            jb = j % 2
            P.step("gpsimd", lambda E, e=e, j=j, jb=jb: E.dma_start(
                out=wg5[jb][:], in_=gupw[e, j].rearrange("p h c -> p (h c)")),
                writes=[b_wg5[jb]])
            P.step("gpsimd", lambda E, e=e, j=j, jb=jb: E.dma_start(
                out=wu5[jb][:], in_=gupw[e, j + 8].rearrange("p h c -> p (h c)")),
                writes=[b_wu5[jb]])

        def emit_wd(e, fg):
            wb = fg % 2
            for ic in range(NIC):
                P.step("gpsimd", lambda E, e=e, fg=fg, ic=ic, wb=wb: E.dma_start(
                    out=wd5[wb][:, ic * 512:(ic + 1) * 512],
                    in_=downw[e, ic][:, fg * 512:fg * 512 + 512]),
                    writes=[b_wd5[wb]] if ic == NIC - 1 else [],
                    extra_waits=tuple(b_wd5[wb].free.items()) if ic == 0 else ())

        P.phase_barrier(ALLSEMS)
        for e in range(NE):
            eb = e % 2
            emit_lora(e)
            for sc in range(NIC):
                gb = sc % 2
                P.step("gpsimd", lambda E, e=e, sc=sc, gb=gb: E.indirect_dma_start(
                    out=Xg[gb][:], out_offset=None,
                    in_=X1tok,
                    in_offset=IOA(ap=eidxall[:, e * NIC + sc: e * NIC + sc + 1],
                                  axis=0),
                    bounds_check=REGS["tok"], oob_is_err=False),
                    reads=[b_X1tok, b_eidxall], writes=[b_Xg[gb]])
                for g4 in range(4):
                    def xtr5(E, gb=gb, g4=g4):
                        last = None
                        for k in range(4):
                            hc = g4 * 4 + k
                            last = E.matmul(
                                psTR[:, k * 128:(k + 1) * 128],
                                Xg[gb][:, hc * 128:(hc + 1) * 128],
                                id_b[:], start=(k == 0), stop=(k == 3),
                                is_transpose=True)
                        return last
                    P.step("tensor", xtr5, reads=[b_Xg[gb], b_id], writes=[b_psTR])

                    def xev5(E, sc=sc, g4=g4):
                        last = None
                        for k in range(4):
                            hc = g4 * 4 + k
                            last = E.tensor_copy(
                                Xf[:, hc * CAP + sc * 128: hc * CAP + sc * 128 + 128],
                                psTR[:, k * 128:(k + 1) * 128])
                        return last
                    P.step("vector", xev5, reads=[b_psTR],
                           writes=[b_Xf] if (sc == NIC - 1 and g4 == 3) else [],
                           extra_waits=tuple(b_Xf.free.items())
                           if (sc == 0 and g4 == 0) else ())
            for qt in range(NQC):
                def t2mm(E, qt=qt):
                    last = None
                    for hc in range(NHC):
                        last = E.matmul(
                            psT2[:], gAb[:, hc * R:(hc + 1) * R],
                            Xf[:, hc * CAP + qt * 512: hc * CAP + qt * 512 + 512],
                            start=(hc == 0), stop=(hc == NHC - 1))
                    return last
                P.step("tensor", t2mm, reads=[b_gA, b_Xf], writes=[b_psT2])
                P.step("vector", lambda E, qt=qt: E.tensor_scalar_mul(
                    t2sb[:, qt * 512:qt * 512 + 512], psT2[:], 2.0),
                    reads=[b_psT2], writes=[b_t2] if qt == NQC - 1 else [],
                    extra_waits=tuple(b_t2.free.items()) if qt == 0 else ())
            for j in range(NIC):
                jb = j % 2
                emit_wgwu(e, j)
                for qt in range(NQC):
                    gp = qt % 2

                    def gmm(E, jb=jb, qt=qt, gp=gp, j=j):
                        for hc in range(NHC):
                            E.matmul(psG[gp][:], wg5[jb][:, hc * 128:(hc + 1) * 128],
                                     Xf[:, hc * CAP + qt * 512: hc * CAP + qt * 512 + 512],
                                     start=(hc == 0), stop=False)
                        return E.matmul(psG[gp][:], gBTb[:, j * 128:(j + 1) * 128],
                                        t2sb[:, qt * 512:qt * 512 + 512],
                                        start=False, stop=True)
                    P.step("tensor", gmm, reads=[b_wg5[jb], b_Xf, b_gBT, b_t2],
                           writes=[b_psG[gp]])

                    def umm(E, jb=jb, qt=qt, gp=gp, j=j):
                        for hc in range(NHC):
                            E.matmul(psU[gp][:], wu5[jb][:, hc * 128:(hc + 1) * 128],
                                     Xf[:, hc * CAP + qt * 512: hc * CAP + qt * 512 + 512],
                                     start=(hc == 0), stop=False)
                        return E.matmul(psU[gp][:], gBTb[:, (j + 8) * 128:(j + 9) * 128],
                                        t2sb[:, qt * 512:qt * 512 + 512],
                                        start=False, stop=True)
                    P.step("tensor", umm, reads=[b_wu5[jb], b_Xf, b_gBT, b_t2],
                           writes=[b_psU[gp]])
                    P.step("scalar", lambda E, gp=gp: E.activation(
                        sg5[gp][:], psG[gp][:], AF.Silu),
                        reads=[b_psG[gp]], writes=[b_sg5[gp]])
                    P.step("vector", lambda E, gp=gp, j=j, qt=qt: E.tensor_mul(
                        Hf[:, j * CAP + qt * 512: j * CAP + qt * 512 + 512],
                        sg5[gp][:], psU[gp][:]),
                        reads=[b_sg5[gp], b_psU[gp]],
                        writes=[b_Hf] if (j == NIC - 1 and qt == NQC - 1) else [],
                        extra_waits=tuple(b_Hf.free.items())
                        if (j == 0 and qt == 0) else ())
            for qt in range(NQC):
                def t3mm(E, qt=qt):
                    last = None
                    for ic in range(NIC):
                        last = E.matmul(
                            psT2[:], dAb[:, ic * R:(ic + 1) * R],
                            Hf[:, ic * CAP + qt * 512: ic * CAP + qt * 512 + 512],
                            start=(ic == 0), stop=(ic == NIC - 1))
                    return last
                P.step("tensor", t3mm, reads=[b_dA, b_Hf], writes=[b_psT2])
                P.step("vector", lambda E, qt=qt: E.tensor_scalar_mul(
                    t3sb[:, qt * 512:qt * 512 + 512], psT2[:], 2.0),
                    reads=[b_psT2], writes=[b_t3] if qt == NQC - 1 else [],
                    extra_waits=tuple(b_t3.free.items()) if qt == 0 else ())
            for fg in range(4):
                wb = fg % 2
                emit_wd(e, fg)
                for sc in range(NIC):
                    dp = sc % 2

                    def dmm(E, wb=wb, sc=sc, dp=dp, fg=fg):
                        for ic in range(NIC):
                            E.matmul(psD[dp][:],
                                     Hf[:, ic * CAP + sc * 128: ic * CAP + sc * 128 + 128],
                                     wd5[wb][:, ic * 512:(ic + 1) * 512],
                                     start=(ic == 0), stop=False)
                        return E.matmul(psD[dp][:], t3sb[:, sc * 128:sc * 128 + 128],
                                        dBTb[:, fg * 512:fg * 512 + 512],
                                        start=False, stop=True)
                    P.step("tensor", dmm, reads=[b_Hf, b_wd5[wb], b_t3, b_dBT],
                           writes=[b_psD[dp]])
                    if sc % 2 == 0:
                        P.step("vector", lambda E, dp=dp: E.tensor_copy(
                            dt5[dp][:], psD[dp][:]),
                            reads=[b_psD[dp]], writes=[b_dt5[dp]])
                    else:
                        P.step("scalar", lambda E, dp=dp: E.copy(
                            dt5[dp][:], psD[dp][:]),
                            reads=[b_psD[dp]], writes=[b_dt5[dp]])
                    P.step("sync", lambda E, e=e, sc=sc, fg=fg, dp=dp: E.dma_start(
                        out=Dall[e * CAP + sc * 128: e * CAP + sc * 128 + 128,
                                 fg * 512:fg * 512 + 512],
                        in_=dt5[dp][:]),
                        reads=[b_dt5[dp]], writes=[b_Dall])
        P.phase_barrier(ALLSEMS)
        p5.close()

        NB6 = 3
        p6 = ExitStack()
        d1t = [sb(p6, f"d1t{i}", [128, H], F32) for i in range(NB6)]
        d2t = [sb(p6, f"d2t{i}", [128, H], F32) for i in range(NB6)]
        x1r = [sb(p6, f"x1r{i}", [128, H], BF16) for i in range(NB6)]
        y2t = [sb(p6, f"y2t{i}", [128, H], F32) for i in range(NB6)]
        sqt = [sb(p6, f"sqt{i}", [128, H], F32) for i in range(NB6)]
        ot6 = [sb(p6, f"ot6{i}", [128, H], F32) for i in range(NB6)]
        cst = [sb(p6, f"cst{i}", [128, 8], F32) for i in range(NB6)]
        b_d1 = [Buf(f"d1{i}") for i in range(NB6)]
        b_d2 = [Buf(f"d2{i}") for i in range(NB6)]
        b_x1r = [Buf(f"x1r{i}") for i in range(NB6)]
        b_y2 = [Buf(f"y2{i}") for i in range(NB6)]
        b_sq6 = [Buf(f"sq6{i}") for i in range(NB6)]
        b_ot = [Buf(f"ot{i}") for i in range(NB6)]
        b_cst = [Buf(f"cst{i}") for i in range(NB6)]
        for tc in range(16):
            tb = tc % NB6
            P.step("gpsimd", lambda E, tc=tc, tb=tb: E.indirect_dma_start(
                out=d1t[tb][:], out_offset=None,
                in_=Dall, in_offset=IOA(ap=sidxu[0][:, tc:tc + 1], axis=0),
                bounds_check=REGS["slots"], oob_is_err=False),
                reads=[b_Dall, b_sidxu], writes=[b_d1[tb]])
            P.step("gpsimd", lambda E, tc=tc, tb=tb: E.indirect_dma_start(
                out=d2t[tb][:], out_offset=None,
                in_=Dall, in_offset=IOA(ap=sidxu[1][:, tc:tc + 1], axis=0),
                bounds_check=REGS["slots"], oob_is_err=False),
                reads=[b_Dall, b_sidxu], writes=[b_d2[tb]])
            P.step("sync", lambda E, tc=tc, tb=tb: E.dma_start(
                out=x1r[tb][:], in_=X1tok[tc * 128:(tc + 1) * 128, :]),
                reads=[b_X1tok], writes=[b_x1r[tb]])
            P.step("scalar", lambda E, tc=tc, tb=tb: E.activation(
                y2t[tb][:], d1t[tb][:], AF.Copy, scale=wsel[0][:, tc:tc + 1]),
                reads=[b_d1[tb], b_wsel], writes=[b_y2[tb]])
            P.step("scalar", lambda E, tc=tc, tb=tb: E.activation(
                sqt[tb][:], d2t[tb][:], AF.Copy, scale=wsel[1][:, tc:tc + 1]),
                reads=[b_d2[tb], b_wsel], writes=[b_sq6[tb]])
            P.step("vector", lambda E, tb=tb: E.tensor_add(
                y2t[tb][:], y2t[tb][:], sqt[tb][:]),
                reads=[b_y2[tb], b_sq6[tb]], writes=[b_y2[tb]])
            P.step("vector", lambda E, tb=tb: E.tensor_add(
                y2t[tb][:], y2t[tb][:], x1r[tb][:]),
                reads=[b_y2[tb], b_x1r[tb]], writes=[b_y2[tb]])
            P.step("vector", lambda E, tb=tb: E.tensor_reduce(
                cst[tb][:, 0:1], y2t[tb][:], AX.X, ALU.add),
                reads=[b_y2[tb]], writes=[b_cst[tb]])
            P.step("scalar", lambda E, tb=tb: E.activation(
                sqt[tb][:], y2t[tb][:], AF.Square),
                reads=[b_y2[tb]], writes=[b_sq6[tb]])
            P.step("vector", lambda E, tb=tb: E.tensor_reduce(
                cst[tb][:, 1:2], sqt[tb][:], AX.X, ALU.add),
                reads=[b_sq6[tb]], writes=[b_cst[tb]])
            P.step("vector", lambda E, tb=tb: E.tensor_scalar_mul(
                cst[tb][:, 2:3], cst[tb][:, 0:1], 1.0 / H),
                reads=[b_cst[tb]], writes=[b_cst[tb]])
            P.step("vector", lambda E, tb=tb: E.tensor_scalar_mul(
                cst[tb][:, 3:4], cst[tb][:, 1:2], 1.0 / H),
                reads=[b_cst[tb]], writes=[b_cst[tb]])
            P.step("scalar", lambda E, tb=tb: E.activation(
                cst[tb][:, 4:5], cst[tb][:, 2:3], AF.Square),
                reads=[b_cst[tb]], writes=[b_cst[tb]])
            P.step("vector", lambda E, tb=tb: E.tensor_sub(
                cst[tb][:, 5:6], cst[tb][:, 3:4], cst[tb][:, 4:5]),
                reads=[b_cst[tb]], writes=[b_cst[tb]])
            P.step("scalar", lambda E, tb=tb: E.activation(
                cst[tb][:, 6:7], cst[tb][:, 5:6], AF.Ln, bias=epscol[:]),
                reads=[b_cst[tb], b_ones], writes=[b_cst[tb]])
            P.step("scalar", lambda E, tb=tb: E.activation(
                cst[tb][:, 7:8], cst[tb][:, 6:7], AF.Exp, scale=-0.5),
                reads=[b_cst[tb]], writes=[b_cst[tb]])
            P.step("vector", lambda E, tb=tb: E.tensor_mul(
                cst[tb][:, 4:5], cst[tb][:, 2:3], cst[tb][:, 7:8]),
                reads=[b_cst[tb]], writes=[b_cst[tb]])
            P.step("vector", lambda E, tb=tb: E.tensor_scalar_mul(
                cst[tb][:, 5:6], cst[tb][:, 4:5], -1.0),
                reads=[b_cst[tb]], writes=[b_cst[tb]])
            P.step("scalar", lambda E, tb=tb: E.activation(
                ot6[tb][:], y2t[tb][:], AF.Copy, scale=cst[tb][:, 7:8]),
                reads=[b_y2[tb], b_cst[tb]], writes=[b_ot[tb]])
            P.step("scalar", lambda E, tb=tb: E.add(
                sqt[tb][:], ot6[tb][:], cst[tb][:, 5:6]),
                reads=[b_ot[tb], b_cst[tb]], writes=[b_sq6[tb]])
            P.step("sync", lambda E, tc=tc, tb=tb: E.dma_start(
                out=OUT[tc * 128:(tc + 1) * 128, :], in_=sqt[tb][:]),
                reads=[b_sq6[tb]], writes=[])
        p6.close()

    P.raw_wait("sync", ss, ss.n)
    P.raw_wait("sync", sg, sg.n)
    P.raw_wait("sync", st, st.n)
    P.raw_wait("sync", sv, sv.n)
    P.raw_wait("sync", sa, sa.n)

    nops = {e: len(P.prog[e]) for e in ENGINES}

    with nc.Block() as block:
        @block.tensor
        def _(Eng):
            for fn in P.prog["tensor"]:
                fn(Eng)

        @block.vector
        def _(Eng):
            for fn in P.prog["vector"]:
                fn(Eng)

        @block.scalar
        def _(Eng):
            for fn in P.prog["scalar"]:
                fn(Eng)

        @block.gpsimd
        def _(Eng):
            for fn in P.prog["gpsimd"]:
                fn(Eng)

        @block.sync
        def _(Eng):
            for fn in P.prog["sync"]:
                fn(Eng)

    return nc, nops


def prep_inputs(inp):
    f32 = np.float32
    x = np.asarray(inp["L0_attn_hidden_states"], f32)
    Wq = np.asarray(inp["L0_attn_Wq"], f32)
    Wk = np.asarray(inp["L0_attn_Wk"], f32)
    Wv = np.asarray(inp["L0_attn_Wv"], f32)
    Wo = np.asarray(inp["L0_attn_Wo"], f32)
    bq = np.asarray(inp["L0_attn_bq"], f32)
    bk = np.asarray(inp["L0_attn_bk"], f32)
    bv = np.asarray(inp["L0_attn_bv"], f32)
    gw = np.asarray(inp["L0_ffn_gate_weight"], f32)
    glA_ = np.asarray(inp["L0_ffn_gate_lora_A"], f32)
    glB_ = np.asarray(inp["L0_ffn_gate_lora_B"], f32)
    gup = np.asarray(inp["L0_ffn_gate_up_proj"], f32)
    dwn = np.asarray(inp["L0_ffn_down_proj"], f32)
    gA_ = np.asarray(inp["L0_ffn_gup_lora_A"], f32)
    gB_ = np.asarray(inp["L0_ffn_gup_lora_B"], f32)
    dA_ = np.asarray(inp["L0_ffn_down_lora_A"], f32)
    dB_ = np.asarray(inp["L0_ffn_down_lora_B"], f32)

    def blockT(WT):
        return np.ascontiguousarray(
            WT.reshape(NHC, 128, NHC, 128).transpose(2, 1, 0, 3))

    shared = {
        "wq": blockT(Wq.T), "wk": blockT(Wk.T), "wv": blockT(Wv.T),
        "wo": blockT(Wo.T),
        "bias3": np.ascontiguousarray(
            np.stack([bq, bk, bv]).reshape(3, 16, 128).transpose(2, 0, 1)),
        "gwT": np.ascontiguousarray(gw.reshape(NE, NHC, 128).transpose(2, 1, 0)),
        "glA": np.ascontiguousarray(glA_),
        "glBT": np.ascontiguousarray(glB_.T),
        "gupw": np.ascontiguousarray(
            gup.reshape(NE, NHC, 128, NHC, 128).transpose(0, 3, 2, 1, 4)),
        "downw": np.ascontiguousarray(dwn.reshape(NE, NIC, 128, H)),
        "gupA": np.ascontiguousarray(
            gA_.reshape(NE, R, NHC, 128).transpose(0, 3, 2, 1)),
        "gupBT": np.ascontiguousarray(gB_.transpose(0, 2, 1)),
        "downA": np.ascontiguousarray(
            dA_.reshape(NE, R, NIC, 128).transpose(0, 3, 2, 1)),
        "downBT": np.ascontiguousarray(dB_.transpose(0, 2, 1)),
        "ident": np.eye(128, dtype=f32),
    }
    kk = np.arange(128)
    shared["ltri"] = np.triu(np.ones((128, 128), f32))
    shared["mpre"] = (((kk // 8)[:, None] < (kk // 8)[None, :])
                      & ((kk % 8)[:, None] == (kk % 8)[None, :])).astype(f32)
    shared["ecmat"] = np.broadcast_to(
        ((kk % 8) * CAP).astype(f32), (128, 128)).copy()
    shared["tokidf"] = np.ascontiguousarray(
        (np.arange(16, dtype=f32)[None, :] * 128 + kk[:, None].astype(f32)))

    inv_freq = 1.0 / (1000000.0 ** (np.arange(0, HD, 2, dtype=np.float64) / HD))
    pos = np.arange(S, dtype=np.float64)
    fr = np.outer(inv_freq, pos)
    cosf = np.cos(fr).astype(f32)
    sinf = np.sin(fr).astype(f32)
    cosd = np.ascontiguousarray(np.vstack([cosf, cosf]))
    sind = np.ascontiguousarray(np.vstack([sinf, -sinf]))
    swap = np.zeros((128, 128), f32)
    swap[0:64, 64:128] = np.eye(64)
    swap[64:128, 0:64] = np.eye(64)
    shared["swapm"] = swap

    percore = []
    for c in range(NCORES):
        b, q = c // 4, c % 4
        percore.append({
            "xT": np.ascontiguousarray(np.roll(x[b].T, -q * TOK, axis=1)),
            "cosD": np.ascontiguousarray(np.roll(cosd, -q * TOK, axis=1)),
            "sinD": np.ascontiguousarray(np.roll(sind, -q * TOK, axis=1)),
        })
    return shared, percore


_CACHED = {}


def kernel(**inputs):
    shared, percore = prep_inputs(inputs)
    if "full" not in _CACHED:
        _CACHED["full"] = build_program()
    nc, nops = _CACHED["full"]
    in_maps = [dict(shared, **m) for m in percore]
    res = run_bass_kernel_spmd(nc, in_maps, core_ids=list(range(NCORES)))
    out = np.empty((B, S, H), np.float32)
    for c in range(NCORES):
        b, q = c // 4, c % 4
        out[b, q * TOK:(q + 1) * TOK, :] = np.asarray(
            res.results[c]["out"], np.float32)
    return out
